# version 03
# baseline (speedup 1.0000x reference)
import numpy as np
import ml_dtypes
import concourse.bass as bass
import concourse.mybir as mybir
from contextlib import ExitStack
from concourse.bass_utils import run_bass_kernel_spmd

F32 = mybir.dt.float32
BF16 = mybir.dt.bfloat16
I32 = mybir.dt.int32
ALU = mybir.AluOpType
ACT = mybir.ActivationFunctionType
AXX = mybir.AxisListType.X

H = 4096
T = 1024
LW = 1056
PROJ = 8512
CONV = 4352
NS = 128
NH = 64
Q = 128
NCH = 8
NKT = 32
EPS = 1e-5
NEG = -1e30
R48 = 48


class Ctx:
    def __init__(self, nc, blk):
        self.nc = nc
        self.blk = blk
        self.sems = {}

    def inc(self, instr, name, k=1):
        s = self.sems[name]
        instr.then_inc(s[0], k)
        s[1] += k
        return instr

    def mark(self, name):
        return self.sems[name][1]

    def wait(self, eng, name, val=None):
        s = self.sems[name]
        v = s[1] if val is None else val
        if v > 0:
            eng.wait_ge(s[0], v)

    def bar(self):
        names = list(self.sems)
        for sec in (self.blk.tensor, self.blk.vector, self.blk.scalar,
                    self.blk.gpsimd, self.blk.sync):
            def f(eng):
                for n in names:
                    self.wait(eng, n)
            sec(f)


def make_ident(g, ap, n=128):
    g.memset(ap, 0.0)
    g.affine_select(out=ap, in_=ap, compare_op=ALU.not_equal, fill=1.0,
                    base=0, pattern=[[-1, n]], channel_multiplier=1)


def make_maskbias(g, ap, n=128):
    g.memset(ap, 0.0)
    g.affine_select(out=ap, in_=ap, compare_op=ALU.is_ge, fill=NEG,
                    base=0, pattern=[[1, n]], channel_multiplier=-1)


def stats_cols(v, s, mcol, rcol, nmr, sumc, sqc, n_feat, rms_fused, ta, tb):
    inv = 1.0 / n_feat
    v.tensor_scalar(out=mcol, in0=sumc, scalar1=inv, scalar2=None, op0=ALU.mult)
    v.tensor_scalar(out=ta, in0=sqc, scalar1=inv, scalar2=None, op0=ALU.mult)
    v.tensor_tensor(out=tb, in0=mcol, in1=mcol, op=ALU.mult)
    v.tensor_tensor(out=ta, in0=ta, in1=tb, op=ALU.subtract)
    v.tensor_scalar(out=tb, in0=ta, scalar1=EPS, scalar2=None, op0=ALU.add)
    v.reciprocal(out=tb, in_=tb)
    s.activation(out=rcol, in_=tb, func=ACT.Sqrt)
    if rms_fused:
        v.tensor_tensor(out=ta, in0=ta, in1=tb, op=ALU.mult)
        v.tensor_scalar(out=ta, in0=ta, scalar1=EPS, scalar2=None, op0=ALU.add)
        v.reciprocal(out=ta, in_=ta)
        s.activation(out=ta, in_=ta, func=ACT.Sqrt)
        v.tensor_tensor(out=rcol, in0=rcol, in1=ta, op=ALU.mult)
    v.tensor_tensor(out=ta, in0=mcol, in1=rcol, op=ALU.mult)
    v.tensor_scalar(out=nmr, in0=ta, scalar1=-1.0, scalar2=None, op0=ALU.mult)


def build_l1(debug=0, upto=9):
    nc = bass.Bass("TRN2", target_bir_lowering=False)
    es = ExitStack()

    def din(name, shape, dt=F32):
        return nc.dram_tensor(name, shape, dt, kind="ExternalInput").ap()

    def dout(name, shape, dt=F32):
        return nc.dram_tensor(name, shape, dt, kind="ExternalOutput").ap()

    def sb(name, p, f, dt):
        return es.enter_context(nc.sbuf_tensor(name, [p, f], dt))

    def ps(name, p, f, dt=F32):
        return es.enter_context(nc.psum_tensor(name, [p, f], dt))

    xin = din("xin", [T + 3, H])
    halom = din("halom", [128, 4])
    ffn_w = din("ffn_w", [H, H])
    ffn_b = din("ffn_b", [1, H])
    lora_A = din("lora_A", [R48, H])
    lora_B = din("lora_B", [3 * H, 16])
    lsi = din("lsi", [1, 4], I32)
    ipw = din("ipw", [H, PROJ])
    conv_w = din("conv_w", [CONV, 4])
    a_log = din("a_log", [1, NH])
    dt_bias = din("dt_bias", [1, NH])
    d_in = din("d_in", [1, NH])
    mup_i = din("mup_i", [1, PROJ + 64])
    ln_w = din("ln_w", [32, 128])

    o_x1 = dout("o_x1", [T, H])
    o_siluz = dout("o_siluz", [H, T], BF16)
    o_g1 = dout("o_g1", [T, H], BF16)
    o_sztok = dout("o_sztok", [T, H], BF16)
    o_C = dout("o_C", [NS, T], BF16)
    o_cum = dout("o_cum", [T, NH])
    o_ctot = dout("o_ctot", [NCH, NH])
    o_hloc = dout("o_hloc", [NS, H])
    xs_dram = nc.dram_tensor("xs_dram", [H, T], BF16, kind="Internal").ap()
    flat_dram = nc.dram_tensor("flat_dram", [NH, 8, Q], F32, kind="Internal").ap()
    x1h_dram = nc.dram_tensor("x1h_dram", [3, H], F32, kind="Internal").ap()
    dbg = {}
    if debug:
        dbg["lnx"] = dout("d_lnx", [32, 128, LW], BF16)
        dbg["xst"] = dout("d_xst", [128, 4096])
        dbg["lnxtok"] = dout("d_lnxtok", [128, 4096], BF16)
        dbg["stats"] = dout("d_stats", [128, 8])
        dbg["s2"] = dout("d_s2", [128, 64])
        dbg["scr4"] = dout("d_scr4", [128, 4])
        dbg["xs"] = dout("d_xs", [H, T], BF16)
        dbg["B"] = dout("d_B", [NS, T], BF16)
        dbg["dt"] = dout("d_dt", [NH, T])
        dbg["x1h"] = dout("d_x1h", [3, H])
        dbg["zz"] = dout("d_zz", [128, T], BF16)
        dbg["wff"] = dout("d_wff", [128, 16384], BF16)
        dbg["rmk"] = dout("d_rmk", [64, LW], BF16)

    LNX = sb("LNX", 128, 32 * LW, BF16)
    A2 = sb("A2", 128, 40960, BF16)
    lnxtok = sb("lnxtok", 128, H, BF16)
    Bfm = sb("Bfm", 128, T, BF16)
    Cfm = sb("Cfm", 128, T, BF16)
    dtf = sb("dtf", 64, T, F32)
    cumb = sb("cumb", 64, Q, F32)
    sch = sb("sch", 64, Q, F32)
    ones64 = sb("ones64", 64, Q, F32)
    cumT = sb("cumT", 128, 64, F32)
    ncumT = sb("ncumT", 128, 64, F32)
    dtT = sb("dtT", 128, 64, F32)
    dend = sb("dend", 128, 64, F32)
    EEs = sb("EEs", 128, 64, F32)
    de64 = sb("de64", 64, Q, F32)
    idb = sb("idb", 128, 128, BF16)
    idf = sb("idf", 128, 128, F32)
    mkb4 = sb("mkb4", 128, 512, BF16)
    onescf = sb("onescf", 1, 128, F32)
    ones128b = sb("ones128b", 128, 1, BF16)
    halfcol = sb("halfcol", 128, 1, F32)
    mupc = sb("mupc", 128, 67, F32)
    lnwc = sb("lnwc", 128, 32, F32)
    dtbias_c = sb("dtbias_c", 64, 1, F32)
    nA_c = sb("nA_c", 64, 1, F32)
    Dcol = sb("Dcol", 128, 32, F32)
    ATf = sb("ATf", 128, 32 * R48, BF16)
    BTFB = sb("BTFB", 128, H, BF16)
    BTf = BTFB[0:R48, :]
    biasrow = BTFB[48:49, :]
    rmk = sb("rmk", 64, LW, BF16)

    statS1 = sb("statS1", 128, 64, F32)
    statS2 = sb("statS2", 128, 64, F32)
    statS3 = sb("statS3", 128, 64, F32)
    statS4 = sb("statS4", 128, 64, F32)
    hs3 = sb("hs3", 3, 8, F32)
    hs4 = sb("hs4", 3, 8, F32)
    hs1 = sb("hs1", 3, 8, F32)
    hs2 = sb("hs2", 3, 8, F32)
    colM = sb("colM", 128, 1, F32)
    colR = sb("colR", 128, 1, F32)
    colN = sb("colN", 128, 1, F32)
    colTa = sb("colTa", 128, 1, F32)
    colTb = sb("colTb", 128, 1, F32)
    scr = sb("scr", 128, 512, F32)
    x1b = sb("x1b", 128, 1024, F32)
    hcolM = sb("hcolM", 3, 1, F32)
    hcolR = sb("hcolR", 3, 1, F32)
    hcolN = sb("hcolN", 3, 1, F32)
    hcolTa = sb("hcolTa", 3, 1, F32)
    hcolTb = sb("hcolTb", 3, 1, F32)
    hscr = sb("hscr", 3, 512, F32)
    halom_s = sb("halom_s", 128, 4, F32)
    xbctmp = sb("xbctmp", 128, 1056, BF16)
    convw = sb("convw", 128, 136, F32)
    xstmp = sb("xstmp", 128, T, BF16)
    lsis = sb("lsis", 1, 4, I32)
    lsif = sb("lsif", 1, 4, F32)
    cnt1 = sb("cnt1", 1, 1, F32)
    cst13 = sb("cst13", 1, 3, F32)
    msk13 = sb("msk13", 1, 3, F32)
    prm = sb("prm", 128, 128, F32)
    prm2 = sb("prm2", 128, 128, F32)
    prm3 = sb("prm3", 1, 192, F32)
    ctotb = sb("ctotb", 64, 1, F32)
    colS = sb("colS", 128, 1, F32)
    colQ = sb("colQ", 128, 1, F32)
    onescb = sb("onescb", 1, 128, BF16)
    xcb = sb("xcb", 128, 1024, F32)
    xbc2tmp = sb("xbc2tmp", 128, 1056, BF16)
    h512 = sb("h512", 3, 512, F32)
    ecrow = sb("ecrow", 1, 64, F32)
    ECtok = sb("ECtok", 128, 64, F32)
    Dtok = sb("Dtok", 128, 64, F32)
    Btok = sb("Btok", 128, 128, BF16)
    ocumb = sb("ocumb", 128, 64, F32)

    WFF = [A2[:, 0:16384], A2[:, 16384:32768]]
    OTB = [A2[:, 32768:36864], A2[:, 36864:40960]]
    WP = [A2[:, 0:8192], A2[:, 8192:16384]]
    DIAG = A2[:, 16384:33792]
    LAB = A2[:, 0:4096]
    LBB = A2[:, 4096:5632]
    XCH = [A2[:, 0:4096], A2[:, 4096:8192]]
    XTOK = A2[:, 8192:12288]
    XPP = A2[:, 12288:16384]
    G1CH = A2[:, 16384:20480]
    HTf = A2[:, 20480:28672].bitcast(F32)
    HTb = A2[:, 28672:32768]
    HTs = A2[:, 32768:36864]
    xst = A2[:, 0:8192].bitcast(F32)
    hlnb1 = A2[:, 8192:12288][0:3, :]
    x1hv = A2[:, 8192:16384].bitcast(F32)[0:3, :]
    hlnb3 = A2[:, 34304:38400][0:3, :]
    hzx = A2[:, 33792:38144][0:3, :]
    xhb1 = A2[:, 12288:20480].bitcast(F32)[0:3, :]
    x1hbuf = A2[:, 8192:16384].bitcast(F32)[0:3, :]
    dtraw = A2[:, 38144:40192].bitcast(F32)[0:64, 0:1024]
    G4 = A2[:, 36864:37376]
    C4 = A2[:, 37376:37888]
    E4 = A2[:, 37888:38400]
    W4 = A2[:, 38400:38912]
    Cw4 = A2[:, 38912:39424]
    btmp = A2[:, 5632:6656][0:16, :]
    dtabs = x1b[0:64, :]
    cumflat = LNX[:, 0:16384].bitcast(F32)[0:1, :]
    ecumflat = LNX[:, 16384:32768].bitcast(F32)[0:1, :]
    cumall = LNX[:, 16384:20480].bitcast(F32)[0:64, 0:1024]

    p0 = ps("p0", 128, 512)
    p1 = ps("p1", 128, 512)
    p2 = ps("p2", 128, 512)
    p3 = ps("p3", 128, 512)
    p4 = ps("p4", 128, 512)
    p5 = ps("p5", 128, 512)
    p6 = ps("p6", 128, 512)
    pT = ps("pT", 128, 1024, BF16)
    PS = [p0, p1, p2, p3]

    blk = es.enter_context(nc.Block())
    c = Ctx(nc, blk)
    for s in ["qS", "qC", "qG", "qW", "eT", "eV", "eS", "eG"]:
        c.sems[s] = [es.enter_context(nc.semaphore(s)), 0]

    @blk.gpsimd
    def _(g):
        make_ident(g, idb[:])
        make_ident(g, idf[:])
        make_maskbias(g, mkb4[:, 0:128])
        g.memset(cst13[0:1, 0:1], 1.0)
        g.memset(cst13[0:1, 1:2], 2.0)
        c.inc(g.memset(cst13[0:1, 2:3], 3.0), "eG")
        c.inc(g.memset(ones64[:], 1.0), "eG")
        c.inc(g.dma_start(out=halom_s[:], in_=halom), "qG", 16)
        c.inc(g.dma_start(out=lsis[:], in_=lsi), "qG", 16)
        c.inc(g.dma_start(out=biasrow, in_=ffn_b), "qG", 16)
        c.inc(g.dma_start(out=LAB[0:R48, :], in_=lora_A), "qG", 16)
        c.inc(g.dma_start(out=LBB[:, 0:1536].rearrange("p (a r) -> p a r", r=16),
                          in_=lora_B.rearrange("(a p) r -> p a r", p=128)), "qG", 16)
        c.inc(g.dma_start(out=prm[0:67, :], in_=mup_i.rearrange("o (a b) -> (o a) b", b=128)), "qG", 16)
        c.inc(g.dma_start(out=prm2[0:32, :], in_=ln_w), "qG", 16)
        c.inc(g.dma_start(out=prm3[0:1, 0:64], in_=dt_bias), "qG", 16)
        c.inc(g.dma_start(out=prm3[0:1, 64:128], in_=a_log), "qG", 16)
        c.inc(g.dma_start(out=prm3[0:1, 128:192], in_=d_in), "qG", 16)
        c.inc(g.dma_start(out=convw[:].rearrange("p (a k) -> p a k", k=4),
                          in_=conv_w.rearrange("(a p) k -> p a k", p=128)), "qG", 16)

    @blk.vector
    def _(v):
        v.memset(onescb[:], 1.0)
        v.memset(rmk[32:64, :], 1.0)
        v.memset(statS2[:], 0.0)
        v.memset(statS4[:], 0.0)
        v.memset(hs2[:], 0.0)
        v.memset(hs4[:], 0.0)
        v.memset(onescf[:], 1.0)
        v.memset(ones128b[:], 1.0)
        v.memset(halfcol[:], 0.5)
        c.wait(v, "eG")
        v.tensor_copy(out=mkb4[:, 128:256], in_=mkb4[:, 0:128])
        v.tensor_copy(out=mkb4[:, 256:384], in_=mkb4[:, 0:128])
        v.tensor_copy(out=mkb4[:, 384:512], in_=mkb4[:, 0:128])
        c.wait(v, "qG")
        v.tensor_copy(out=lsif[:], in_=lsis[:])
        v.tensor_single_scalar(out=lsif[:], in_=lsif[:], scalar=0.0, op=ALU.is_le)
        v.reduce_sum(out=cnt1[:], in_=lsif[:], axis=AXX)
        v.tensor_scalar(out=cnt1[:], in0=cnt1[:], scalar1=-1.0, scalar2=None, op0=ALU.add)
        c.inc(v.tensor_scalar(out=msk13[:], in0=cst13[:], scalar1=cnt1[:], scalar2=None,
                              op0=ALU.is_equal), "eV")
    c.bar()

    @blk.tensor
    def _(t):
        c.inc(t.transpose(p6[:, 0:67], prm[0:67, :], idf[0:67, 0:67]), "eT")
        c.inc(t.transpose(p6[:, 67:99], prm2[0:32, :], idf[0:32, 0:32]), "eT")
        c.inc(t.transpose(p6[0:64, 99:100], prm3[0:1, 0:64], idf[0:1, 0:1]), "eT")
        c.inc(t.transpose(p6[0:64, 100:101], prm3[0:1, 64:128], idf[0:1, 0:1]), "eT")
        c.inc(t.transpose(p6[0:3, 102:103], msk13[:], idf[0:1, 0:1]), "eT")
        for i in range(16):
            c.inc(t.transpose(pT[:, 48 * i:48 * (i + 1)], LAB[0:R48, 128 * i:128 * (i + 1)],
                              idb[0:R48, 0:R48]), "eT")
    c.bar()

    @blk.vector
    def _(v):
        v.tensor_copy(out=mupc[:], in_=p6[:, 0:67])
        v.tensor_copy(out=lnwc[:], in_=p6[:, 67:99])
        v.tensor_copy(out=dtbias_c[:], in_=p6[0:64, 99:100])
        c.inc(v.tensor_copy(out=ATf[:, 0:768], in_=pT[:, 0:768]), "eV")

    @blk.scalar
    def _(s):
        s.activation(out=nA_c[:], in_=p6[0:64, 100:101], func=ACT.Exp)
        c.inc(s.activation(out=nA_c[:], in_=nA_c[:], func=ACT.Copy, scale=-1.0), "eS")
    c.bar()

    @blk.tensor
    def _(t):
        for i in range(16):
            c.inc(t.transpose(pT[:, 48 * i:48 * (i + 1)], LAB[0:R48, 128 * (16 + i):128 * (17 + i)],
                              idb[0:R48, 0:R48]), "eT")
        c.inc(t.matmul(p6[:, 103:106], onescf[:], msk13[:], start=True, stop=True), "eT")
        c.inc(t.matmul(p6[:, 106:170], onescf[:], prm3[0:1, 128:192], start=True, stop=True), "eT")
    c.bar()

    @blk.vector
    def _(v):
        c.inc(v.tensor_copy(out=ATf[:, 768:1536], in_=pT[:, 0:768]), "eV")
        for i in range(32):
            for a in range(3):
                c.inc(v.tensor_scalar(out=ATf[:, 48 * i + 16 * a:48 * i + 16 * (a + 1)],
                                      in0=ATf[:, 48 * i + 16 * a:48 * i + 16 * (a + 1)],
                                      scalar1=p6[:, 103 + a:104 + a], scalar2=None,
                                      op0=ALU.mult), "eV")
        dr = p6[:, 106:170]
        c.inc(v.tensor_copy(out=Dcol[0:64, :], in_=dr[0:64, :].rearrange("p (i two) -> p i two", two=2)[:, :, 0:1].squeeze(2)), "eV")
        c.inc(v.tensor_copy(out=Dcol[64:128, :], in_=dr[64:128, :].rearrange("p (i two) -> p i two", two=2)[:, :, 1:2].squeeze(2)), "eV")
        for c34 in range(34):
            c.inc(v.tensor_scalar(out=convw[:, 4 * c34:4 * (c34 + 1)],
                                  in0=convw[:, 4 * c34:4 * (c34 + 1)],
                                  scalar1=mupc[:, 32 + c34:33 + c34], scalar2=None,
                                  op0=ALU.mult), "eV")
    c.bar()

    for a in range(3):
        for w in range(4):
            @blk.tensor
            def _(t, a=a, w=w):
                for j in range(8):
                    i = 8 * w + j
                    c.inc(t.transpose(pT[0:16, 128 * j:128 * (j + 1)],
                                      LBB[:, 16 * (32 * a + i):16 * (32 * a + i + 1)],
                                      idb[:]), "eT")
            c.bar()

            @blk.vector
            def _(v, a=a, w=w):
                c.inc(v.tensor_copy(out=btmp, in_=pT[0:16, 0:1024]), "eV")
            c.bar()

            @blk.gpsimd
            def _(g, a=a, w=w):
                c.inc(g.dma_start(out=BTFB[16 * a:16 * (a + 1), 1024 * w:1024 * (w + 1)],
                                  in_=btmp), "qG", 16)
            c.bar()

    if upto < 1:
        es.close()
        return nc, dbg

    def emit_colstats(sumc, sqc, rms_fused=False):
        def vsec(vv):
            inv = 1.0 / H
            vv.tensor_scalar(out=colM[:], in0=sumc, scalar1=inv, scalar2=None, op0=ALU.mult)
            vv.drain()
            vv.tensor_scalar(out=colTa[:], in0=sqc, scalar1=inv, scalar2=None, op0=ALU.mult)
            vv.drain()
            vv.tensor_tensor(out=colTb[:], in0=colM[:], in1=colM[:], op=ALU.mult)
            vv.drain()
            vv.tensor_tensor(out=colTa[:], in0=colTa[:], in1=colTb[:], op=ALU.subtract)
            vv.drain()
            vv.tensor_scalar(out=colTb[:], in0=colTa[:], scalar1=EPS, scalar2=None, op0=ALU.add)
            vv.drain()
            vv.reciprocal(out=colTb[:], in_=colTb[:])
            c.inc(vv.drain(), "eV")
        blk.vector(vsec)

        def ssec(ss):
            c.wait(ss, "eV")
            ss.activation(out=colR[:], in_=colTb[:], func=ACT.Sqrt)
            c.inc(ss.drain(), "eS")
        blk.scalar(ssec)

        if rms_fused:
            def vsec2(vv):
                c.wait(vv, "eS")
                vv.tensor_tensor(out=colTa[:], in0=colTa[:], in1=colTb[:], op=ALU.mult)
                vv.drain()
                vv.tensor_scalar(out=colTa[:], in0=colTa[:], scalar1=EPS, scalar2=None, op0=ALU.add)
                vv.drain()
                vv.reciprocal(out=colTa[:], in_=colTa[:])
                c.inc(vv.drain(), "eV")
            blk.vector(vsec2)

            def ssec2(ss):
                c.wait(ss, "eV")
                ss.activation(out=colTa[:], in_=colTa[:], func=ACT.Sqrt)
                c.inc(ss.drain(), "eS")
            blk.scalar(ssec2)

            def vsec2b(vv):
                c.wait(vv, "eS")
                vv.tensor_tensor(out=colR[:], in0=colR[:], in1=colTa[:], op=ALU.mult)
                c.inc(vv.drain(), "eV")
            blk.vector(vsec2b)

        def vsec3(vv):
            c.wait(vv, "eS")
            c.wait(vv, "eV")
            vv.tensor_tensor(out=colTa[:], in0=colM[:], in1=colR[:], op=ALU.mult)
            vv.drain()
            vv.tensor_scalar(out=colN[:], in0=colTa[:], scalar1=-1.0, scalar2=None, op0=ALU.mult)
            c.inc(vv.drain(), "eV")
        blk.vector(vsec3)

    def emit_ln_pass(src_dma, rms_fused, use_partials):
        for tt in range(8):
            @blk.scalar
            def _(s, tt=tt):
                if tt == 0:
                    c.inc(s.dma_start(out=xst, in_=src_dma(0)), "qC", 16)
                if not use_partials:
                    c.wait(s, "qC", c.mark("qC"))
                    for j in range(8):
                        s.activation(out=scr[:], in_=xst[:, 512 * j:512 * (j + 1)], func=ACT.Square,
                                     accum_out=statS2[:, 8 * tt + j:8 * tt + j + 1])
                    c.inc(s.drain(), "eS")

            @blk.vector
            def _(v, tt=tt):
                c.wait(v, "qC", c.mark("qC"))
                if use_partials:
                    v.reduce_sum(out=colS[:], in_=statS3[:, 8 * tt:8 * tt + 8], axis=AXX)
                    v.drain()
                    v.reduce_sum(out=colQ[:], in_=statS4[:, 8 * tt:8 * tt + 8], axis=AXX)
                    c.inc(v.drain(), "eV")
                else:
                    v.reduce_sum(out=colS[:], in_=xst, axis=AXX)
                    v.drain()
                    c.wait(v, "eS")
                    v.reduce_sum(out=colQ[:], in_=statS2[:, 8 * tt:8 * tt + 8], axis=AXX)
                    c.inc(v.drain(), "eV")

            emit_colstats(colS[:], colQ[:], rms_fused)

            @blk.vector
            def _(v, tt=tt):
                c.wait(v, "eV")
                c.wait(v, "eS")
                c.wait(v, "qC", c.mark("qC"))
                c.wait(v, "qS", c.mark("qS"))
                v.tensor_scalar(out=lnxtok[:], in0=xst, scalar1=colR[:], scalar2=colN[:],
                                op0=ALU.mult, op1=ALU.add)
                c.inc(v.drain(), "eV")
            nm = c.mark("eV")

            @blk.scalar
            def _(s, tt=tt, nm=nm):
                c.wait(s, "eV", nm)
                if tt < 7:
                    c.inc(s.dma_start(out=xst, in_=src_dma(tt + 1)), "qC", 16)

            @blk.sync
            def _(sy, tt=tt, nm=nm):
                c.wait(sy, "eV", nm)
                for i in range(32):
                    c.inc(sy.dma_start(out=LNX[:, LW * i + 128 * tt:LW * i + 128 * (tt + 1)],
                                       in_=lnxtok[:, 128 * i:128 * (i + 1)], transpose=True),
                          "qS", 16)

    def emit_halo_ln(xsrc_ap, dst_bf, rms_fused, sq_buf):
        @blk.scalar
        def _(s):
            for j in range(8):
                s.activation(out=hscr[:], in_=xsrc_ap[:, 512 * j:512 * (j + 1)], func=ACT.Square,
                             accum_out=sq_buf[:, j:j + 1])
            c.inc(s.drain(), "eS")

        @blk.vector
        def _(v):
            c.wait(v, "eS")
            v.reduce_sum(out=hcolM[:], in_=xsrc_ap, axis=AXX)
            v.drain()
            v.reduce_sum(out=hcolTb[:], in_=sq_buf[:], axis=AXX)
            v.drain()
            inv = 1.0 / H
            v.tensor_scalar(out=hcolM[:], in0=hcolM[:], scalar1=inv, scalar2=None, op0=ALU.mult)
            v.drain()
            v.tensor_scalar(out=hcolTa[:], in0=hcolTb[:], scalar1=inv, scalar2=None, op0=ALU.mult)
            v.drain()
            v.tensor_tensor(out=hcolTb[:], in0=hcolM[:], in1=hcolM[:], op=ALU.mult)
            v.drain()
            v.tensor_tensor(out=hcolTa[:], in0=hcolTa[:], in1=hcolTb[:], op=ALU.subtract)
            v.drain()
            v.tensor_scalar(out=hcolTb[:], in0=hcolTa[:], scalar1=EPS, scalar2=None, op0=ALU.add)
            v.drain()
            v.reciprocal(out=hcolTb[:], in_=hcolTb[:])
            c.inc(v.drain(), "eV")

        @blk.scalar
        def _(s):
            c.wait(s, "eV")
            s.activation(out=hcolR[:], in_=hcolTb[:], func=ACT.Sqrt)
            c.inc(s.drain(), "eS")

        if rms_fused:
            @blk.vector
            def _(v):
                c.wait(v, "eS")
                v.tensor_tensor(out=hcolTa[:], in0=hcolTa[:], in1=hcolTb[:], op=ALU.mult)
                v.drain()
                v.tensor_scalar(out=hcolTa[:], in0=hcolTa[:], scalar1=EPS, scalar2=None, op0=ALU.add)
                v.drain()
                v.reciprocal(out=hcolTa[:], in_=hcolTa[:])
                c.inc(v.drain(), "eV")

            @blk.scalar
            def _(s):
                c.wait(s, "eV")
                s.activation(out=hcolTa[:], in_=hcolTa[:], func=ACT.Sqrt)
                c.inc(s.drain(), "eS")

            @blk.vector
            def _(v):
                c.wait(v, "eS")
                v.tensor_tensor(out=hcolR[:], in0=hcolR[:], in1=hcolTa[:], op=ALU.mult)
                c.inc(v.drain(), "eV")

        @blk.vector
        def _(v):
            c.wait(v, "eS")
            c.wait(v, "eV")
            v.tensor_tensor(out=hcolTa[:], in0=hcolM[:], in1=hcolR[:], op=ALU.mult)
            v.drain()
            v.tensor_scalar(out=hcolN[:], in0=hcolTa[:], scalar1=-1.0, scalar2=None, op0=ALU.mult)
            v.drain()
            v.tensor_scalar(out=dst_bf, in0=xsrc_ap, scalar1=hcolR[:], scalar2=hcolN[:],
                            op0=ALU.mult, op1=ALU.add)
            c.inc(v.drain(), "eV")
        c.bar()

        for w in range(4):
            @blk.tensor
            def _(t, w=w):
                for j in range(8):
                    i = 8 * w + j
                    c.inc(t.transpose(pT[:, 128 * j:128 * j + 3],
                                      dst_bf[:, 128 * i:128 * (i + 1)], idb[0:3, 0:3]), "eT")
            c.bar()

            @blk.vector
            def _(v, w=w):
                for j in range(8):
                    i = 8 * w + j
                    c.inc(v.tensor_copy(out=LNX[:, LW * i + 1024:LW * i + 1027],
                                        in_=pT[:, 128 * j:128 * j + 3]), "eV")
            c.bar()

    emit_ln_pass(lambda tt: xin[3 + 128 * tt:3 + 128 * (tt + 1), :], False, False)
    c.bar()

    @blk.gpsimd
    def _(g):
        c.inc(g.dma_start(out=xhb1, in_=xin[0:3, :]), "qG", 16)
    c.bar()
    emit_halo_ln(xhb1, hlnb1, False, hs2)

    if debug and upto == 1:
        @blk.gpsimd
        def _(g):
            for i in range(32):
                c.inc(g.dma_start(out=dbg["lnx"][i], in_=LNX[:, LW * i:LW * i + LW]), "qG", 16)
            c.wait(g, "qG")
        es.close()
        return nc, dbg

    @blk.tensor
    def _(t):
        for th in range(2):
            pp = p4 if th == 0 else p5
            for k in range(NKT):
                t.matmul(pp[0:R48, :], ATf[:, R48 * k:R48 * (k + 1)],
                         LNX[:, LW * k + 512 * th:LW * k + 512 * (th + 1)],
                         start=(k == 0), stop=(k == NKT - 1))
        for k in range(NKT):
            mi = t.matmul(p6[0:R48, 0:3], ATf[:, R48 * k:R48 * (k + 1)],
                          LNX[:, LW * k + 1024:LW * k + 1027],
                          start=(k == 0), stop=(k == NKT - 1))
        c.inc(mi, "eT")

    @blk.vector
    def _(v):
        c.wait(v, "eT")
        v.tensor_copy(out=rmk[0:R48, 0:512], in_=p4[0:R48, :])
        v.tensor_copy(out=rmk[0:R48, 512:1024], in_=p5[0:R48, :])
        v.tensor_copy(out=rmk[0:R48, 1024:1027], in_=p6[0:R48, 0:3])
        c.inc(v.drain(), "eV")
    c.bar()

    dmat_mark = {}
    cast_mark = {}
    evac_mark = {}
    sq_mark = {}
    x1w_mark = {}
    xld_mark = {}
    mm_mark = {}
    hevac_mark = {}
    hmm_mark = {}
    h512_mark = {}

    for oc in range(8):
        for sidx in range(4):
            @blk.gpsimd
            def _(g, oc=oc, sidx=sidx):
                prev = (oc, sidx - 2) if sidx >= 2 else (oc - 1, sidx + 2)
                if prev in dmat_mark:
                    c.wait(g, "qS", dmat_mark[prev])
                c.inc(g.dma_start(out=OTB[sidx % 2],
                                  in_=ffn_w[512 * oc + 128 * sidx:512 * oc + 128 * (sidx + 1), :]),
                      "qG", 16)
                cast_mark[(oc, sidx)] = c.mark("qG")

            @blk.sync
            def _(sy, oc=oc, sidx=sidx):
                c.wait(sy, "qG", cast_mark[(oc, sidx)])
                if (oc - 2) in hmm_mark:
                    c.wait(sy, "eT", hmm_mark[oc - 2])
                for k in range(NKT):
                    c.inc(sy.dma_start(out=WFF[oc % 2][:, 512 * k + 128 * sidx:512 * k + 128 * (sidx + 1)],
                                       in_=OTB[sidx % 2][:, 128 * k:128 * (k + 1)],
                                       transpose=True), "qS", 16)
                dmat_mark[(oc, sidx)] = c.mark("qS")

        for tt in range(8):
            @blk.scalar
            def _(s, oc=oc, tt=tt):
                prev = (oc, tt - 2) if tt >= 2 else (oc - 1, tt + 6)
                if prev in evac_mark:
                    c.wait(s, "eV", evac_mark[prev])
                c.inc(s.dma_start(out=xcb[:, 512 * (tt % 2):512 * (tt % 2 + 1)],
                                  in_=xin[3 + 128 * tt:3 + 128 * (tt + 1), 512 * oc:512 * (oc + 1)]),
                      "qC", 16)
                xld_mark[(oc, tt)] = c.mark("qC")

            @blk.tensor
            def _(t, oc=oc, tt=tt):
                if tt == 0:
                    c.wait(t, "qS", dmat_mark[(oc, 3)])
                bank = PS[tt % 4]
                prevg = (oc, tt - 4) if tt >= 4 else (oc - 1, tt + 4)
                if prevg in evac_mark:
                    c.wait(t, "eV", evac_mark[prevg])
                for k in range(NKT):
                    t.matmul(bank[:], LNX[:, LW * k + 128 * tt:LW * k + 128 * (tt + 1)],
                             WFF[oc % 2][:, 512 * k:512 * (k + 1)],
                             start=(k == 0), stop=False)
                c.inc(t.matmul(bank[:], rmk[0:49, 128 * tt:128 * (tt + 1)],
                               BTFB[0:49, 512 * oc:512 * (oc + 1)], start=False, stop=True), "eT")
                mm_mark[(oc, tt)] = c.mark("eT")

            @blk.vector
            def _(v, oc=oc, tt=tt):
                c.wait(v, "eT", mm_mark[(oc, tt)])
                c.wait(v, "qC", xld_mark[(oc, tt)])
                prevslot = (oc, tt - 2) if tt >= 2 else (oc - 1, tt + 6)
                if prevslot in x1w_mark:
                    c.wait(v, "qW", x1w_mark[prevslot])
                if prevslot in sq_mark:
                    c.wait(v, "eS", sq_mark[prevslot])
                slot = x1b[:, 512 * (tt % 2):512 * (tt % 2 + 1)]
                v.scalar_tensor_tensor(out=slot, in0=PS[tt % 4][:], scalar=halfcol[:],
                                       in1=xcb[:, 512 * (tt % 2):512 * (tt % 2 + 1)],
                                       op0=ALU.mult, op1=ALU.add)
                v.drain()
                v.reduce_sum(out=statS3[:, 8 * tt + oc:8 * tt + oc + 1], in_=slot, axis=AXX)
                c.inc(v.drain(), "eV")
                evac_mark[(oc, tt)] = c.mark("eV")

            @blk.scalar
            def _(s, oc=oc, tt=tt):
                c.wait(s, "eV", evac_mark[(oc, tt)])
                s.activation(out=scr[:], in_=x1b[:, 512 * (tt % 2):512 * (tt % 2 + 1)],
                             func=ACT.Square, accum_out=statS4[:, 8 * tt + oc:8 * tt + oc + 1])
                c.inc(s.drain(), "eS")
                sq_mark[(oc, tt)] = c.mark("eS")

            @blk.gpsimd
            def _(g, oc=oc, tt=tt):
                c.wait(g, "eV", evac_mark[(oc, tt)])
                c.inc(g.dma_start(out=o_x1[128 * tt:128 * (tt + 1), 512 * oc:512 * (oc + 1)],
                                  in_=x1b[:, 512 * (tt % 2):512 * (tt % 2 + 1)]), "qW", 16)
                x1w_mark[(oc, tt)] = c.mark("qW")

        @blk.scalar
        def _(s, oc=oc):
            if (oc - 1) in hevac_mark:
                c.wait(s, "eV", hevac_mark[oc - 1])
            c.inc(s.dma_start(out=hscr[:], in_=xin[0:3, 512 * oc:512 * (oc + 1)]), "qC", 16)

        @blk.tensor
        def _(t, oc=oc):
            if (oc - 1) in hevac_mark:
                c.wait(t, "eV", hevac_mark[oc - 1])
            for k in range(NKT):
                t.matmul(p6[0:3, 0:512], LNX[:, LW * k + 1024:LW * k + 1027],
                         WFF[oc % 2][:, 512 * k:512 * (k + 1)], start=(k == 0), stop=False)
            c.inc(t.matmul(p6[0:3, 0:512], rmk[0:49, 1024:1027],
                           BTFB[0:49, 512 * oc:512 * (oc + 1)], start=False, stop=True), "eT")
            hmm_mark[oc] = c.mark("eT")

        @blk.vector
        def _(v, oc=oc):
            c.wait(v, "eT", hmm_mark[oc])
            c.wait(v, "qC")
            if oc > 0:
                c.wait(v, "qW", h512_mark[oc - 1])
            v.scalar_tensor_tensor(out=h512[:], in0=p6[0:3, 0:512],
                                   scalar=halfcol[0:3, 0:1], in1=hscr[:], op0=ALU.mult, op1=ALU.add)
            c.inc(v.drain(), "eV")
            hevac_mark[oc] = c.mark("eV")

        @blk.gpsimd
        def _(g, oc=oc):
            c.wait(g, "eV", hevac_mark[oc])
            c.inc(g.dma_start(out=x1h_dram[:, 512 * oc:512 * (oc + 1)], in_=h512[:]), "qW", 16)
            h512_mark[oc] = c.mark("qW")
    c.bar()

    if debug and upto == 2:
        @blk.gpsimd
        def _(g):
            c.inc(g.dma_start(out=dbg["x1h"], in_=x1h_dram[:]), "qG", 16)
            c.inc(g.dma_start(out=dbg["wff"], in_=WFF[1]), "qG", 16)
            c.inc(g.dma_start(out=dbg["rmk"], in_=rmk[0:64, :]), "qG", 16)
            c.wait(g, "qG")
        es.close()
        return nc, dbg

    @blk.vector
    def _(v):
        for c34 in range(34):
            for k4 in range(4):
                d = 4 * c34 + k4
                v.tensor_scalar(out=DIAG[:, 128 * d:128 * (d + 1)], in0=idb[:],
                                scalar1=convw[:, d:d + 1], scalar2=None, op0=ALU.mult)
        c.inc(v.drain(), "eV")

    emit_ln_pass(lambda tt: o_x1[128 * tt:128 * (tt + 1), :], True, True)
    c.bar()

    @blk.gpsimd
    def _(g):
        c.inc(g.dma_start(out=x1hbuf, in_=x1h_dram), "qG", 16)
    c.bar()
    emit_halo_ln(x1hbuf, hlnb3, True, hs4)

    @blk.vector
    def _(v):
        for i in range(32):
            v.tensor_scalar(out=LNX[:, LW * i:LW * i + 1027], in0=LNX[:, LW * i:LW * i + 1027],
                            scalar1=lnwc[:, i:i + 1], scalar2=None, op0=ALU.mult)
        c.inc(v.drain(), "eV")
    c.bar()

    if debug and upto == 3:
        @blk.gpsimd
        def _(g):
            for i in range(32):
                c.inc(g.dma_start(out=dbg["lnx"][i], in_=LNX[:, LW * i:LW * i + LW]), "qG", 16)
            c.wait(g, "qG")
        es.close()
        return nc, dbg

    wp_mark = {}
    ipmm_mark = {}

    MKSTOP = 9
    for p in range(34):
        tiles = [66] if p == 33 else [2 * p, 2 * p + 1]
        width = 64 if p == 33 else 256
        colbase = 8448 if p == 33 else 256 * p
        is_xbc = 32 <= tiles[0] < 66

        @blk.gpsimd
        def _(g, p=p, width=width, colbase=colbase):
            if (p - 2) in ipmm_mark:
                c.wait(g, "eT", ipmm_mark[p - 2])
            for k in range(NKT):
                c.inc(g.dma_start(
                    out=WP[p % 2][:, k * width:(k + 1) * width],
                    in_=ipw[128 * k:128 * (k + 1), colbase:colbase + width]), "qG", 16)
            wp_mark[p] = c.mark("qG")

        if MKSTOP < 2:
            ipmm_mark[p] = c.mark("eT")
            continue

        @blk.tensor
        def _(t, p=p, tiles=tiles, width=width, is_xbc=is_xbc):
            c.wait(t, "qG", wp_mark[p])
            c.wait(t, "eS")
            c.wait(t, "eV")
            for ti, o in enumerate(tiles):
                m = 64 if o == 66 else 128
                for th in range(2):
                    bank = PS[2 * ti + th]
                    for k in range(NKT):
                        mi = t.matmul(bank[0:m, :],
                                      WP[p % 2][:, k * width + 128 * ti:k * width + 128 * ti + m],
                                      LNX[:, LW * k + 512 * th:LW * k + 512 * (th + 1)],
                                      start=(k == 0), stop=(k == NKT - 1))
            if is_xbc:
                for k in range(NKT):
                    mi = t.matmul(p6[0:3, 0:width], LNX[:, LW * k + 1024:LW * k + 1027],
                                  WP[p % 2][:, k * width:(k + 1) * width],
                                  start=(k == 0), stop=(k == NKT - 1))
            c.inc(mi, "eT")
            ipmm_mark[p] = c.mark("eT")

        if MKSTOP < 3:
            continue

        @blk.scalar
        def _(s, p=p, tiles=tiles, width=width, is_xbc=is_xbc):
            c.wait(s, "eT", ipmm_mark[p])
            c.wait(s, "qW")
            for ti, o in enumerate(tiles):
                if o < 32:
                    zb = xstmp if ti == 0 else xbctmp[:, 0:1024]
                    s.activation(out=zb[:, 0:512], in_=PS[2 * ti][:],
                                 func=ACT.Silu, scale=mupc[:, o:o + 1])
                    s.activation(out=zb[:, 512:1024], in_=PS[2 * ti + 1][:],
                                 func=ACT.Silu, scale=mupc[:, o:o + 1])
                elif o < 66:
                    buf = xbctmp if ti == 0 else xbc2tmp
                    s.copy(out=hzx[:, 128 * (o - 32):128 * (o - 31)],
                           in_=p6[0:3, 128 * ti:128 * (ti + 1)])
                    s.copy(out=buf[:, 3:515], in_=PS[2 * ti][:])
                    s.copy(out=buf[:, 515:1027], in_=PS[2 * ti + 1][:])
                else:
                    s.activation(out=dtraw[0:64, 0:512], in_=PS[0][0:64, :], func=ACT.Identity,
                                 bias=dtbias_c[:], scale=mupc[0:64, 66:67])
                    s.activation(out=dtraw[0:64, 512:1024], in_=PS[1][0:64, :], func=ACT.Identity,
                                 bias=dtbias_c[:], scale=mupc[0:64, 66:67])
            c.inc(s.drain(), "eS")
        semark = c.mark("eS")

        if tiles[0] < 32 and MKSTOP >= 4:
            @blk.gpsimd
            def _(g, p=p, tiles=tiles, semark=semark):
                c.wait(g, "eS", semark)
                for ti, o in enumerate(tiles):
                    zb = xstmp if ti == 0 else xbctmp[:, 0:1024]
                    c.inc(g.dma_start(out=o_siluz.rearrange("(i pp) t -> i pp t", pp=128)[o],
                                      in_=zb[:]), "qW", 16)

        if is_xbc:
            @blk.tensor
            def _(t, p=p, tiles=tiles, semark=semark):
                c.wait(t, "eS", semark)
                for ti, o in enumerate(tiles):
                    c.inc(t.transpose(pT[:, 256 * ti:256 * ti + 3],
                                      hzx[:, 128 * (o - 32):128 * (o - 31)],
                                      idb[0:3, 0:3]), "eT")
            tmark = c.mark("eT")

            @blk.vector
            def _(v, p=p, tiles=tiles, tmark=tmark):
                c.wait(v, "eT", tmark)
                for ti, o in enumerate(tiles):
                    buf = xbctmp if ti == 0 else xbc2tmp
                    v.tensor_tensor(out=buf[:, 0:3], in0=pT[:, 256 * ti:256 * ti + 3],
                                    in1=halom_s[:, 0:3], op=ALU.mult)
                c.inc(v.drain(), "eV")
            vmark = c.mark("eV")

            for ti in range(2):
                o = tiles[ti]

                @blk.tensor
                def _(t, p=p, o=o, ti=ti, vmark=vmark):
                    c.wait(t, "eV", vmark)
                    c.wait(t, "eS")
                    buf = xbctmp if ti == 0 else xbc2tmp
                    for th in range(2):
                        bank = p4 if th == 0 else p5
                        for k4 in range(4):
                            d = 4 * (o - 32) + k4
                            mi = t.matmul(bank[:], DIAG[:, 128 * d:128 * (d + 1)],
                                          buf[:, 512 * th + k4:512 * th + k4 + 512],
                                          start=(k4 == 0), stop=(k4 == 3))
                    c.inc(mi, "eT")
                cmark = c.mark("eT")

                @blk.scalar
                def _(s, p=p, o=o, cmark=cmark):
                    c.wait(s, "eT", cmark)
                    c.wait(s, "qW")
                    if o == 64:
                        s.activation(out=Bfm[:, 0:512], in_=p4[:], func=ACT.Silu)
                        s.activation(out=Bfm[:, 512:1024], in_=p5[:], func=ACT.Silu)
                    elif o == 65:
                        s.activation(out=Cfm[:, 0:512], in_=p4[:], func=ACT.Silu)
                        s.activation(out=Cfm[:, 512:1024], in_=p5[:], func=ACT.Silu)
                    else:
                        s.activation(out=xstmp[:, 0:512], in_=p4[:], func=ACT.Silu)
                        s.activation(out=xstmp[:, 512:1024], in_=p5[:], func=ACT.Silu)
                    c.inc(s.drain(), "eS")
                smark = c.mark("eS")

                if o < 64:
                    @blk.gpsimd
                    def _(g, o=o, smark=smark):
                        c.wait(g, "eS", smark)
                        c.inc(g.dma_start(
                            out=xs_dram.rearrange("(i pp) t -> i pp t", pp=128)[o - 32],
                            in_=xstmp[:]), "qW", 16)

        if tiles[0] == 66:
            @blk.vector
            def _(v, p=p):
                c.wait(v, "eS")
                v.tensor_scalar(out=dtabs, in0=dtraw[0:64, :], scalar1=-1.0, scalar2=None,
                                op0=ALU.mult)
                v.drain()
                v.tensor_tensor(out=dtabs, in0=dtabs, in1=dtraw[0:64, :], op=ALU.max)
                c.inc(v.drain(), "eV")

            @blk.scalar
            def _(s, p=p):
                c.wait(s, "eV")
                s.activation(out=dtabs, in_=dtabs, func=ACT.Exp, scale=-1.0)
                s.drain()
                s.activation(out=dtabs, in_=dtabs, func=ACT.Ln, bias=1.0)
                c.inc(s.drain(), "eS")

            @blk.vector
            def _(v, p=p):
                c.wait(v, "eS")
                v.tensor_single_scalar(out=dtf[:], in_=dtraw[0:64, :], scalar=0.0, op=ALU.max)
                v.drain()
                v.tensor_tensor(out=dtf[:], in0=dtf[:], in1=dtabs, op=ALU.add)
                c.inc(v.drain(), "eV")
    c.bar()

    @blk.gpsimd
    def _(g):
        c.inc(g.dma_start(out=o_C, in_=Cfm[:]), "qW", 16)
        c.wait(g, "qW")
    c.bar()

    if debug and upto == 4:

        @blk.gpsimd
        def _(g):
            c.inc(g.dma_start(out=dbg["B"], in_=Bfm[:]), "qG", 16)
            c.inc(g.dma_start(out=dbg["dt"], in_=dtf[:]), "qG", 16)
            c.inc(g.dma_start(out=dbg["zz"], in_=xstmp[:]), "qG", 16)
            for i in range(32):
                c.inc(g.dma_start(out=dbg["xs"].rearrange("(i pp) t -> i pp t", pp=128)[i],
                                  in_=xs_dram.rearrange("(i pp) t -> i pp t", pp=128)[i]), "qG", 16)
            c.wait(g, "qG")
        es.close()
        return nc, dbg

    XC = XCH[0]

    @blk.vector
    def _(v):
        v.memset(A2[:, 20480:32768], 0.0)
        v.memset(prm3[0:1, 0:64], 0.0)
        v.tensor_scalar(out=x1b[0:64, :], in0=dtf[:], scalar1=nA_c[:], scalar2=None,
                        op0=ALU.mult)
        v.drain()
        for qq in range(NCH):
            v.tensor_tensor_scan(out=cumall[:, Q * qq:Q * (qq + 1)], data0=ones64[:],
                                 data1=x1b[0:64, Q * qq:Q * (qq + 1)], initial=0.0,
                                 op0=ALU.mult, op1=ALU.add)
        c.inc(v.drain(), "eV")

    @blk.gpsimd
    def _(g):
        c.wait(g, "eV")
        c.inc(g.dma_start(out=flat_dram.rearrange("h c t -> h (c t)"), in_=cumall[:]), "qW", 16)

    @blk.tensor
    def _(t):
        c.inc(t.matmul(p6[:, 0:64], onescf[:], prm3[0:1, 128:192], start=True, stop=True), "eT")

    @blk.vector
    def _(v):
        c.wait(v, "eT")
        v.tensor_copy(out=Dtok[:], in_=p6[:, 0:64])
        c.inc(v.drain(), "eV")
    c.bar()

    for q in range(NCH):
        @blk.scalar
        def _(s, q=q):
            for i in range(32):
                c.inc(s.dma_start(out=XPP[:, 128 * i:128 * (i + 1)],
                                  in_=xs_dram.rearrange("(i pp) t -> i pp t", pp=128)[i][:, Q * q:Q * (q + 1)]),
                      "qC", 16)
        xstm = c.mark("qC")

        @blk.gpsimd
        def _(g, q=q):
            for i in range(32):
                c.inc(g.dma_start(out=G1CH[:, 128 * i:128 * (i + 1)],
                                  in_=o_siluz.rearrange("(i pp) t -> i pp t", pp=128)[i][:, Q * q:Q * (q + 1)]),
                      "qG", 16)
            c.inc(g.dma_start(out=cumflat[0:1, :], in_=flat_dram[:, q, :]), "qG", 16)
        szlm = c.mark("qG")

        @blk.sync
        def _(sy, q=q, xstm=xstm, szlm=szlm):
            c.wait(sy, "qC", xstm)
            c.wait(sy, "qG", szlm)
            for i in range(32):
                c.inc(sy.dma_start(out=XC[:, 128 * i:128 * (i + 1)],
                                   in_=XPP[:, 128 * i:128 * (i + 1)], transpose=True), "qS", 16)
            for i in range(32):
                c.inc(sy.dma_start(out=HTs[:, 128 * i:128 * (i + 1)],
                                   in_=G1CH[:, 128 * i:128 * (i + 1)], transpose=True), "qS", 16)
            c.inc(sy.dma_start(out=Btok[:], in_=Bfm[:, Q * q:Q * (q + 1)], transpose=True), "qS", 16)
        szm = c.mark("qS")

        @blk.tensor
        def _(t, q=q):
            c.inc(t.transpose(p6[:, 64:128], cumall[:, Q * q:Q * (q + 1)], idf[0:64, 0:64]), "eT")
            c.inc(t.transpose(p6[:, 128:192], dtf[:, Q * q:Q * (q + 1)], idf[0:64, 0:64]), "eT")
            c.inc(t.transpose(p6[0:1, 320:384], cumall[:, Q * q + 127:Q * q + 128],
                              idf[0:64, 0:64]), "eT")
        tam = c.mark("eT")

        @blk.vector
        def _(v, q=q, tam=tam):
            c.wait(v, "eT", tam)
            v.tensor_copy(out=cumT[:], in_=p6[:, 64:128])
            v.tensor_copy(out=dtT[:], in_=p6[:, 128:192])
            v.tensor_copy(out=ecrow[:], in_=p6[0:1, 320:384])
            v.drain()
            v.tensor_scalar(out=ncumT[:], in0=cumT[:], scalar1=-1.0, scalar2=None, op0=ALU.mult)
            c.inc(v.drain(), "eV")
        vam = c.mark("eV")

        @blk.scalar
        def _(s, q=q, vam=vam):
            c.wait(s, "eV", vam)
            s.activation(out=ECtok[:], in_=cumT[:], func=ACT.Exp)
            c.inc(s.drain(), "eS")
        ecm = c.mark("eS")

        @blk.tensor
        def _(t, q=q, vam=vam):
            c.wait(t, "eV", vam)
            c.inc(t.matmul(p6[:, 192:256], onescf[:], ecrow[:], start=True, stop=True), "eT")
            c.inc(t.matmul(p6[:, 384:448], onescf[:], prm3[0:1, 0:64], start=True, stop=True), "eT")
        tbm = c.mark("eT")

        @blk.vector
        def _(v, q=q, tbm=tbm):
            c.wait(v, "eT", tbm)
            v.tensor_tensor(out=xcb[:, 128:192], in0=p6[:, 192:256], in1=cumT[:], op=ALU.subtract)
            v.tensor_tensor(out=ocumb[:], in0=cumT[:], in1=p6[:, 384:448], op=ALU.add)
            v.drain()
            v.tensor_tensor(out=prm3[0:1, 0:64], in0=prm3[0:1, 0:64], in1=ecrow[:], op=ALU.add)
            c.inc(v.drain(), "eV")
        vbm = c.mark("eV")

        @blk.scalar
        def _(s, q=q, vbm=vbm):
            c.wait(s, "eV", vbm)
            s.activation(out=EEs[:, 0:64], in_=xcb[:, 128:192], func=ACT.Exp)
            s.activation(out=dend[:, 0:64], in_=p6[:, 192:256], func=ACT.Exp)
            c.inc(s.drain(), "eS")
        eem = c.mark("eS")

        @blk.gpsimd
        def _(g, q=q, vbm=vbm):
            c.wait(g, "eV", vbm)
            c.inc(g.dma_start(out=o_cum[Q * q:Q * (q + 1), :], in_=ocumb[:]), "qW", 16)
            c.inc(g.dma_start(out=o_ctot[q:q + 1, :], in_=prm3[0:1, 0:64]), "qW", 16)
            c.wait(g, "qS", szm)
            c.inc(g.dma_start(out=o_sztok[Q * q:Q * (q + 1), :], in_=HTs[:]), "qW", 16)

        @blk.vector
        def _(v, q=q, eem=eem, szm=szm):
            c.wait(v, "eS", eem)
            c.wait(v, "qS", szm)
            for h in range(NH):
                v.tensor_scalar(out=XTOK[:, 64 * h:64 * (h + 1)], in0=XC[:, 64 * h:64 * (h + 1)],
                                scalar1=dtT[:, h:h + 1], scalar2=None, op0=ALU.mult)
            for h in range(NH):
                v.tensor_scalar(out=XPP[:, 64 * h:64 * (h + 1)], in0=XTOK[:, 64 * h:64 * (h + 1)],
                                scalar1=EEs[:, h:h + 1], scalar2=None, op0=ALU.mult)
            c.inc(v.drain(), "eV")
        foldm = c.mark("eV")

        @blk.tensor
        def _(t, q=q):
            c.inc(t.matmul(p4[:, 0:128], Bfm[:, Q * q:Q * (q + 1)], Cfm[:, Q * q:Q * (q + 1)],
                           start=True, stop=True), "eT")
        gtm = c.mark("eT")

        @blk.vector
        def _(v, q=q, gtm=gtm):
            c.wait(v, "eT", gtm)
            v.tensor_copy(out=C4[:, 0:128], in_=p4[:, 0:128])
            v.tensor_copy(out=C4[:, 128:256], in_=p4[:, 0:128])
            c.inc(v.drain(), "eV")
        gdm = c.mark("eV")

        repm, maskm, expm, wm, y1m, carrym, evacm = {}, {}, {}, {}, {}, {}, {}
        for ip in range(32):
            iq = ip // 4
            e0 = 256 * (ip % 2)

            if ip % 4 == 0:
                @blk.tensor
                def _(t, q=q, iq=iq):
                    if iq >= 2:
                        c.wait(t, "eV", evacm[iq - 2])
                    c.inc(t.matmul(PS[2 + iq % 2][:], Cfm[:, Q * q:Q * (q + 1)],
                                   HTb[:, 512 * iq:512 * (iq + 1)], start=True, stop=True), "eT")
                carrym[iq] = c.mark("eT")

            @blk.tensor
            def _(t, q=q, ip=ip, e0=e0, szlm=szlm):
                c.wait(t, "qG", szlm)
                if ip >= 2:
                    c.wait(t, "eV", maskm[ip - 2])
                c.inc(t.matmul(p5[:, e0:e0 + 256], onescf[:],
                               cumflat[0:1, 256 * ip:256 * (ip + 1)], start=True, stop=True), "eT")
            repm[ip] = c.mark("eT")

            @blk.vector
            def _(v, q=q, ip=ip, e0=e0):
                c.wait(v, "eT", repm[ip])
                if ip >= 2:
                    c.wait(v, "eS", expm[ip - 2])
                v.tensor_tensor(out=scr[:, e0:e0 + 256], in0=p5[:, e0:e0 + 256],
                                in1=mkb4[:, 0:256], op=ALU.add)
                c.inc(v.drain(), "eV")
            maskm[ip] = c.mark("eV")

            @blk.scalar
            def _(s, q=q, ip=ip, e0=e0):
                c.wait(s, "eV", maskm[ip])
                if ip >= 2:
                    c.wait(s, "eV", wm[ip - 2])
                s.activation(out=E4[:, e0:e0 + 128], in_=scr[:, e0:e0 + 128],
                             func=ACT.Exp, bias=ncumT[:, 2 * ip:2 * ip + 1])
                s.activation(out=E4[:, e0 + 128:e0 + 256], in_=scr[:, e0 + 128:e0 + 256],
                             func=ACT.Exp, bias=ncumT[:, 2 * ip + 1:2 * ip + 2])
                c.inc(s.drain(), "eS")
            expm[ip] = c.mark("eS")

            @blk.vector
            def _(v, q=q, ip=ip, e0=e0):
                c.wait(v, "eS", expm[ip])
                c.wait(v, "eV", gdm)
                if ip >= 2:
                    c.wait(v, "eT", y1m[ip - 2])
                v.tensor_tensor(out=W4[:, e0:e0 + 256], in0=E4[:, e0:e0 + 256],
                                in1=C4[:, 0:256], op=ALU.mult)
                c.inc(v.drain(), "eV")
            wm[ip] = c.mark("eV")

            @blk.tensor
            def _(t, q=q, ip=ip, iq=iq, e0=e0, foldm=foldm):
                c.wait(t, "eV", wm[ip])
                c.wait(t, "eV", foldm)
                if ip % 4 == 0 and iq >= 2:
                    c.wait(t, "eV", evacm[iq - 2])
                l = 2 * ip - 8 * iq
                t.matmul(PS[iq % 2][:, 64 * l:64 * (l + 1)], W4[:, e0:e0 + 128],
                         XTOK[:, 64 * (2 * ip):64 * (2 * ip + 1)], start=True, stop=True)
                c.inc(t.matmul(PS[iq % 2][:, 64 * (l + 1):64 * (l + 2)], W4[:, e0 + 128:e0 + 256],
                               XTOK[:, 64 * (2 * ip + 1):64 * (2 * ip + 2)], start=True, stop=True),
                      "eT")
            y1m[ip] = c.mark("eT")

            if ip % 4 == 3:
                @blk.vector
                def _(v, q=q, ip=ip, iq=iq, ecm=ecm, szm=szm):
                    c.wait(v, "eT", y1m[ip])
                    c.wait(v, "eT", carrym[iq])
                    c.wait(v, "eS", ecm)
                    c.wait(v, "qS", szm)
                    for l in range(8):
                        h = 8 * iq + l
                        v.tensor_scalar(out=xcb[:, 0:64], in0=PS[2 + iq % 2][:, 64 * l:64 * (l + 1)],
                                        scalar1=ECtok[:, h:h + 1], scalar2=None, op0=ALU.mult)
                        v.tensor_tensor(out=xcb[:, 64:128], in0=xcb[:, 0:64],
                                        in1=PS[iq % 2][:, 64 * l:64 * (l + 1)], op=ALU.add)
                        v.scalar_tensor_tensor(out=xcb[:, 0:64], in0=XC[:, 64 * h:64 * (h + 1)],
                                               scalar=Dtok[:, h:h + 1], in1=xcb[:, 64:128],
                                               op0=ALU.mult, op1=ALU.add)
                        v.tensor_tensor(out=G1CH[:, 64 * h:64 * (h + 1)], in0=xcb[:, 0:64],
                                        in1=HTs[:, 64 * h:64 * (h + 1)], op=ALU.mult)
                    c.inc(v.drain(), "eV")
                evacm[iq] = c.mark("eV")

        @blk.gpsimd
        def _(g, q=q):
            c.wait(g, "eV", evacm[7])
            c.inc(g.dma_start(out=o_g1[Q * q:Q * (q + 1), :], in_=G1CH[:]), "qW", 16)

        hqm, hupm = {}, {}
        for iq in range(8):
            @blk.tensor
            def _(t, q=q, iq=iq, foldm=foldm, szm=szm):
                if iq == 0:
                    c.wait(t, "eV", evacm[7])
                    c.wait(t, "qS", szm)
                if iq >= 4:
                    c.wait(t, "eV", hupm[iq - 4])
                c.inc(t.matmul(PS[iq % 4][:], Btok[:], XPP[:, 512 * iq:512 * (iq + 1)],
                               start=True, stop=True), "eT")
            hqm[iq] = c.mark("eT")

            @blk.vector
            def _(v, q=q, iq=iq, eem=eem):
                c.wait(v, "eT", hqm[iq])
                c.wait(v, "eS", eem)
                for l in range(8):
                    h = 8 * iq + l
                    v.scalar_tensor_tensor(out=HTf[:, 64 * h:64 * (h + 1)], in0=HTf[:, 64 * h:64 * (h + 1)],
                                           scalar=dend[:, h:h + 1], in1=PS[iq % 4][:, 64 * l:64 * (l + 1)],
                                           op0=ALU.mult, op1=ALU.add)
                v.drain()
                v.tensor_copy(out=HTb[:, 512 * iq:512 * (iq + 1)], in_=HTf[:, 512 * iq:512 * (iq + 1)])
                c.inc(v.drain(), "eV")
            hupm[iq] = c.mark("eV")
        c.bar()

    @blk.gpsimd
    def _(g):
        c.inc(g.dma_start(out=o_hloc[:], in_=HTf[:]), "qW", 16)
        c.wait(g, "qW")
    c.bar()

    es.close()
    return nc, dbg


def build_l2(debug=0):
    STOP = 9
    nc = bass.Bass("TRN2", target_bir_lowering=False)
    es = ExitStack()

    def din(name, shape, dt=F32):
        return nc.dram_tensor(name, shape, dt, kind="ExternalInput").ap()

    def dout(name, shape, dt=F32):
        return nc.dram_tensor(name, shape, dt, kind="ExternalOutput").ap()

    def sb(name, p, f, dt):
        return es.enter_context(nc.sbuf_tensor(name, [p, f], dt))

    def ps(name, p, f, dt=F32):
        return es.enter_context(nc.psum_tensor(name, [p, f], dt))

    x1 = din("x1", [T, H])
    g1 = din("g1", [T, H], BF16)
    sz = din("sz", [T, H], BF16)
    Cbf = din("Cbf", [NS, T], BF16)
    cum = din("cum", [T, NH])
    ctcore = din("ctcore", [8, NH])
    hloc = din("hloc", [8 * NS, H])
    prefw = din("prefw", [NH, 8])
    norm_w = din("norm_w", [1, H])
    opw = din("opw", [H, H])
    ffn_wT = din("ffn_wT", [H, H])
    ffn_b = din("ffn_b", [1, H])
    lora_A = din("lora_A", [R48, H])
    lora_B = din("lora_B", [3 * H, 16])
    lsi = din("lsi", [1, 4], I32)

    o_out = dout("o_out", [T, H])
    x2_dram = nc.dram_tensor("x2_dram", [T, H], F32, kind="Internal").ap()

    dbg = {}
    if debug:
        dbg["gy"] = dout("d_gy", [Q, H])
        dbg["x2"] = dout("d_x2", [T, H])
        dbg["hin"] = dout("d_hin", [NS, H], BF16)
        dbg["ew"] = dout("d_ew", [NH, 8])
        dbg["wff"] = dout("d_wff", [128, 32768], BF16)
        dbg["lnf"] = dout("d_lnf", [128, 1024], BF16)
        dbg["rmk"] = dout("d_rmk", [64, 1024], BF16)
        dbg["btfb"] = dout("d_btfb", [128, 4096], BF16)

    LNF = sb("LNF", 128, 32768, BF16)
    WB = sb("WB", 128, 32768, BF16)
    AR2 = sb("AR2", 128, 8192, BF16)
    GY = sb("GY", 128, 4096, F32)
    GYN = sb("GYN", 128, 4096, BF16)
    G1B = sb("G1B", 128, 4096, BF16)
    HINb = AR2[:, 0:4096]
    NWREP = AR2[:, 4096:8192]
    Cfm = sb("Cfm", 128, T, BF16)
    cumL2 = sb("cumL2", 128, 512, F32)
    EC2 = sb("EC2", 128, 64, F32)
    ATf = sb("ATf", 128, 32 * R48, BF16)
    BTFB = sb("BTFB", 128, H, BF16)
    biasrow = BTFB[48:49, :]
    rmk2 = sb("rmk2", 64, 1024, BF16)
    xcb2 = sb("xcb2", 128, 1024, F32)
    x2b = sb("x2b", 128, 1024, F32)
    statS3 = sb("statS3", 128, 64, F32)
    statS4 = sb("statS4", 128, 64, F32)
    colM = sb("colM", 128, 1, F32)
    colR = sb("colR", 128, 1, F32)
    colN = sb("colN", 128, 1, F32)
    colTa = sb("colTa", 128, 1, F32)
    colTb = sb("colTb", 128, 1, F32)
    colS = sb("colS", 128, 1, F32)
    colQ = sb("colQ", 128, 1, F32)
    scr = sb("scr", 128, 512, F32)
    EWR = scr
    idb = sb("idb", 128, 128, BF16)
    idf = sb("idf", 128, 128, F32)
    onescf = sb("onescf", 1, 128, F32)
    halfcol = sb("halfcol", 128, 1, F32)
    cts = sb("cts", 8, 64, F32)
    ct64 = sb("ct64", 64, 8, F32)
    sfx = sb("sfx", 64, 8, F32)
    ew = sb("ew", 64, 8, F32)
    pw64 = sb("pw64", 64, 8, F32)
    ewrow = sb("ewrow", 1, 64, F32)
    lsis = sb("lsis", 1, 4, I32)
    lsif = sb("lsif", 1, 4, F32)
    cnt1 = sb("cnt1", 1, 1, F32)
    cst13 = sb("cst13", 1, 3, F32)
    msk13 = sb("msk13", 1, 3, F32)

    HLD = LNF[:, 0:8192].bitcast(F32)
    LAB = WB[:, 0:4096]
    LBB = WB[:, 4096:5632]
    btmp = WB[0:16, 5632:6656]
    OPW = [WB[:, 0:16384], WB[:, 16384:32768]]
    WFF = [WB[:, 0:16384], WB[:, 16384:32768]]
    OTB = [AR2[:, 0:4096], AR2[:, 4096:8192]]
    xst2 = GY
    lnxtok2 = GYN
    SZB = GYN

    p0 = ps("p0", 128, 512)
    p1 = ps("p1", 128, 512)
    p2 = ps("p2", 128, 512)
    p3 = ps("p3", 128, 512)
    p4 = ps("p4", 128, 512)
    p5 = ps("p5", 128, 512)
    p6 = ps("p6", 128, 512)
    pT = ps("pT", 128, 1024, BF16)
    PS = [p0, p1, p2, p3]

    blk = es.enter_context(nc.Block())
    c = Ctx(nc, blk)
    for s in ["qS", "qC", "qG", "qW", "qA", "qB", "qD", "eT", "eV", "eS", "eG"]:
        c.sems[s] = [es.enter_context(nc.semaphore(s)), 0]

    @blk.gpsimd
    def _(g):
        make_ident(g, idb[:])
        make_ident(g, idf[:])
        g.memset(cst13[0:1, 0:1], 1.0)
        g.memset(cst13[0:1, 1:2], 2.0)
        c.inc(g.memset(cst13[0:1, 2:3], 3.0), "eG")
        c.inc(g.dma_start(out=lsis[:], in_=lsi), "qG", 16)
        c.inc(g.dma_start(out=biasrow, in_=ffn_b), "qG", 16)
        c.inc(g.dma_start(out=LAB[0:R48, :], in_=lora_A), "qG", 16)
        c.inc(g.dma_start(out=LBB[:, 0:1536].rearrange("p (a r) -> p a r", r=16),
                          in_=lora_B.rearrange("(a p) r -> p a r", p=128)), "qG", 16)
        c.inc(g.dma_start(out=cts[:], in_=ctcore), "qG", 16)
        c.inc(g.dma_start(out=pw64[:], in_=prefw), "qG", 16)
        c.inc(g.dma_start(out=Cfm[:], in_=Cbf), "qG", 16)
        for qq in range(8):
            c.inc(g.dma_start(out=cumL2[:, 64 * qq:64 * (qq + 1)],
                              in_=cum[Q * qq:Q * (qq + 1), :]), "qG", 16)

    @blk.vector
    def _(v):
        v.memset(rmk2[32:64, :], 1.0)
        v.memset(statS3[:], 0.0)
        v.memset(statS4[:], 0.0)
        v.memset(onescf[:], 1.0)
        v.memset(halfcol[:], 0.5)
        c.wait(v, "eG")
        c.wait(v, "qG")
        v.tensor_copy(out=lsif[:], in_=lsis[:])
        v.drain()
        v.tensor_single_scalar(out=lsif[:], in_=lsif[:], scalar=0.0, op=ALU.is_le)
        v.drain()
        v.reduce_sum(out=cnt1[:], in_=lsif[:], axis=AXX)
        v.drain()
        v.tensor_scalar(out=cnt1[:], in0=cnt1[:], scalar1=-1.0, scalar2=None, op0=ALU.add)
        v.drain()
        c.inc(v.tensor_scalar(out=msk13[:], in0=cst13[:], scalar1=cnt1[:], scalar2=None,
                              op0=ALU.is_equal), "eV")
    c.bar()

    @blk.tensor
    def _(t):
        c.inc(t.transpose(p6[0:3, 102:103], msk13[:], idf[0:1, 0:1]), "eT")
        for i in range(16):
            c.inc(t.transpose(pT[:, 48 * i:48 * (i + 1)], LAB[0:R48, 128 * i:128 * (i + 1)],
                              idb[0:R48, 0:R48]), "eT")
        c.inc(t.transpose(p6[0:64, 110:118], cts[:], idf[0:8, 0:8]), "eT")
    c.bar()

    @blk.vector
    def _(v):
        c.inc(v.tensor_copy(out=ATf[:, 0:768], in_=pT[:, 0:768]), "eV")
        c.inc(v.tensor_copy(out=ct64[:], in_=p6[0:64, 110:118]), "eV")
    c.bar()

    @blk.tensor
    def _(t):
        for i in range(16):
            c.inc(t.transpose(pT[:, 48 * i:48 * (i + 1)], LAB[0:R48, 128 * (16 + i):128 * (17 + i)],
                              idb[0:R48, 0:R48]), "eT")
        c.inc(t.matmul(p6[:, 103:106], onescf[:], msk13[:], start=True, stop=True), "eT")
    c.bar()

    @blk.vector
    def _(v):
        c.inc(v.tensor_copy(out=ATf[:, 768:1536], in_=pT[:, 0:768]), "eV")
        for i in range(32):
            for a in range(3):
                c.inc(v.tensor_scalar(out=ATf[:, 48 * i + 16 * a:48 * i + 16 * (a + 1)],
                                      in0=ATf[:, 48 * i + 16 * a:48 * i + 16 * (a + 1)],
                                      scalar1=p6[:, 103 + a:104 + a], scalar2=None,
                                      op0=ALU.mult), "eV")
        v.tensor_tensor(out=ct64[:], in0=ct64[:], in1=pw64[:], op=ALU.mult)
        v.drain()
        v.memset(sfx[:, 7:8], 0.0)
        v.drain()
        for j in range(6, -1, -1):
            v.tensor_tensor(out=sfx[:, j:j + 1], in0=sfx[:, j + 1:j + 2],
                            in1=ct64[:, j + 1:j + 2], op=ALU.add)
            v.drain()
        c.inc(v.drain(), "eV")
    c.bar()

    @blk.scalar
    def _(s):
        s.activation(out=ew[:], in_=sfx[:], func=ACT.Exp)
        c.inc(s.drain(), "eS")

    @blk.vector
    def _(v):
        c.wait(v, "eS")
        c.inc(v.tensor_tensor(out=ew[:], in0=ew[:], in1=pw64[:], op=ALU.mult), "eV")
    c.bar()

    for a in range(3):
        for w in range(4):
            @blk.tensor
            def _(t, a=a, w=w):
                for j in range(8):
                    i = 8 * w + j
                    c.inc(t.transpose(pT[0:16, 128 * j:128 * (j + 1)],
                                      LBB[:, 16 * (32 * a + i):16 * (32 * a + i + 1)],
                                      idb[:]), "eT")
            c.bar()

            @blk.vector
            def _(v, a=a, w=w):
                c.inc(v.tensor_copy(out=btmp, in_=pT[0:16, 0:1024]), "eV")
            c.bar()

            @blk.gpsimd
            def _(g, a=a, w=w):
                c.inc(g.dma_start(out=BTFB[16 * a:16 * (a + 1), 1024 * w:1024 * (w + 1)],
                                  in_=btmp), "qG", 16)
            c.bar()
    if debug:
        @blk.gpsimd
        def _(g):
            c.inc(g.dma_start(out=dbg["ew"], in_=ew[:]), "qG", 16)

    for j in range(8):
        @blk.tensor
        def _(t, j=j):
            c.inc(t.transpose(p6[0:1, 120:184], ew[:, j:j + 1], idf[0:64, 0:64]), "eT")

        @blk.vector
        def _(v, j=j):
            c.wait(v, "eT")
            c.inc(v.tensor_copy(out=ewrow[:], in_=p6[0:1, 120:184]), "eV")

        @blk.tensor
        def _(t, j=j):
            c.wait(t, "eV")
            c.inc(t.matmul(p6[:, 184:248], onescf[:], ewrow[:], start=True, stop=True), "eT")

        @blk.vector
        def _(v, j=j):
            c.wait(v, "eT")
            c.inc(v.tensor_copy(out=EWR[:, 64 * j:64 * (j + 1)], in_=p6[:, 184:248]), "eV")
        c.bar()

    for jj in range(4):
        @blk.gpsimd
        def _(g, jj=jj):
            c.inc(g.dma_start(out=xcb2[0:1, :], in_=norm_w[0:1, 1024 * jj:1024 * (jj + 1)]), "qG", 16)
        c.bar()

        @blk.tensor
        def _(t, jj=jj):
            c.inc(t.matmul(p4[:], onescf[:], xcb2[0:1, 0:512], start=True, stop=True), "eT")
            c.inc(t.matmul(p5[:], onescf[:], xcb2[0:1, 512:1024], start=True, stop=True), "eT")
        c.bar()

        @blk.vector
        def _(v, jj=jj):
            v.tensor_copy(out=NWREP[:, 1024 * jj:1024 * jj + 512], in_=p4[:])
            c.inc(v.tensor_copy(out=NWREP[:, 1024 * jj + 512:1024 * (jj + 1)], in_=p5[:]), "eV")
        c.bar()

    for j in range(8):
        @blk.gpsimd
        def _(g, j=j):
            c.inc(g.dma_start(out=HLD[:], in_=hloc[NS * j:NS * (j + 1), :]), "qG", 16)
        c.bar()

        @blk.vector
        def _(v, j=j):
            for h in range(NH):
                if j == 0:
                    v.tensor_scalar(out=HINb[:, 64 * h:64 * (h + 1)], in0=HLD[:, 64 * h:64 * (h + 1)],
                                    scalar1=EWR[:, 64 * j + h:64 * j + h + 1], scalar2=None,
                                    op0=ALU.mult)
                else:
                    v.scalar_tensor_tensor(out=HINb[:, 64 * h:64 * (h + 1)],
                                           in0=HLD[:, 64 * h:64 * (h + 1)],
                                           scalar=EWR[:, 64 * j + h:64 * j + h + 1],
                                           in1=HINb[:, 64 * h:64 * (h + 1)],
                                           op0=ALU.mult, op1=ALU.add)
            c.inc(v.drain(), "eV")
        c.bar()
    if debug:
        @blk.gpsimd
        def _(g):
            c.inc(g.dma_start(out=dbg["hin"], in_=HINb[:]), "qG", 16)
        c.bar()

    if STOP < 2:
        es.close()
        return nc, dbg
    NQ2 = 8
    SUB = 9
    gytm = {}
    prev_ev7 = {}
    for q in range(NQ2):
        @blk.gpsimd
        def _(g, q=q):
            if q >= 1:
                c.wait(g, "qS", gytm[q - 1])
                c.wait(g, "eV", prev_ev7[q - 1])
            c.inc(g.dma_start(out=G1B[:], in_=g1[Q * q:Q * (q + 1), :]), "qD", 16)
            c.inc(g.dma_start(out=SZB[:], in_=sz[Q * q:Q * (q + 1), :]), "qD", 16)
        ldm = c.mark("qD")

        @blk.scalar
        def _(s, q=q):
            s.activation(out=EC2[:], in_=cumL2[:, 64 * q:64 * (q + 1)], func=ACT.Exp)
            c.inc(s.drain(), "eS")
        ecm = c.mark("eS")

        mmqm, evqm, corrm = {}, {}, {}
        for iq in range(8):
            @blk.tensor
            def _(t, q=q, iq=iq):
                if iq >= 4:
                    c.wait(t, "eS", corrm[iq - 4])
                t.matmul(PS[iq % 4][:], Cfm[:, Q * q:Q * (q + 1)],
                         HINb[:, 512 * iq:512 * (iq + 1)], start=True, stop=True)
                c.inc(t.matmul(p6[0:1, 448:449], onescf[0:1, 0:1], onescf[0:1, 0:1],
                               start=True, stop=True), "eT")
            mmqm[iq] = c.mark("eT")

            if SUB < 2:
                @blk.vector
                def _(v, q=q, iq=iq):
                    c.wait(v, "eT", mmqm[iq])
                    if SUB == 0:
                        v.tensor_copy(out=GY[:, 512 * iq:512 * (iq + 1)], in_=PS[iq % 4][:])
                    else:
                        v.memset(GY[:, 512 * iq:512 * (iq + 1)], 0.5)
                    c.inc(v.drain(), "eV")
                evqm[iq] = c.mark("eV")
                continue

            cs = scr if iq % 2 == 0 else xcb2[:, 0:512]

            @blk.scalar
            def _(s, q=q, iq=iq, cs=cs, ecm=ecm):
                c.wait(s, "eT", mmqm[iq])
                if iq >= 2:
                    c.wait(s, "eV", evqm[iq - 2])
                for l in range(8):
                    h = 8 * iq + l
                    s.activation(out=cs[:, 64 * l:64 * (l + 1)], in_=PS[iq % 4][:, 64 * l:64 * (l + 1)],
                                 func=ACT.Identity, scale=EC2[:, h:h + 1])
                c.inc(s.drain(), "eS")
            corrm[iq] = c.mark("eS")

            @blk.vector
            def _(v, q=q, iq=iq, cs=cs, ldm=ldm):
                c.wait(v, "eS", corrm[iq])
                c.wait(v, "qD", ldm)
                for l in range(8):
                    h = 8 * iq + l
                    v.tensor_tensor(out=x2b[:, 0:64], in0=cs[:, 64 * l:64 * (l + 1)],
                                    in1=SZB[:, 64 * h:64 * (h + 1)], op=ALU.mult)
                    v.tensor_tensor(out=GY[:, 64 * h:64 * (h + 1)], in0=x2b[:, 0:64],
                                    in1=G1B[:, 64 * h:64 * (h + 1)], op=ALU.add)
                c.inc(v.drain(), "eV")
            evqm[iq] = c.mark("eV")
        prev_ev7[q] = evqm[7]

        @blk.scalar
        def _(s, q=q):
            c.wait(s, "eV", evqm[7])
            for jj in range(8):
                s.activation(out=scr[:], in_=GY[:, 512 * jj:512 * (jj + 1)], func=ACT.Square,
                             accum_out=statS4[:, jj:jj + 1])
            c.inc(s.drain(), "eS")

        @blk.vector
        def _(v, q=q):
            c.wait(v, "eS")
            v.reduce_sum(out=colQ[:], in_=statS4[:, 0:8], axis=AXX)
            v.drain()
            v.tensor_scalar(out=colTa[:], in0=colQ[:], scalar1=1.0 / H, scalar2=EPS,
                            op0=ALU.mult, op1=ALU.add)
            v.drain()
            v.reciprocal(out=colTb[:], in_=colTa[:])
            c.inc(v.drain(), "eV")

        @blk.scalar
        def _(s, q=q):
            c.wait(s, "eV")
            s.activation(out=colR[:], in_=colTb[:], func=ACT.Sqrt)
            c.inc(s.drain(), "eS")

        @blk.vector
        def _(v, q=q):
            c.wait(v, "eS")
            c.wait(v, "qS", c.mark("qS"))
            v.tensor_scalar(out=GYN[:], in0=GY[:], scalar1=colR[:], scalar2=None, op0=ALU.mult)
            v.drain()
            v.tensor_tensor(out=GYN[:], in0=GYN[:], in1=NWREP[:], op=ALU.mult)
            c.inc(v.drain(), "eV")
        gynm = c.mark("eV")

        @blk.sync
        def _(sy, q=q, gynm=gynm):
            c.wait(sy, "eV", gynm)
            for i in range(32):
                c.inc(sy.dma_start(out=LNF[:, 1024 * i + 128 * q:1024 * i + 128 * (q + 1)],
                                   in_=GYN[:, 128 * i:128 * (i + 1)], transpose=True), "qS", 16)
        gytm[q] = c.mark("qS")
    c.bar()
    if debug:
        @blk.gpsimd
        def _(g):
            c.inc(g.dma_start(out=dbg["gy"], in_=GYN[:]), "qG", 16)
        c.bar()

    if STOP < 3:
        es.close()
        return nc, dbg
    opw_mark = {}
    opmm_mark = {}
    opev_mark = {}
    opsq_mark = {}
    opw2_mark = {}
    opcp_mark = {}
    for oc in range(8):
        qsem = "qA" if oc % 2 == 0 else "qB"

        @blk.gpsimd
        def _(g, oc=oc, qsem=qsem):
            if (oc - 2, 7) in opmm_mark:
                c.wait(g, "eT", opmm_mark[(oc - 2, 7)])
            for dt in range(NKT):
                c.inc(g.dma_start(out=OPW[oc % 2][:, 512 * dt:512 * (dt + 1)],
                                  in_=opw[128 * dt:128 * (dt + 1), 512 * oc:512 * (oc + 1)]),
                      qsem, 16)
            opw_mark[oc] = (qsem, c.mark(qsem))

        for q in range(8):
            @blk.scalar
            def _(s, oc=oc, q=q):
                prev = (oc, q - 2) if q >= 2 else (oc - 1, q + 6)
                if prev in opev_mark:
                    c.wait(s, "eV", opev_mark[prev])
                c.inc(s.dma_start(out=xcb2[:, 512 * (q % 2):512 * (q % 2 + 1)],
                                  in_=x1[Q * q:Q * (q + 1), 512 * oc:512 * (oc + 1)]), "qC", 16)
            xldm = c.mark("qC")

            @blk.tensor
            def _(t, oc=oc, q=q):
                if q == 0:
                    c.wait(t, opw_mark[oc][0], opw_mark[oc][1])
                bank = PS[q % 4]
                prevg = (oc, q - 4) if q >= 4 else (oc - 1, q + 4)
                if prevg in opcp_mark:
                    c.wait(t, "eS", opcp_mark[prevg])
                for i in range(NKT):
                    mi = t.matmul(bank[:], LNF[:, 1024 * i + 128 * q:1024 * i + 128 * (q + 1)],
                                  OPW[oc % 2][:, 512 * i:512 * (i + 1)],
                                  start=(i == 0), stop=(i == NKT - 1))
                c.inc(mi, "eT")
                opmm_mark[(oc, q)] = c.mark("eT")

            @blk.scalar
            def _(s, oc=oc, q=q):
                c.wait(s, "eT", opmm_mark[(oc, q)])
                prevslot = (oc, q - 2) if q >= 2 else (oc - 1, q + 6)
                if prevslot in opev_mark:
                    c.wait(s, "eV", opev_mark[prevslot])
                s.copy(out=GY[:, 512 * (q % 2):512 * (q % 2 + 1)], in_=PS[q % 4][:])
                c.inc(s.drain(), "eS")
                opcp_mark[(oc, q)] = c.mark("eS")

            @blk.vector
            def _(v, oc=oc, q=q, xldm=xldm):
                c.wait(v, "eS", opcp_mark[(oc, q)])
                c.wait(v, "qC", xldm)
                prevslot = (oc, q - 2) if q >= 2 else (oc - 1, q + 6)
                if prevslot in opsq_mark:
                    c.wait(v, "eS", opsq_mark[prevslot])
                if prevslot in opw2_mark:
                    c.wait(v, "qW", opw2_mark[prevslot])
                slot = x2b[:, 512 * (q % 2):512 * (q % 2 + 1)]
                v.tensor_tensor(out=slot, in0=GY[:, 512 * (q % 2):512 * (q % 2 + 1)],
                                in1=xcb2[:, 512 * (q % 2):512 * (q % 2 + 1)], op=ALU.add)
                v.drain()
                v.reduce_sum(out=statS3[:, 8 * q + oc:8 * q + oc + 1], in_=slot, axis=AXX)
                c.inc(v.drain(), "eV")
            opev_mark[(oc, q)] = c.mark("eV")

            @blk.scalar
            def _(s, oc=oc, q=q):
                c.wait(s, "eV", opev_mark[(oc, q)])
                s.activation(out=scr[:], in_=x2b[:, 512 * (q % 2):512 * (q % 2 + 1)],
                             func=ACT.Square, accum_out=statS4[:, 8 * q + oc:8 * q + oc + 1])
                c.inc(s.drain(), "eS")
                opsq_mark[(oc, q)] = c.mark("eS")

            @blk.sync
            def _(sy, oc=oc, q=q):
                c.wait(sy, "eV", opev_mark[(oc, q)])
                c.inc(sy.dma_start(out=x2_dram[Q * q:Q * (q + 1), 512 * oc:512 * (oc + 1)],
                                   in_=x2b[:, 512 * (q % 2):512 * (q % 2 + 1)]), "qW", 16)
                opw2_mark[(oc, q)] = c.mark("qW")
    c.bar()
    if debug:
        @blk.gpsimd
        def _(g):
            c.inc(g.dma_start(out=dbg["x2"], in_=x2_dram[:]), "qG", 16)
        c.bar()

    if STOP < 4:
        es.close()
        return nc, dbg

    def emit_colstats2():
        def vsec(vv):
            inv = 1.0 / H
            vv.tensor_scalar(out=colM[:], in0=colS[:], scalar1=inv, scalar2=None, op0=ALU.mult)
            vv.drain()
            vv.tensor_scalar(out=colTa[:], in0=colQ[:], scalar1=inv, scalar2=None, op0=ALU.mult)
            vv.drain()
            vv.tensor_tensor(out=colTb[:], in0=colM[:], in1=colM[:], op=ALU.mult)
            vv.drain()
            vv.tensor_tensor(out=colTa[:], in0=colTa[:], in1=colTb[:], op=ALU.subtract)
            vv.drain()
            vv.tensor_scalar(out=colTb[:], in0=colTa[:], scalar1=EPS, scalar2=None, op0=ALU.add)
            vv.drain()
            vv.reciprocal(out=colTb[:], in_=colTb[:])
            c.inc(vv.drain(), "eV")
        blk.vector(vsec)

        def ssec(ss):
            c.wait(ss, "eV")
            ss.activation(out=colR[:], in_=colTb[:], func=ACT.Sqrt)
            c.inc(ss.drain(), "eS")
        blk.scalar(ssec)

        def vsec3(vv):
            c.wait(vv, "eS")
            vv.tensor_tensor(out=colTa[:], in0=colM[:], in1=colR[:], op=ALU.mult)
            vv.drain()
            vv.tensor_scalar(out=colN[:], in0=colTa[:], scalar1=-1.0, scalar2=None, op0=ALU.mult)
            c.inc(vv.drain(), "eV")
        blk.vector(vsec3)

    for tt in range(8):
        @blk.scalar
        def _(s, tt=tt):
            if tt == 0:
                c.inc(s.dma_start(out=xst2[:], in_=x2_dram[Q * 0:Q * 1, :]), "qC", 16)

        @blk.vector
        def _(v, tt=tt):
            c.wait(v, "qC", c.mark("qC"))
            v.reduce_sum(out=colS[:], in_=statS3[:, 8 * tt:8 * tt + 8], axis=AXX)
            v.drain()
            v.reduce_sum(out=colQ[:], in_=statS4[:, 8 * tt:8 * tt + 8], axis=AXX)
            c.inc(v.drain(), "eV")

        emit_colstats2()

        @blk.vector
        def _(v, tt=tt):
            c.wait(v, "eV")
            c.wait(v, "eS")
            c.wait(v, "qC", c.mark("qC"))
            c.wait(v, "qS", c.mark("qS"))
            v.tensor_scalar(out=lnxtok2[:], in0=xst2[:], scalar1=colR[:], scalar2=colN[:],
                            op0=ALU.mult, op1=ALU.add)
            c.inc(v.drain(), "eV")
        nm = c.mark("eV")

        @blk.scalar
        def _(s, tt=tt, nm=nm):
            c.wait(s, "eV", nm)
            if tt < 7:
                c.inc(s.dma_start(out=xst2[:], in_=x2_dram[Q * (tt + 1):Q * (tt + 2), :]), "qC", 16)

        @blk.sync
        def _(sy, tt=tt, nm=nm):
            c.wait(sy, "eV", nm)
            for i in range(32):
                c.inc(sy.dma_start(out=LNF[:, 1024 * i + 128 * tt:1024 * i + 128 * (tt + 1)],
                                   in_=lnxtok2[:, 128 * i:128 * (i + 1)], transpose=True),
                      "qS", 16)
    c.bar()

    if STOP < 5:
        es.close()
        return nc, dbg

    @blk.tensor
    def _(t):
        for th in range(2):
            pp = p4 if th == 0 else p5
            for k in range(NKT):
                mi = t.matmul(pp[0:R48, :], ATf[:, R48 * k:R48 * (k + 1)],
                              LNF[:, 1024 * k + 512 * th:1024 * k + 512 * (th + 1)],
                              start=(k == 0), stop=(k == NKT - 1))
        c.inc(mi, "eT")

    @blk.scalar
    def _(s):
        c.wait(s, "eT")
        s.copy(out=rmk2[0:R48, 0:512], in_=p4[0:R48, :])
        s.copy(out=rmk2[0:R48, 512:1024], in_=p5[0:R48, :])
        c.inc(s.drain(), "eS")
    c.bar()

    fev_mark = {}
    fmm_mark = {}
    fout_mark = {}
    fcp_mark = {}
    fw_mark = {}
    for oc in range(8):
        qsem = "qA" if oc % 2 == 0 else "qB"

        @blk.gpsimd
        def _(g, oc=oc, qsem=qsem):
            if (oc - 2, 7) in fmm_mark:
                c.wait(g, "eT", fmm_mark[(oc - 2, 7)])
            for k in range(NKT):
                c.inc(g.dma_start(out=WFF[oc % 2][:, 512 * k:512 * (k + 1)],
                                  in_=ffn_wT[128 * k:128 * (k + 1), 512 * oc:512 * (oc + 1)]),
                      qsem, 16)
            fw_mark[oc] = (qsem, c.mark(qsem))

        for tt in range(8):
            @blk.scalar
            def _(s, oc=oc, tt=tt):
                prev = (oc, tt - 2) if tt >= 2 else (oc - 1, tt + 6)
                if prev in fev_mark:
                    c.wait(s, "eV", fev_mark[prev])
                c.inc(s.dma_start(out=xcb2[:, 512 * (tt % 2):512 * (tt % 2 + 1)],
                                  in_=x2_dram[Q * tt:Q * (tt + 1), 512 * oc:512 * (oc + 1)]),
                      "qC", 16)
            xldm = c.mark("qC")

            @blk.tensor
            def _(t, oc=oc, tt=tt):
                if tt == 0:
                    c.wait(t, fw_mark[oc][0], fw_mark[oc][1])
                bank = PS[tt % 4]
                prevg = (oc, tt - 4) if tt >= 4 else (oc - 1, tt + 4)
                if prevg in fcp_mark:
                    c.wait(t, "eS", fcp_mark[prevg])
                for k in range(NKT):
                    t.matmul(bank[:], LNF[:, 1024 * k + 128 * tt:1024 * k + 128 * (tt + 1)],
                             WFF[oc % 2][:, 512 * k:512 * (k + 1)],
                             start=(k == 0), stop=False)
                c.inc(t.matmul(bank[:], rmk2[0:49, 128 * tt:128 * (tt + 1)],
                               BTFB[0:49, 512 * oc:512 * (oc + 1)], start=False, stop=True), "eT")
                fmm_mark[(oc, tt)] = c.mark("eT")

            @blk.scalar
            def _(s, oc=oc, tt=tt):
                c.wait(s, "eT", fmm_mark[(oc, tt)])
                prevslot = (oc, tt - 2) if tt >= 2 else (oc - 1, tt + 6)
                if prevslot in fev_mark:
                    c.wait(s, "eV", fev_mark[prevslot])
                s.activation(out=GY[:, 512 * (tt % 2):512 * (tt % 2 + 1)], in_=PS[tt % 4][:],
                             func=ACT.Identity, scale=halfcol[:])
                c.inc(s.drain(), "eS")
                fcp_mark[(oc, tt)] = c.mark("eS")

            @blk.vector
            def _(v, oc=oc, tt=tt, xldm=xldm):
                c.wait(v, "eS", fcp_mark[(oc, tt)])
                c.wait(v, "qC", xldm)
                prevslot = (oc, tt - 2) if tt >= 2 else (oc - 1, tt + 6)
                if prevslot in fout_mark:
                    c.wait(v, "qW", fout_mark[prevslot])
                slot = x2b[:, 512 * (tt % 2):512 * (tt % 2 + 1)]
                v.tensor_tensor(out=slot, in0=GY[:, 512 * (tt % 2):512 * (tt % 2 + 1)],
                                in1=xcb2[:, 512 * (tt % 2):512 * (tt % 2 + 1)], op=ALU.add)
                c.inc(v.drain(), "eV")
            fev_mark[(oc, tt)] = c.mark("eV")

            @blk.sync
            def _(sy, oc=oc, tt=tt):
                c.wait(sy, "eV", fev_mark[(oc, tt)])
                c.inc(sy.dma_start(out=o_out[Q * tt:Q * (tt + 1), 512 * oc:512 * (oc + 1)],
                                   in_=x2b[:, 512 * (tt % 2):512 * (tt % 2 + 1)]), "qW", 16)
                fout_mark[(oc, tt)] = c.mark("qW")
    c.bar()

    if debug:
        @blk.gpsimd
        def _(g):
            c.inc(g.dma_start(out=dbg["wff"], in_=WB[:]), "qG", 16)
            c.inc(g.dma_start(out=dbg["btfb"], in_=BTFB[:]), "qG", 16)
            c.inc(g.dma_start(out=dbg["lnf"], in_=LNF[:, 0:1024]), "qG", 16)
            c.inc(g.dma_start(out=dbg["rmk"], in_=rmk2[:]), "qG", 16)
        c.bar()

    @blk.gpsimd
    def _(g):
        c.wait(g, "qG")
        c.wait(g, "qW")
    c.bar()

    es.close()
    return nc, dbg


BF = ml_dtypes.bfloat16
H = 4096; T = 1024; PROJ = 8512; CONV = 4352; NH = 64; NS = 128


def l1_inputs(inp, core):
    x = np.asarray(inp["L0_attn_hidden_states"][0])
    xpad = np.concatenate([np.zeros((3, H), np.float32), x], 0)
    lo = T * core
    m = {}
    m["xin"] = np.ascontiguousarray(xpad[lo:lo + T + 3])
    m["halom"] = np.full((128, 4), 0.0 if core == 0 else 1.0, np.float32)
    m["ffn_w"] = np.asarray(inp["L0_ffn_weight"])
    m["ffn_b"] = np.asarray(inp["L0_ffn_bias"]).reshape(1, H)
    m["lora_A"] = np.asarray(inp["L0_ffn_lora_A"]).reshape(48, H)
    m["lora_B"] = np.asarray(inp["L0_ffn_lora_B"]).reshape(3 * H, 16)
    m["lsi"] = np.asarray(inp["L0_ffn_lora_start_indices"]).reshape(1, 4).astype(np.int32)
    m["ipw"] = np.asarray(inp["L0_attn_in_proj_w"])
    m["conv_w"] = np.asarray(inp["L0_attn_conv_w"])
    m["a_log"] = np.asarray(inp["L0_attn_A_log"]).reshape(1, NH)
    m["dt_bias"] = np.asarray(inp["L0_attn_dt_bias"]).reshape(1, NH)
    m["d_in"] = np.asarray(inp["L0_attn_D"]).reshape(1, NH)
    mup = np.zeros((1, PROJ + 64), np.float32)
    mup[0, :PROJ] = np.asarray(inp["L0_attn_mup_vector"])
    m["mup_i"] = mup
    m["ln_w"] = np.asarray(inp["L0_attn_ln_w"]).reshape(32, 128)
    return m


def l2_inputs(inp, core, l1res):
    r = l1res[core]
    m = {}
    m["x1"] = np.asarray(r["o_x1"])
    m["g1"] = np.asarray(r["o_g1"])
    m["sz"] = np.asarray(r["o_sztok"])
    m["Cbf"] = np.asarray(r["o_C"])
    m["cum"] = np.asarray(r["o_cum"])
    m["ctcore"] = np.stack([np.asarray(l1res[j]["o_ctot"])[7] for j in range(8)])
    m["hloc"] = np.concatenate([np.asarray(l1res[j]["o_hloc"]) for j in range(8)], 0)
    m["prefw"] = np.tile((np.arange(8) < core).astype(np.float32), (64, 1))
    m["norm_w"] = np.asarray(inp["L0_attn_norm_w"]).reshape(1, H)
    m["opw"] = np.asarray(inp["L0_attn_out_proj_w"])
    m["ffn_wT"] = np.ascontiguousarray(np.asarray(inp["L0_ffn_weight"]).T)
    m["ffn_b"] = np.asarray(inp["L0_ffn_bias"]).reshape(1, H)
    m["lora_A"] = np.asarray(inp["L0_ffn_lora_A"]).reshape(48, H)
    m["lora_B"] = np.asarray(inp["L0_ffn_lora_B"]).reshape(3 * H, 16)
    m["lsi"] = np.asarray(inp["L0_ffn_lora_start_indices"]).reshape(1, 4).astype(np.int32)
    return m


_BUILT = {}


def kernel(**inputs):
    if "nc1" not in _BUILT:
        _BUILT["nc1"], _ = build_l1(debug=0, upto=9)
        _BUILT["nc2"], _ = build_l2(debug=0)
    nc1 = _BUILT["nc1"]
    nc2 = _BUILT["nc2"]
    in1 = [l1_inputs(inputs, c) for c in range(8)]
    res1 = run_bass_kernel_spmd(nc1, in1, core_ids=list(range(8)))
    l1res = [{k: np.asarray(v) for k, v in res1.results[c].items()} for c in range(8)]
    in2 = [l2_inputs(inputs, c, l1res) for c in range(8)]
    res2 = run_bass_kernel_spmd(nc2, in2, core_ids=list(range(8)))
    out = np.concatenate([np.asarray(res2.results[c]["o_out"], np.float32) for c in range(8)], 0)
    return out.reshape(1, 8192, 4096)


# version 04
# speedup vs baseline: 1.1308x; 1.1308x over previous
import numpy as np
import ml_dtypes
import concourse.bass as bass
import concourse.mybir as mybir
from contextlib import ExitStack
from concourse.bass_utils import run_bass_kernel_spmd

F32 = mybir.dt.float32
BF16 = mybir.dt.bfloat16
I32 = mybir.dt.int32
ALU = mybir.AluOpType
ACT = mybir.ActivationFunctionType
AXX = mybir.AxisListType.X

H = 4096
T = 1024
LW = 1056
PROJ = 8512
CONV = 4352
NS = 128
NH = 64
Q = 128
NCH = 8
NKT = 32
EPS = 1e-5
NEG = -1e30
R48 = 48


class Ctx:
    def __init__(self, nc, blk):
        self.nc = nc
        self.blk = blk
        self.sems = {}

    def inc(self, instr, name, k=1):
        s = self.sems[name]
        instr.then_inc(s[0], k)
        s[1] += k
        return instr

    def mark(self, name):
        return self.sems[name][1]

    def wait(self, eng, name, val=None):
        s = self.sems[name]
        v = s[1] if val is None else val
        if v > 0:
            eng.wait_ge(s[0], v)

    def bar(self):
        names = list(self.sems)
        for sec in (self.blk.tensor, self.blk.vector, self.blk.scalar,
                    self.blk.gpsimd, self.blk.sync):
            def f(eng):
                for n in names:
                    self.wait(eng, n)
            sec(f)


def make_ident(g, ap, n=128):
    g.memset(ap, 0.0)
    g.affine_select(out=ap, in_=ap, compare_op=ALU.not_equal, fill=1.0,
                    base=0, pattern=[[-1, n]], channel_multiplier=1)


def make_maskbias(g, ap, n=128):
    g.memset(ap, 0.0)
    g.affine_select(out=ap, in_=ap, compare_op=ALU.is_ge, fill=NEG,
                    base=0, pattern=[[1, n]], channel_multiplier=-1)


def stats_cols(v, s, mcol, rcol, nmr, sumc, sqc, n_feat, rms_fused, ta, tb):
    inv = 1.0 / n_feat
    v.tensor_scalar(out=mcol, in0=sumc, scalar1=inv, scalar2=None, op0=ALU.mult)
    v.tensor_scalar(out=ta, in0=sqc, scalar1=inv, scalar2=None, op0=ALU.mult)
    v.tensor_tensor(out=tb, in0=mcol, in1=mcol, op=ALU.mult)
    v.tensor_tensor(out=ta, in0=ta, in1=tb, op=ALU.subtract)
    v.tensor_scalar(out=tb, in0=ta, scalar1=EPS, scalar2=None, op0=ALU.add)
    v.reciprocal(out=tb, in_=tb)
    s.activation(out=rcol, in_=tb, func=ACT.Sqrt)
    if rms_fused:
        v.tensor_tensor(out=ta, in0=ta, in1=tb, op=ALU.mult)
        v.tensor_scalar(out=ta, in0=ta, scalar1=EPS, scalar2=None, op0=ALU.add)
        v.reciprocal(out=ta, in_=ta)
        s.activation(out=ta, in_=ta, func=ACT.Sqrt)
        v.tensor_tensor(out=rcol, in0=rcol, in1=ta, op=ALU.mult)
    v.tensor_tensor(out=ta, in0=mcol, in1=rcol, op=ALU.mult)
    v.tensor_scalar(out=nmr, in0=ta, scalar1=-1.0, scalar2=None, op0=ALU.mult)


def build_l1(debug=0, upto=9):
    nc = bass.Bass("TRN2", target_bir_lowering=False)
    es = ExitStack()

    def din(name, shape, dt=F32):
        return nc.dram_tensor(name, shape, dt, kind="ExternalInput").ap()

    def dout(name, shape, dt=F32):
        return nc.dram_tensor(name, shape, dt, kind="ExternalOutput").ap()

    def sb(name, p, f, dt):
        return es.enter_context(nc.sbuf_tensor(name, [p, f], dt))

    def ps(name, p, f, dt=F32):
        return es.enter_context(nc.psum_tensor(name, [p, f], dt))

    xin = din("xin", [T + 3, H])
    halom = din("halom", [128, 4])
    ffn_wT = din("ffn_wT", [H, H])
    ffn_b = din("ffn_b", [1, H])
    lora_A = din("lora_A", [R48, H])
    lora_B = din("lora_B", [3 * H, 16])
    lsi = din("lsi", [1, 4], I32)
    ipw = din("ipw", [H, PROJ])
    conv_w = din("conv_w", [CONV, 4])
    a_log = din("a_log", [1, NH])
    dt_bias = din("dt_bias", [1, NH])
    d_in = din("d_in", [1, NH])
    mup_i = din("mup_i", [1, PROJ + 64])
    ln_w = din("ln_w", [32, 128])

    o_x1 = dout("o_x1", [T, H])
    o_siluz = dout("o_siluz", [H, T], BF16)
    o_g1 = dout("o_g1", [T, H], BF16)
    o_sztok = dout("o_sztok", [T, H], BF16)
    o_C = dout("o_C", [NS, T], BF16)
    o_cum = dout("o_cum", [T, NH])
    o_ctot = dout("o_ctot", [NCH, NH])
    o_hloc = dout("o_hloc", [NS, H])
    xs_dram = nc.dram_tensor("xs_dram", [H, T], BF16, kind="Internal").ap()
    flat_dram = nc.dram_tensor("flat_dram", [NH, 8, Q], F32, kind="Internal").ap()
    x1h_dram = nc.dram_tensor("x1h_dram", [3, H], F32, kind="Internal").ap()
    dbg = {}
    if debug:
        dbg["lnx"] = dout("d_lnx", [32, 128, LW], BF16)
        dbg["xst"] = dout("d_xst", [128, 4096])
        dbg["lnxtok"] = dout("d_lnxtok", [128, 4096], BF16)
        dbg["stats"] = dout("d_stats", [128, 8])
        dbg["s2"] = dout("d_s2", [128, 64])
        dbg["scr4"] = dout("d_scr4", [128, 4])
        dbg["xs"] = dout("d_xs", [H, T], BF16)
        dbg["B"] = dout("d_B", [NS, T], BF16)
        dbg["dt"] = dout("d_dt", [NH, T])
        dbg["x1h"] = dout("d_x1h", [3, H])
        dbg["zz"] = dout("d_zz", [128, T], BF16)
        dbg["wff"] = dout("d_wff", [128, 16384], BF16)
        dbg["rmk"] = dout("d_rmk", [64, LW], BF16)

    LNX = sb("LNX", 128, 32 * LW, BF16)
    A2 = sb("A2", 128, 40960, BF16)
    lnxtok = sb("lnxtok", 128, H, BF16)
    Bfm = sb("Bfm", 128, T, BF16)
    Cfm = sb("Cfm", 128, T, BF16)
    dtf = sb("dtf", 64, T, F32)
    cumb = sb("cumb", 64, Q, F32)
    sch = sb("sch", 64, Q, F32)
    ones64 = sb("ones64", 64, Q, F32)
    cumT = sb("cumT", 128, 64, F32)
    ncumT = sb("ncumT", 128, 64, F32)
    dtT = sb("dtT", 128, 64, F32)
    dend = sb("dend", 128, 64, F32)
    EEs = sb("EEs", 128, 64, F32)
    de64 = sb("de64", 64, Q, F32)
    idb = sb("idb", 128, 128, BF16)
    idf = sb("idf", 128, 128, F32)
    mkb4 = sb("mkb4", 128, 512, BF16)
    onescf = sb("onescf", 1, 128, F32)
    ones128b = sb("ones128b", 128, 1, BF16)
    halfcol = sb("halfcol", 128, 1, F32)
    mupc = sb("mupc", 128, 67, F32)
    lnwc = sb("lnwc", 128, 32, F32)
    dtbias_c = sb("dtbias_c", 64, 1, F32)
    nA_c = sb("nA_c", 64, 1, F32)
    Dcol = sb("Dcol", 128, 32, F32)
    ATf = sb("ATf", 128, 32 * R48, BF16)
    BTFB = sb("BTFB", 128, H, BF16)
    BTf = BTFB[0:R48, :]
    biasrow = BTFB[48:49, :]
    rmk = sb("rmk", 64, LW, BF16)

    statS1 = sb("statS1", 128, 64, F32)
    statS2 = sb("statS2", 128, 64, F32)
    statS3 = sb("statS3", 128, 64, F32)
    statS4 = sb("statS4", 128, 64, F32)
    hs3 = sb("hs3", 3, 8, F32)
    hs4 = sb("hs4", 3, 8, F32)
    hs1 = sb("hs1", 3, 8, F32)
    hs2 = sb("hs2", 3, 8, F32)
    colM = sb("colM", 128, 1, F32)
    colR = sb("colR", 128, 1, F32)
    colN = sb("colN", 128, 1, F32)
    colTa = sb("colTa", 128, 1, F32)
    colTb = sb("colTb", 128, 1, F32)
    scr = sb("scr", 128, 512, F32)
    x1b = sb("x1b", 128, 1024, F32)
    hcolM = sb("hcolM", 3, 1, F32)
    hcolR = sb("hcolR", 3, 1, F32)
    hcolN = sb("hcolN", 3, 1, F32)
    hcolTa = sb("hcolTa", 3, 1, F32)
    hcolTb = sb("hcolTb", 3, 1, F32)
    hscr = sb("hscr", 3, 512, F32)
    halom_s = sb("halom_s", 128, 4, F32)
    xbctmp = sb("xbctmp", 128, 1056, BF16)
    convw = sb("convw", 128, 136, F32)
    xstmp = sb("xstmp", 128, T, BF16)
    lsis = sb("lsis", 1, 4, I32)
    lsif = sb("lsif", 1, 4, F32)
    cnt1 = sb("cnt1", 1, 1, F32)
    cst13 = sb("cst13", 1, 3, F32)
    msk13 = sb("msk13", 1, 3, F32)
    prm = sb("prm", 128, 128, F32)
    prm2 = sb("prm2", 128, 128, F32)
    prm3 = sb("prm3", 1, 192, F32)
    ctotb = sb("ctotb", 64, 1, F32)
    colS = sb("colS", 128, 1, F32)
    colQ = sb("colQ", 128, 1, F32)
    onescb = sb("onescb", 1, 128, BF16)
    xcb = sb("xcb", 128, 1024, F32)
    xbc2tmp = sb("xbc2tmp", 128, 1056, BF16)
    h512 = sb("h512", 3, 512, F32)
    ecrow = sb("ecrow", 1, 64, F32)
    ECtok = sb("ECtok", 128, 64, F32)
    Dtok = sb("Dtok", 128, 64, F32)
    Btok = sb("Btok", 128, 128, BF16)
    ocumb = sb("ocumb", 128, 64, F32)

    WFF = [A2[:, 0:16384], A2[:, 16384:32768]]
    OTB = [A2[:, 32768:36864], A2[:, 36864:40960]]
    WP = [A2[:, 0:8192], A2[:, 8192:16384]]
    DIAG = A2[:, 16384:33792]
    LAB = A2[:, 0:4096]
    LBB = A2[:, 4096:5632]
    XCH = [A2[:, 0:4096], A2[:, 4096:8192]]
    XTOK = A2[:, 8192:12288]
    XPP = A2[:, 12288:16384]
    G1CH = A2[:, 16384:20480]
    HTf = A2[:, 20480:28672].bitcast(F32)
    HTb = A2[:, 28672:32768]
    HTs = A2[:, 32768:36864]
    xst = A2[:, 0:8192].bitcast(F32)
    hlnb1 = A2[:, 8192:12288][0:3, :]
    x1hv = A2[:, 8192:16384].bitcast(F32)[0:3, :]
    hlnb3 = A2[:, 34304:38400][0:3, :]
    hzx = A2[:, 33792:38144][0:3, :]
    xhb1 = A2[:, 12288:20480].bitcast(F32)[0:3, :]
    x1hbuf = A2[:, 8192:16384].bitcast(F32)[0:3, :]
    dtraw = A2[:, 38144:40192].bitcast(F32)[0:64, 0:1024]
    G4 = A2[:, 36864:37376]
    C4 = A2[:, 37376:37888]
    E4 = A2[:, 37888:38400]
    W4 = A2[:, 38400:38912]
    Cw4 = A2[:, 38912:39424]
    btmp = A2[:, 5632:6656][0:16, :]
    dtabs = x1b[0:64, :]
    cumflat = LNX[:, 0:16384].bitcast(F32)[0:1, :]
    ecumflat = LNX[:, 16384:32768].bitcast(F32)[0:1, :]
    cumall = LNX[:, 16384:20480].bitcast(F32)[0:64, 0:1024]

    p0 = ps("p0", 128, 512)
    p1 = ps("p1", 128, 512)
    p2 = ps("p2", 128, 512)
    p3 = ps("p3", 128, 512)
    p4 = ps("p4", 128, 512)
    p5 = ps("p5", 128, 512)
    p6 = ps("p6", 128, 512)
    pT = ps("pT", 128, 1024, BF16)
    PS = [p0, p1, p2, p3]

    blk = es.enter_context(nc.Block())
    c = Ctx(nc, blk)
    for s in ["qS", "qC", "qG", "qW", "qA", "qB", "eT", "eV", "eS", "eG"]:
        c.sems[s] = [es.enter_context(nc.semaphore(s)), 0]

    @blk.gpsimd
    def _(g):
        make_ident(g, idb[:])
        make_ident(g, idf[:])
        make_maskbias(g, mkb4[:, 0:128])
        g.memset(cst13[0:1, 0:1], 1.0)
        g.memset(cst13[0:1, 1:2], 2.0)
        c.inc(g.memset(cst13[0:1, 2:3], 3.0), "eG")
        c.inc(g.memset(ones64[:], 1.0), "eG")
        c.inc(g.dma_start(out=halom_s[:], in_=halom), "qG", 16)
        c.inc(g.dma_start(out=lsis[:], in_=lsi), "qG", 16)
        c.inc(g.dma_start(out=biasrow, in_=ffn_b), "qG", 16)
        c.inc(g.dma_start(out=LAB[0:R48, :], in_=lora_A), "qG", 16)
        c.inc(g.dma_start(out=LBB[:, 0:1536].rearrange("p (a r) -> p a r", r=16),
                          in_=lora_B.rearrange("(a p) r -> p a r", p=128)), "qG", 16)
        c.inc(g.dma_start(out=prm[0:67, :], in_=mup_i.rearrange("o (a b) -> (o a) b", b=128)), "qG", 16)
        c.inc(g.dma_start(out=prm2[0:32, :], in_=ln_w), "qG", 16)
        c.inc(g.dma_start(out=prm3[0:1, 0:64], in_=dt_bias), "qG", 16)
        c.inc(g.dma_start(out=prm3[0:1, 64:128], in_=a_log), "qG", 16)
        c.inc(g.dma_start(out=prm3[0:1, 128:192], in_=d_in), "qG", 16)
        c.inc(g.dma_start(out=convw[:].rearrange("p (a k) -> p a k", k=4),
                          in_=conv_w.rearrange("(a p) k -> p a k", p=128)), "qG", 16)

    @blk.vector
    def _(v):
        v.memset(onescb[:], 1.0)
        v.memset(rmk[32:64, :], 1.0)
        v.memset(statS2[:], 0.0)
        v.memset(statS4[:], 0.0)
        v.memset(hs2[:], 0.0)
        v.memset(hs4[:], 0.0)
        v.memset(onescf[:], 1.0)
        v.memset(ones128b[:], 1.0)
        v.memset(halfcol[:], 0.5)
        c.wait(v, "eG")
        v.tensor_copy(out=mkb4[:, 128:256], in_=mkb4[:, 0:128])
        v.tensor_copy(out=mkb4[:, 256:384], in_=mkb4[:, 0:128])
        v.tensor_copy(out=mkb4[:, 384:512], in_=mkb4[:, 0:128])
        c.wait(v, "qG")
        v.tensor_copy(out=lsif[:], in_=lsis[:])
        v.tensor_single_scalar(out=lsif[:], in_=lsif[:], scalar=0.0, op=ALU.is_le)
        v.reduce_sum(out=cnt1[:], in_=lsif[:], axis=AXX)
        v.tensor_scalar(out=cnt1[:], in0=cnt1[:], scalar1=-1.0, scalar2=None, op0=ALU.add)
        c.inc(v.tensor_scalar(out=msk13[:], in0=cst13[:], scalar1=cnt1[:], scalar2=None,
                              op0=ALU.is_equal), "eV")
    c.bar()

    @blk.tensor
    def _(t):
        c.inc(t.transpose(p6[:, 0:67], prm[0:67, :], idf[0:67, 0:67]), "eT")
        c.inc(t.transpose(p6[:, 67:99], prm2[0:32, :], idf[0:32, 0:32]), "eT")
        c.inc(t.transpose(p6[0:64, 99:100], prm3[0:1, 0:64], idf[0:1, 0:1]), "eT")
        c.inc(t.transpose(p6[0:64, 100:101], prm3[0:1, 64:128], idf[0:1, 0:1]), "eT")
        c.inc(t.transpose(p6[0:3, 102:103], msk13[:], idf[0:1, 0:1]), "eT")
        for i in range(16):
            c.inc(t.transpose(pT[:, 48 * i:48 * (i + 1)], LAB[0:R48, 128 * i:128 * (i + 1)],
                              idb[0:R48, 0:R48]), "eT")
    c.bar()

    @blk.vector
    def _(v):
        v.tensor_copy(out=mupc[:], in_=p6[:, 0:67])
        v.tensor_copy(out=lnwc[:], in_=p6[:, 67:99])
        v.tensor_copy(out=dtbias_c[:], in_=p6[0:64, 99:100])
        c.inc(v.tensor_copy(out=ATf[:, 0:768], in_=pT[:, 0:768]), "eV")

    @blk.scalar
    def _(s):
        s.activation(out=nA_c[:], in_=p6[0:64, 100:101], func=ACT.Exp)
        c.inc(s.activation(out=nA_c[:], in_=nA_c[:], func=ACT.Copy, scale=-1.0), "eS")
    c.bar()

    @blk.tensor
    def _(t):
        for i in range(16):
            c.inc(t.transpose(pT[:, 48 * i:48 * (i + 1)], LAB[0:R48, 128 * (16 + i):128 * (17 + i)],
                              idb[0:R48, 0:R48]), "eT")
        c.inc(t.matmul(p6[:, 103:106], onescf[:], msk13[:], start=True, stop=True), "eT")
        c.inc(t.matmul(p6[:, 106:170], onescf[:], prm3[0:1, 128:192], start=True, stop=True), "eT")
    c.bar()

    @blk.vector
    def _(v):
        c.inc(v.tensor_copy(out=ATf[:, 768:1536], in_=pT[:, 0:768]), "eV")
        for i in range(32):
            for a in range(3):
                c.inc(v.tensor_scalar(out=ATf[:, 48 * i + 16 * a:48 * i + 16 * (a + 1)],
                                      in0=ATf[:, 48 * i + 16 * a:48 * i + 16 * (a + 1)],
                                      scalar1=p6[:, 103 + a:104 + a], scalar2=None,
                                      op0=ALU.mult), "eV")
        dr = p6[:, 106:170]
        c.inc(v.tensor_copy(out=Dcol[0:64, :], in_=dr[0:64, :].rearrange("p (i two) -> p i two", two=2)[:, :, 0:1].squeeze(2)), "eV")
        c.inc(v.tensor_copy(out=Dcol[64:128, :], in_=dr[64:128, :].rearrange("p (i two) -> p i two", two=2)[:, :, 1:2].squeeze(2)), "eV")
        for c34 in range(34):
            c.inc(v.tensor_scalar(out=convw[:, 4 * c34:4 * (c34 + 1)],
                                  in0=convw[:, 4 * c34:4 * (c34 + 1)],
                                  scalar1=mupc[:, 32 + c34:33 + c34], scalar2=None,
                                  op0=ALU.mult), "eV")
    c.bar()

    for a in range(3):
        for w in range(4):
            @blk.tensor
            def _(t, a=a, w=w):
                for j in range(8):
                    i = 8 * w + j
                    c.inc(t.transpose(pT[0:16, 128 * j:128 * (j + 1)],
                                      LBB[:, 16 * (32 * a + i):16 * (32 * a + i + 1)],
                                      idb[:]), "eT")
            c.bar()

            @blk.vector
            def _(v, a=a, w=w):
                c.inc(v.tensor_copy(out=btmp, in_=pT[0:16, 0:1024]), "eV")
            c.bar()

            @blk.gpsimd
            def _(g, a=a, w=w):
                c.inc(g.dma_start(out=BTFB[16 * a:16 * (a + 1), 1024 * w:1024 * (w + 1)],
                                  in_=btmp), "qG", 16)
            c.bar()

    if upto < 1:
        es.close()
        return nc, dbg

    def emit_colstats(sumc, sqc, rms_fused=False):
        def vsec(vv):
            inv = 1.0 / H
            vv.tensor_scalar(out=colM[:], in0=sumc, scalar1=inv, scalar2=None, op0=ALU.mult)
            vv.drain()
            vv.tensor_scalar(out=colTa[:], in0=sqc, scalar1=inv, scalar2=None, op0=ALU.mult)
            vv.drain()
            vv.tensor_tensor(out=colTb[:], in0=colM[:], in1=colM[:], op=ALU.mult)
            vv.drain()
            vv.tensor_tensor(out=colTa[:], in0=colTa[:], in1=colTb[:], op=ALU.subtract)
            vv.drain()
            vv.tensor_scalar(out=colTb[:], in0=colTa[:], scalar1=EPS, scalar2=None, op0=ALU.add)
            vv.drain()
            vv.reciprocal(out=colTb[:], in_=colTb[:])
            c.inc(vv.drain(), "eV")
        blk.vector(vsec)

        def ssec(ss):
            c.wait(ss, "eV")
            ss.activation(out=colR[:], in_=colTb[:], func=ACT.Sqrt)
            c.inc(ss.drain(), "eS")
        blk.scalar(ssec)

        if rms_fused:
            def vsec2(vv):
                c.wait(vv, "eS")
                vv.tensor_tensor(out=colTa[:], in0=colTa[:], in1=colTb[:], op=ALU.mult)
                vv.drain()
                vv.tensor_scalar(out=colTa[:], in0=colTa[:], scalar1=EPS, scalar2=None, op0=ALU.add)
                vv.drain()
                vv.reciprocal(out=colTa[:], in_=colTa[:])
                c.inc(vv.drain(), "eV")
            blk.vector(vsec2)

            def ssec2(ss):
                c.wait(ss, "eV")
                ss.activation(out=colTa[:], in_=colTa[:], func=ACT.Sqrt)
                c.inc(ss.drain(), "eS")
            blk.scalar(ssec2)

            def vsec2b(vv):
                c.wait(vv, "eS")
                vv.tensor_tensor(out=colR[:], in0=colR[:], in1=colTa[:], op=ALU.mult)
                c.inc(vv.drain(), "eV")
            blk.vector(vsec2b)

        def vsec3(vv):
            c.wait(vv, "eS")
            c.wait(vv, "eV")
            vv.tensor_tensor(out=colTa[:], in0=colM[:], in1=colR[:], op=ALU.mult)
            vv.drain()
            vv.tensor_scalar(out=colN[:], in0=colTa[:], scalar1=-1.0, scalar2=None, op0=ALU.mult)
            c.inc(vv.drain(), "eV")
        blk.vector(vsec3)

    def emit_ln_pass(src_dma, rms_fused, use_partials):
        for tt in range(8):
            @blk.scalar
            def _(s, tt=tt):
                if tt == 0:
                    c.inc(s.dma_start(out=xst, in_=src_dma(0)), "qC", 16)
                if not use_partials:
                    c.wait(s, "qC", c.mark("qC"))
                    for j in range(8):
                        s.activation(out=scr[:], in_=xst[:, 512 * j:512 * (j + 1)], func=ACT.Square,
                                     accum_out=statS2[:, 8 * tt + j:8 * tt + j + 1])
                    c.inc(s.drain(), "eS")

            @blk.vector
            def _(v, tt=tt):
                c.wait(v, "qC", c.mark("qC"))
                if use_partials:
                    v.reduce_sum(out=colS[:], in_=statS3[:, 8 * tt:8 * tt + 8], axis=AXX)
                    v.drain()
                    v.reduce_sum(out=colQ[:], in_=statS4[:, 8 * tt:8 * tt + 8], axis=AXX)
                    c.inc(v.drain(), "eV")
                else:
                    v.reduce_sum(out=colS[:], in_=xst, axis=AXX)
                    v.drain()
                    c.wait(v, "eS")
                    v.reduce_sum(out=colQ[:], in_=statS2[:, 8 * tt:8 * tt + 8], axis=AXX)
                    c.inc(v.drain(), "eV")

            emit_colstats(colS[:], colQ[:], rms_fused)

            @blk.vector
            def _(v, tt=tt):
                c.wait(v, "eV")
                c.wait(v, "eS")
                c.wait(v, "qC", c.mark("qC"))
                c.wait(v, "qS", c.mark("qS"))
                v.tensor_scalar(out=lnxtok[:], in0=xst, scalar1=colR[:], scalar2=colN[:],
                                op0=ALU.mult, op1=ALU.add)
                c.inc(v.drain(), "eV")
            nm = c.mark("eV")

            @blk.scalar
            def _(s, tt=tt, nm=nm):
                c.wait(s, "eV", nm)
                if tt < 7:
                    c.inc(s.dma_start(out=xst, in_=src_dma(tt + 1)), "qC", 16)

            @blk.sync
            def _(sy, tt=tt, nm=nm):
                c.wait(sy, "eV", nm)
                for i in range(32):
                    c.inc(sy.dma_start(out=LNX[:, LW * i + 128 * tt:LW * i + 128 * (tt + 1)],
                                       in_=lnxtok[:, 128 * i:128 * (i + 1)], transpose=True),
                          "qS", 16)

    def emit_halo_ln(xsrc_ap, dst_bf, rms_fused, sq_buf):
        @blk.scalar
        def _(s):
            for j in range(8):
                s.activation(out=hscr[:], in_=xsrc_ap[:, 512 * j:512 * (j + 1)], func=ACT.Square,
                             accum_out=sq_buf[:, j:j + 1])
            c.inc(s.drain(), "eS")

        @blk.vector
        def _(v):
            c.wait(v, "eS")
            v.reduce_sum(out=hcolM[:], in_=xsrc_ap, axis=AXX)
            v.drain()
            v.reduce_sum(out=hcolTb[:], in_=sq_buf[:], axis=AXX)
            v.drain()
            inv = 1.0 / H
            v.tensor_scalar(out=hcolM[:], in0=hcolM[:], scalar1=inv, scalar2=None, op0=ALU.mult)
            v.drain()
            v.tensor_scalar(out=hcolTa[:], in0=hcolTb[:], scalar1=inv, scalar2=None, op0=ALU.mult)
            v.drain()
            v.tensor_tensor(out=hcolTb[:], in0=hcolM[:], in1=hcolM[:], op=ALU.mult)
            v.drain()
            v.tensor_tensor(out=hcolTa[:], in0=hcolTa[:], in1=hcolTb[:], op=ALU.subtract)
            v.drain()
            v.tensor_scalar(out=hcolTb[:], in0=hcolTa[:], scalar1=EPS, scalar2=None, op0=ALU.add)
            v.drain()
            v.reciprocal(out=hcolTb[:], in_=hcolTb[:])
            c.inc(v.drain(), "eV")

        @blk.scalar
        def _(s):
            c.wait(s, "eV")
            s.activation(out=hcolR[:], in_=hcolTb[:], func=ACT.Sqrt)
            c.inc(s.drain(), "eS")

        if rms_fused:
            @blk.vector
            def _(v):
                c.wait(v, "eS")
                v.tensor_tensor(out=hcolTa[:], in0=hcolTa[:], in1=hcolTb[:], op=ALU.mult)
                v.drain()
                v.tensor_scalar(out=hcolTa[:], in0=hcolTa[:], scalar1=EPS, scalar2=None, op0=ALU.add)
                v.drain()
                v.reciprocal(out=hcolTa[:], in_=hcolTa[:])
                c.inc(v.drain(), "eV")

            @blk.scalar
            def _(s):
                c.wait(s, "eV")
                s.activation(out=hcolTa[:], in_=hcolTa[:], func=ACT.Sqrt)
                c.inc(s.drain(), "eS")

            @blk.vector
            def _(v):
                c.wait(v, "eS")
                v.tensor_tensor(out=hcolR[:], in0=hcolR[:], in1=hcolTa[:], op=ALU.mult)
                c.inc(v.drain(), "eV")

        @blk.vector
        def _(v):
            c.wait(v, "eS")
            c.wait(v, "eV")
            v.tensor_tensor(out=hcolTa[:], in0=hcolM[:], in1=hcolR[:], op=ALU.mult)
            v.drain()
            v.tensor_scalar(out=hcolN[:], in0=hcolTa[:], scalar1=-1.0, scalar2=None, op0=ALU.mult)
            v.drain()
            v.tensor_scalar(out=dst_bf, in0=xsrc_ap, scalar1=hcolR[:], scalar2=hcolN[:],
                            op0=ALU.mult, op1=ALU.add)
            c.inc(v.drain(), "eV")
        c.bar()

        for w in range(4):
            @blk.tensor
            def _(t, w=w):
                for j in range(8):
                    i = 8 * w + j
                    c.inc(t.transpose(pT[:, 128 * j:128 * j + 3],
                                      dst_bf[:, 128 * i:128 * (i + 1)], idb[0:3, 0:3]), "eT")
            c.bar()

            @blk.vector
            def _(v, w=w):
                for j in range(8):
                    i = 8 * w + j
                    c.inc(v.tensor_copy(out=LNX[:, LW * i + 1024:LW * i + 1027],
                                        in_=pT[:, 128 * j:128 * j + 3]), "eV")
            c.bar()

    emit_ln_pass(lambda tt: xin[3 + 128 * tt:3 + 128 * (tt + 1), :], False, False)
    c.bar()

    @blk.gpsimd
    def _(g):
        c.inc(g.dma_start(out=xhb1, in_=xin[0:3, :]), "qG", 16)
    c.bar()
    emit_halo_ln(xhb1, hlnb1, False, hs2)

    if debug and upto == 1:
        @blk.gpsimd
        def _(g):
            for i in range(32):
                c.inc(g.dma_start(out=dbg["lnx"][i], in_=LNX[:, LW * i:LW * i + LW]), "qG", 16)
            c.wait(g, "qG")
        es.close()
        return nc, dbg

    @blk.tensor
    def _(t):
        for th in range(2):
            pp = p4 if th == 0 else p5
            for k in range(NKT):
                t.matmul(pp[0:R48, :], ATf[:, R48 * k:R48 * (k + 1)],
                         LNX[:, LW * k + 512 * th:LW * k + 512 * (th + 1)],
                         start=(k == 0), stop=(k == NKT - 1))
        for k in range(NKT):
            mi = t.matmul(p6[0:R48, 0:3], ATf[:, R48 * k:R48 * (k + 1)],
                          LNX[:, LW * k + 1024:LW * k + 1027],
                          start=(k == 0), stop=(k == NKT - 1))
        c.inc(mi, "eT")

    @blk.vector
    def _(v):
        c.wait(v, "eT")
        v.tensor_copy(out=rmk[0:R48, 0:512], in_=p4[0:R48, :])
        v.tensor_copy(out=rmk[0:R48, 512:1024], in_=p5[0:R48, :])
        v.tensor_copy(out=rmk[0:R48, 1024:1027], in_=p6[0:R48, 0:3])
        c.inc(v.drain(), "eV")
    c.bar()

    dmat_mark = {}
    cast_mark = {}
    evac_mark = {}
    sq_mark = {}
    x1w_mark = {}
    xld_mark = {}
    mm_mark = {}
    hevac_mark = {}
    hmm_mark = {}
    h512_mark = {}

    for oc in range(8):
        qsem = "qA" if oc % 2 == 0 else "qB"

        @blk.gpsimd
        def _(g, oc=oc, qsem=qsem):
            if (oc - 2) in hmm_mark:
                c.wait(g, "eT", hmm_mark[oc - 2])
            for k in range(NKT):
                c.inc(g.dma_start(out=WFF[oc % 2][:, 512 * k:512 * (k + 1)],
                                  in_=ffn_wT[128 * k:128 * (k + 1), 512 * oc:512 * (oc + 1)]),
                      qsem, 16)
            dmat_mark[oc] = (qsem, c.mark(qsem))

        for tt in range(8):
            @blk.scalar
            def _(s, oc=oc, tt=tt):
                prev = (oc, tt - 2) if tt >= 2 else (oc - 1, tt + 6)
                if prev in evac_mark:
                    c.wait(s, "eV", evac_mark[prev])
                c.inc(s.dma_start(out=xcb[:, 512 * (tt % 2):512 * (tt % 2 + 1)],
                                  in_=xin[3 + 128 * tt:3 + 128 * (tt + 1), 512 * oc:512 * (oc + 1)]),
                      "qC", 16)
                xld_mark[(oc, tt)] = c.mark("qC")

            @blk.tensor
            def _(t, oc=oc, tt=tt):
                if tt == 0:
                    c.wait(t, dmat_mark[oc][0], dmat_mark[oc][1])
                bank = PS[tt % 4]
                prevg = (oc, tt - 4) if tt >= 4 else (oc - 1, tt + 4)
                if prevg in evac_mark:
                    c.wait(t, "eV", evac_mark[prevg])
                for k in range(NKT):
                    t.matmul(bank[:], LNX[:, LW * k + 128 * tt:LW * k + 128 * (tt + 1)],
                             WFF[oc % 2][:, 512 * k:512 * (k + 1)],
                             start=(k == 0), stop=False)
                c.inc(t.matmul(bank[:], rmk[0:49, 128 * tt:128 * (tt + 1)],
                               BTFB[0:49, 512 * oc:512 * (oc + 1)], start=False, stop=True), "eT")
                mm_mark[(oc, tt)] = c.mark("eT")

            @blk.vector
            def _(v, oc=oc, tt=tt):
                c.wait(v, "eT", mm_mark[(oc, tt)])
                c.wait(v, "qC", xld_mark[(oc, tt)])
                prevslot = (oc, tt - 2) if tt >= 2 else (oc - 1, tt + 6)
                if prevslot in x1w_mark:
                    c.wait(v, "qW", x1w_mark[prevslot])
                if prevslot in sq_mark:
                    c.wait(v, "eS", sq_mark[prevslot])
                slot = x1b[:, 512 * (tt % 2):512 * (tt % 2 + 1)]
                v.scalar_tensor_tensor(out=slot, in0=PS[tt % 4][:], scalar=halfcol[:],
                                       in1=xcb[:, 512 * (tt % 2):512 * (tt % 2 + 1)],
                                       op0=ALU.mult, op1=ALU.add)
                v.drain()
                v.reduce_sum(out=statS3[:, 8 * tt + oc:8 * tt + oc + 1], in_=slot, axis=AXX)
                c.inc(v.drain(), "eV")
                evac_mark[(oc, tt)] = c.mark("eV")

            @blk.scalar
            def _(s, oc=oc, tt=tt):
                c.wait(s, "eV", evac_mark[(oc, tt)])
                s.activation(out=scr[:], in_=x1b[:, 512 * (tt % 2):512 * (tt % 2 + 1)],
                             func=ACT.Square, accum_out=statS4[:, 8 * tt + oc:8 * tt + oc + 1])
                c.inc(s.drain(), "eS")
                sq_mark[(oc, tt)] = c.mark("eS")

            @blk.gpsimd
            def _(g, oc=oc, tt=tt):
                c.wait(g, "eV", evac_mark[(oc, tt)])
                c.inc(g.dma_start(out=o_x1[128 * tt:128 * (tt + 1), 512 * oc:512 * (oc + 1)],
                                  in_=x1b[:, 512 * (tt % 2):512 * (tt % 2 + 1)]), "qW", 16)
                x1w_mark[(oc, tt)] = c.mark("qW")

        @blk.scalar
        def _(s, oc=oc):
            if (oc - 1) in hevac_mark:
                c.wait(s, "eV", hevac_mark[oc - 1])
            c.inc(s.dma_start(out=hscr[:], in_=xin[0:3, 512 * oc:512 * (oc + 1)]), "qC", 16)

        @blk.tensor
        def _(t, oc=oc):
            if (oc - 1) in hevac_mark:
                c.wait(t, "eV", hevac_mark[oc - 1])
            for k in range(NKT):
                t.matmul(p6[0:3, 0:512], LNX[:, LW * k + 1024:LW * k + 1027],
                         WFF[oc % 2][:, 512 * k:512 * (k + 1)], start=(k == 0), stop=False)
            c.inc(t.matmul(p6[0:3, 0:512], rmk[0:49, 1024:1027],
                           BTFB[0:49, 512 * oc:512 * (oc + 1)], start=False, stop=True), "eT")
            hmm_mark[oc] = c.mark("eT")

        @blk.vector
        def _(v, oc=oc):
            c.wait(v, "eT", hmm_mark[oc])
            c.wait(v, "qC")
            if oc > 0:
                c.wait(v, "qW", h512_mark[oc - 1])
            v.scalar_tensor_tensor(out=h512[:], in0=p6[0:3, 0:512],
                                   scalar=halfcol[0:3, 0:1], in1=hscr[:], op0=ALU.mult, op1=ALU.add)
            c.inc(v.drain(), "eV")
            hevac_mark[oc] = c.mark("eV")

        @blk.gpsimd
        def _(g, oc=oc):
            c.wait(g, "eV", hevac_mark[oc])
            c.inc(g.dma_start(out=x1h_dram[:, 512 * oc:512 * (oc + 1)], in_=h512[:]), "qW", 16)
            h512_mark[oc] = c.mark("qW")
    c.bar()

    if debug and upto == 2:
        @blk.gpsimd
        def _(g):
            c.inc(g.dma_start(out=dbg["x1h"], in_=x1h_dram[:]), "qG", 16)
            c.inc(g.dma_start(out=dbg["wff"], in_=WFF[1]), "qG", 16)
            c.inc(g.dma_start(out=dbg["rmk"], in_=rmk[0:64, :]), "qG", 16)
            c.wait(g, "qG")
        es.close()
        return nc, dbg

    @blk.vector
    def _(v):
        for c34 in range(34):
            for k4 in range(4):
                d = 4 * c34 + k4
                v.tensor_scalar(out=DIAG[:, 128 * d:128 * (d + 1)], in0=idb[:],
                                scalar1=convw[:, d:d + 1], scalar2=None, op0=ALU.mult)
        c.inc(v.drain(), "eV")

    emit_ln_pass(lambda tt: o_x1[128 * tt:128 * (tt + 1), :], True, True)
    c.bar()

    @blk.gpsimd
    def _(g):
        c.inc(g.dma_start(out=x1hbuf, in_=x1h_dram), "qG", 16)
    c.bar()
    emit_halo_ln(x1hbuf, hlnb3, True, hs4)

    @blk.vector
    def _(v):
        for i in range(32):
            v.tensor_scalar(out=LNX[:, LW * i:LW * i + 1027], in0=LNX[:, LW * i:LW * i + 1027],
                            scalar1=lnwc[:, i:i + 1], scalar2=None, op0=ALU.mult)
        c.inc(v.drain(), "eV")
    c.bar()

    if debug and upto == 3:
        @blk.gpsimd
        def _(g):
            for i in range(32):
                c.inc(g.dma_start(out=dbg["lnx"][i], in_=LNX[:, LW * i:LW * i + LW]), "qG", 16)
            c.wait(g, "qG")
        es.close()
        return nc, dbg

    wp_mark = {}
    ipmm_mark = {}

    MKSTOP = 9
    for p in range(34):
        tiles = [66] if p == 33 else [2 * p, 2 * p + 1]
        width = 64 if p == 33 else 256
        colbase = 8448 if p == 33 else 256 * p
        is_xbc = 32 <= tiles[0] < 66

        @blk.gpsimd
        def _(g, p=p, width=width, colbase=colbase):
            if (p - 2) in ipmm_mark:
                c.wait(g, "eT", ipmm_mark[p - 2])
            for k in range(NKT):
                c.inc(g.dma_start(
                    out=WP[p % 2][:, k * width:(k + 1) * width],
                    in_=ipw[128 * k:128 * (k + 1), colbase:colbase + width]), "qG", 16)
            wp_mark[p] = c.mark("qG")

        if MKSTOP < 2:
            ipmm_mark[p] = c.mark("eT")
            continue

        @blk.tensor
        def _(t, p=p, tiles=tiles, width=width, is_xbc=is_xbc):
            c.wait(t, "qG", wp_mark[p])
            c.wait(t, "eS")
            c.wait(t, "eV")
            for ti, o in enumerate(tiles):
                m = 64 if o == 66 else 128
                for th in range(2):
                    bank = PS[2 * ti + th]
                    for k in range(NKT):
                        mi = t.matmul(bank[0:m, :],
                                      WP[p % 2][:, k * width + 128 * ti:k * width + 128 * ti + m],
                                      LNX[:, LW * k + 512 * th:LW * k + 512 * (th + 1)],
                                      start=(k == 0), stop=(k == NKT - 1))
            if is_xbc:
                for k in range(NKT):
                    mi = t.matmul(p6[0:3, 0:width], LNX[:, LW * k + 1024:LW * k + 1027],
                                  WP[p % 2][:, k * width:(k + 1) * width],
                                  start=(k == 0), stop=(k == NKT - 1))
            c.inc(mi, "eT")
            ipmm_mark[p] = c.mark("eT")

        if MKSTOP < 3:
            continue

        @blk.scalar
        def _(s, p=p, tiles=tiles, width=width, is_xbc=is_xbc):
            c.wait(s, "eT", ipmm_mark[p])
            c.wait(s, "qW")
            for ti, o in enumerate(tiles):
                if o < 32:
                    zb = xstmp if ti == 0 else xbctmp[:, 0:1024]
                    s.activation(out=zb[:, 0:512], in_=PS[2 * ti][:],
                                 func=ACT.Silu, scale=mupc[:, o:o + 1])
                    s.activation(out=zb[:, 512:1024], in_=PS[2 * ti + 1][:],
                                 func=ACT.Silu, scale=mupc[:, o:o + 1])
                elif o < 66:
                    buf = xbctmp if ti == 0 else xbc2tmp
                    s.copy(out=hzx[:, 128 * (o - 32):128 * (o - 31)],
                           in_=p6[0:3, 128 * ti:128 * (ti + 1)])
                    s.copy(out=buf[:, 3:515], in_=PS[2 * ti][:])
                    s.copy(out=buf[:, 515:1027], in_=PS[2 * ti + 1][:])
                else:
                    s.activation(out=dtraw[0:64, 0:512], in_=PS[0][0:64, :], func=ACT.Identity,
                                 bias=dtbias_c[:], scale=mupc[0:64, 66:67])
                    s.activation(out=dtraw[0:64, 512:1024], in_=PS[1][0:64, :], func=ACT.Identity,
                                 bias=dtbias_c[:], scale=mupc[0:64, 66:67])
            c.inc(s.drain(), "eS")
        semark = c.mark("eS")

        if tiles[0] < 32 and MKSTOP >= 4:
            @blk.gpsimd
            def _(g, p=p, tiles=tiles, semark=semark):
                c.wait(g, "eS", semark)
                for ti, o in enumerate(tiles):
                    zb = xstmp if ti == 0 else xbctmp[:, 0:1024]
                    c.inc(g.dma_start(out=o_siluz.rearrange("(i pp) t -> i pp t", pp=128)[o],
                                      in_=zb[:]), "qW", 16)

        if is_xbc:
            @blk.tensor
            def _(t, p=p, tiles=tiles, semark=semark):
                c.wait(t, "eS", semark)
                for ti, o in enumerate(tiles):
                    c.inc(t.transpose(pT[:, 256 * ti:256 * ti + 3],
                                      hzx[:, 128 * (o - 32):128 * (o - 31)],
                                      idb[0:3, 0:3]), "eT")
            tmark = c.mark("eT")

            @blk.vector
            def _(v, p=p, tiles=tiles, tmark=tmark):
                c.wait(v, "eT", tmark)
                for ti, o in enumerate(tiles):
                    buf = xbctmp if ti == 0 else xbc2tmp
                    v.tensor_tensor(out=buf[:, 0:3], in0=pT[:, 256 * ti:256 * ti + 3],
                                    in1=halom_s[:, 0:3], op=ALU.mult)
                c.inc(v.drain(), "eV")
            vmark = c.mark("eV")

            for ti in range(2):
                o = tiles[ti]

                @blk.tensor
                def _(t, p=p, o=o, ti=ti, vmark=vmark):
                    c.wait(t, "eV", vmark)
                    c.wait(t, "eS")
                    buf = xbctmp if ti == 0 else xbc2tmp
                    for th in range(2):
                        bank = p4 if th == 0 else p5
                        for k4 in range(4):
                            d = 4 * (o - 32) + k4
                            mi = t.matmul(bank[:], DIAG[:, 128 * d:128 * (d + 1)],
                                          buf[:, 512 * th + k4:512 * th + k4 + 512],
                                          start=(k4 == 0), stop=(k4 == 3))
                    c.inc(mi, "eT")
                cmark = c.mark("eT")

                @blk.scalar
                def _(s, p=p, o=o, cmark=cmark):
                    c.wait(s, "eT", cmark)
                    c.wait(s, "qW")
                    if o == 64:
                        s.activation(out=Bfm[:, 0:512], in_=p4[:], func=ACT.Silu)
                        s.activation(out=Bfm[:, 512:1024], in_=p5[:], func=ACT.Silu)
                    elif o == 65:
                        s.activation(out=Cfm[:, 0:512], in_=p4[:], func=ACT.Silu)
                        s.activation(out=Cfm[:, 512:1024], in_=p5[:], func=ACT.Silu)
                    else:
                        s.activation(out=xstmp[:, 0:512], in_=p4[:], func=ACT.Silu)
                        s.activation(out=xstmp[:, 512:1024], in_=p5[:], func=ACT.Silu)
                    c.inc(s.drain(), "eS")
                smark = c.mark("eS")

                if o < 64:
                    @blk.gpsimd
                    def _(g, o=o, smark=smark):
                        c.wait(g, "eS", smark)
                        c.inc(g.dma_start(
                            out=xs_dram.rearrange("(i pp) t -> i pp t", pp=128)[o - 32],
                            in_=xstmp[:]), "qW", 16)

        if tiles[0] == 66:
            @blk.vector
            def _(v, p=p):
                c.wait(v, "eS")
                v.tensor_scalar(out=dtabs, in0=dtraw[0:64, :], scalar1=-1.0, scalar2=None,
                                op0=ALU.mult)
                v.drain()
                v.tensor_tensor(out=dtabs, in0=dtabs, in1=dtraw[0:64, :], op=ALU.max)
                c.inc(v.drain(), "eV")

            @blk.scalar
            def _(s, p=p):
                c.wait(s, "eV")
                s.activation(out=dtabs, in_=dtabs, func=ACT.Exp, scale=-1.0)
                s.drain()
                s.activation(out=dtabs, in_=dtabs, func=ACT.Ln, bias=1.0)
                c.inc(s.drain(), "eS")

            @blk.vector
            def _(v, p=p):
                c.wait(v, "eS")
                v.tensor_single_scalar(out=dtf[:], in_=dtraw[0:64, :], scalar=0.0, op=ALU.max)
                v.drain()
                v.tensor_tensor(out=dtf[:], in0=dtf[:], in1=dtabs, op=ALU.add)
                c.inc(v.drain(), "eV")
    c.bar()

    @blk.gpsimd
    def _(g):
        c.inc(g.dma_start(out=o_C, in_=Cfm[:]), "qW", 16)
        c.wait(g, "qW")
    c.bar()

    if debug and upto == 4:

        @blk.gpsimd
        def _(g):
            c.inc(g.dma_start(out=dbg["B"], in_=Bfm[:]), "qG", 16)
            c.inc(g.dma_start(out=dbg["dt"], in_=dtf[:]), "qG", 16)
            c.inc(g.dma_start(out=dbg["zz"], in_=xstmp[:]), "qG", 16)
            for i in range(32):
                c.inc(g.dma_start(out=dbg["xs"].rearrange("(i pp) t -> i pp t", pp=128)[i],
                                  in_=xs_dram.rearrange("(i pp) t -> i pp t", pp=128)[i]), "qG", 16)
            c.wait(g, "qG")
        es.close()
        return nc, dbg

    XC = XCH[0]

    @blk.vector
    def _(v):
        v.memset(A2[:, 20480:32768], 0.0)
        v.memset(prm3[0:1, 0:64], 0.0)
        v.tensor_scalar(out=x1b[0:64, :], in0=dtf[:], scalar1=nA_c[:], scalar2=None,
                        op0=ALU.mult)
        v.drain()
        for qq in range(NCH):
            v.tensor_tensor_scan(out=cumall[:, Q * qq:Q * (qq + 1)], data0=ones64[:],
                                 data1=x1b[0:64, Q * qq:Q * (qq + 1)], initial=0.0,
                                 op0=ALU.mult, op1=ALU.add)
        c.inc(v.drain(), "eV")

    @blk.gpsimd
    def _(g):
        c.wait(g, "eV")
        c.inc(g.dma_start(out=flat_dram.rearrange("h c t -> h (c t)"), in_=cumall[:]), "qW", 16)

    @blk.tensor
    def _(t):
        c.inc(t.matmul(p6[:, 0:64], onescf[:], prm3[0:1, 128:192], start=True, stop=True), "eT")

    @blk.vector
    def _(v):
        c.wait(v, "eT")
        v.tensor_copy(out=Dtok[:], in_=p6[:, 0:64])
        c.inc(v.drain(), "eV")
    c.bar()

    for q in range(NCH):
        @blk.scalar
        def _(s, q=q):
            for i in range(32):
                c.inc(s.dma_start(out=XPP[:, 128 * i:128 * (i + 1)],
                                  in_=xs_dram.rearrange("(i pp) t -> i pp t", pp=128)[i][:, Q * q:Q * (q + 1)]),
                      "qC", 16)
        xstm = c.mark("qC")

        @blk.gpsimd
        def _(g, q=q):
            for i in range(32):
                c.inc(g.dma_start(out=G1CH[:, 128 * i:128 * (i + 1)],
                                  in_=o_siluz.rearrange("(i pp) t -> i pp t", pp=128)[i][:, Q * q:Q * (q + 1)]),
                      "qG", 16)
            c.inc(g.dma_start(out=cumflat[0:1, :], in_=flat_dram[:, q, :]), "qG", 16)
        szlm = c.mark("qG")

        @blk.sync
        def _(sy, q=q, xstm=xstm, szlm=szlm):
            c.wait(sy, "qC", xstm)
            c.wait(sy, "qG", szlm)
            for i in range(32):
                c.inc(sy.dma_start(out=XC[:, 128 * i:128 * (i + 1)],
                                   in_=XPP[:, 128 * i:128 * (i + 1)], transpose=True), "qS", 16)
            for i in range(32):
                c.inc(sy.dma_start(out=HTs[:, 128 * i:128 * (i + 1)],
                                   in_=G1CH[:, 128 * i:128 * (i + 1)], transpose=True), "qS", 16)
            c.inc(sy.dma_start(out=Btok[:], in_=Bfm[:, Q * q:Q * (q + 1)], transpose=True), "qS", 16)
        szm = c.mark("qS")

        @blk.tensor
        def _(t, q=q):
            c.inc(t.transpose(p6[:, 64:128], cumall[:, Q * q:Q * (q + 1)], idf[0:64, 0:64]), "eT")
            c.inc(t.transpose(p6[:, 128:192], dtf[:, Q * q:Q * (q + 1)], idf[0:64, 0:64]), "eT")
            c.inc(t.transpose(p6[0:1, 320:384], cumall[:, Q * q + 127:Q * q + 128],
                              idf[0:64, 0:64]), "eT")
        tam = c.mark("eT")

        @blk.vector
        def _(v, q=q, tam=tam):
            c.wait(v, "eT", tam)
            v.tensor_copy(out=cumT[:], in_=p6[:, 64:128])
            v.tensor_copy(out=dtT[:], in_=p6[:, 128:192])
            v.tensor_copy(out=ecrow[:], in_=p6[0:1, 320:384])
            v.drain()
            v.tensor_scalar(out=ncumT[:], in0=cumT[:], scalar1=-1.0, scalar2=None, op0=ALU.mult)
            c.inc(v.drain(), "eV")
        vam = c.mark("eV")

        @blk.scalar
        def _(s, q=q, vam=vam):
            c.wait(s, "eV", vam)
            s.activation(out=ECtok[:], in_=cumT[:], func=ACT.Exp)
            c.inc(s.drain(), "eS")
        ecm = c.mark("eS")

        @blk.tensor
        def _(t, q=q, vam=vam):
            c.wait(t, "eV", vam)
            c.inc(t.matmul(p6[:, 192:256], onescf[:], ecrow[:], start=True, stop=True), "eT")
            c.inc(t.matmul(p6[:, 384:448], onescf[:], prm3[0:1, 0:64], start=True, stop=True), "eT")
        tbm = c.mark("eT")

        @blk.vector
        def _(v, q=q, tbm=tbm):
            c.wait(v, "eT", tbm)
            v.tensor_tensor(out=xcb[:, 128:192], in0=p6[:, 192:256], in1=cumT[:], op=ALU.subtract)
            v.tensor_tensor(out=ocumb[:], in0=cumT[:], in1=p6[:, 384:448], op=ALU.add)
            v.drain()
            v.tensor_tensor(out=prm3[0:1, 0:64], in0=prm3[0:1, 0:64], in1=ecrow[:], op=ALU.add)
            c.inc(v.drain(), "eV")
        vbm = c.mark("eV")

        @blk.scalar
        def _(s, q=q, vbm=vbm):
            c.wait(s, "eV", vbm)
            s.activation(out=EEs[:, 0:64], in_=xcb[:, 128:192], func=ACT.Exp)
            s.activation(out=dend[:, 0:64], in_=p6[:, 192:256], func=ACT.Exp)
            c.inc(s.drain(), "eS")
        eem = c.mark("eS")

        @blk.gpsimd
        def _(g, q=q, vbm=vbm):
            c.wait(g, "eV", vbm)
            c.inc(g.dma_start(out=o_cum[Q * q:Q * (q + 1), :], in_=ocumb[:]), "qW", 16)
            c.inc(g.dma_start(out=o_ctot[q:q + 1, :], in_=prm3[0:1, 0:64]), "qW", 16)
            c.wait(g, "qS", szm)
            c.inc(g.dma_start(out=o_sztok[Q * q:Q * (q + 1), :], in_=HTs[:]), "qW", 16)

        @blk.vector
        def _(v, q=q, eem=eem, szm=szm):
            c.wait(v, "eS", eem)
            c.wait(v, "qS", szm)
            for h in range(NH):
                v.tensor_scalar(out=XTOK[:, 64 * h:64 * (h + 1)], in0=XC[:, 64 * h:64 * (h + 1)],
                                scalar1=dtT[:, h:h + 1], scalar2=None, op0=ALU.mult)
            for h in range(NH):
                v.tensor_scalar(out=XPP[:, 64 * h:64 * (h + 1)], in0=XTOK[:, 64 * h:64 * (h + 1)],
                                scalar1=EEs[:, h:h + 1], scalar2=None, op0=ALU.mult)
            c.inc(v.drain(), "eV")
        foldm = c.mark("eV")

        @blk.tensor
        def _(t, q=q):
            c.inc(t.matmul(p4[:, 0:128], Bfm[:, Q * q:Q * (q + 1)], Cfm[:, Q * q:Q * (q + 1)],
                           start=True, stop=True), "eT")
        gtm = c.mark("eT")

        @blk.vector
        def _(v, q=q, gtm=gtm):
            c.wait(v, "eT", gtm)
            v.tensor_copy(out=C4[:, 0:128], in_=p4[:, 0:128])
            v.tensor_copy(out=C4[:, 128:256], in_=p4[:, 0:128])
            c.inc(v.drain(), "eV")
        gdm = c.mark("eV")

        repm, maskm, expm, wm, y1m, carrym, evacm = {}, {}, {}, {}, {}, {}, {}
        for ip in range(32):
            iq = ip // 4
            e0 = 256 * (ip % 2)

            if ip % 4 == 0:
                @blk.tensor
                def _(t, q=q, iq=iq):
                    if iq >= 2:
                        c.wait(t, "eV", evacm[iq - 2])
                    c.inc(t.matmul(PS[2 + iq % 2][:], Cfm[:, Q * q:Q * (q + 1)],
                                   HTb[:, 512 * iq:512 * (iq + 1)], start=True, stop=True), "eT")
                carrym[iq] = c.mark("eT")

            @blk.tensor
            def _(t, q=q, ip=ip, e0=e0, szlm=szlm):
                c.wait(t, "qG", szlm)
                if ip >= 2:
                    c.wait(t, "eV", maskm[ip - 2])
                c.inc(t.matmul(p5[:, e0:e0 + 256], onescf[:],
                               cumflat[0:1, 256 * ip:256 * (ip + 1)], start=True, stop=True), "eT")
            repm[ip] = c.mark("eT")

            @blk.vector
            def _(v, q=q, ip=ip, e0=e0):
                c.wait(v, "eT", repm[ip])
                if ip >= 2:
                    c.wait(v, "eS", expm[ip - 2])
                v.tensor_tensor(out=scr[:, e0:e0 + 256], in0=p5[:, e0:e0 + 256],
                                in1=mkb4[:, 0:256], op=ALU.add)
                c.inc(v.drain(), "eV")
            maskm[ip] = c.mark("eV")

            @blk.scalar
            def _(s, q=q, ip=ip, e0=e0):
                c.wait(s, "eV", maskm[ip])
                if ip >= 2:
                    c.wait(s, "eV", wm[ip - 2])
                s.activation(out=E4[:, e0:e0 + 128], in_=scr[:, e0:e0 + 128],
                             func=ACT.Exp, bias=ncumT[:, 2 * ip:2 * ip + 1])
                s.activation(out=E4[:, e0 + 128:e0 + 256], in_=scr[:, e0 + 128:e0 + 256],
                             func=ACT.Exp, bias=ncumT[:, 2 * ip + 1:2 * ip + 2])
                c.inc(s.drain(), "eS")
            expm[ip] = c.mark("eS")

            @blk.vector
            def _(v, q=q, ip=ip, e0=e0):
                c.wait(v, "eS", expm[ip])
                c.wait(v, "eV", gdm)
                if ip >= 2:
                    c.wait(v, "eT", y1m[ip - 2])
                v.tensor_tensor(out=W4[:, e0:e0 + 256], in0=E4[:, e0:e0 + 256],
                                in1=C4[:, 0:256], op=ALU.mult)
                c.inc(v.drain(), "eV")
            wm[ip] = c.mark("eV")

            @blk.tensor
            def _(t, q=q, ip=ip, iq=iq, e0=e0, foldm=foldm):
                c.wait(t, "eV", wm[ip])
                c.wait(t, "eV", foldm)
                if ip % 4 == 0 and iq >= 2:
                    c.wait(t, "eV", evacm[iq - 2])
                l = 2 * ip - 8 * iq
                t.matmul(PS[iq % 2][:, 64 * l:64 * (l + 1)], W4[:, e0:e0 + 128],
                         XTOK[:, 64 * (2 * ip):64 * (2 * ip + 1)], start=True, stop=True)
                c.inc(t.matmul(PS[iq % 2][:, 64 * (l + 1):64 * (l + 2)], W4[:, e0 + 128:e0 + 256],
                               XTOK[:, 64 * (2 * ip + 1):64 * (2 * ip + 2)], start=True, stop=True),
                      "eT")
            y1m[ip] = c.mark("eT")

            if ip % 4 == 3:
                @blk.vector
                def _(v, q=q, ip=ip, iq=iq, ecm=ecm, szm=szm):
                    c.wait(v, "eT", y1m[ip])
                    c.wait(v, "eT", carrym[iq])
                    c.wait(v, "eS", ecm)
                    c.wait(v, "qS", szm)
                    for l in range(8):
                        h = 8 * iq + l
                        v.tensor_scalar(out=xcb[:, 0:64], in0=PS[2 + iq % 2][:, 64 * l:64 * (l + 1)],
                                        scalar1=ECtok[:, h:h + 1], scalar2=None, op0=ALU.mult)
                        v.tensor_tensor(out=xcb[:, 64:128], in0=xcb[:, 0:64],
                                        in1=PS[iq % 2][:, 64 * l:64 * (l + 1)], op=ALU.add)
                        v.scalar_tensor_tensor(out=xcb[:, 0:64], in0=XC[:, 64 * h:64 * (h + 1)],
                                               scalar=Dtok[:, h:h + 1], in1=xcb[:, 64:128],
                                               op0=ALU.mult, op1=ALU.add)
                        v.tensor_tensor(out=G1CH[:, 64 * h:64 * (h + 1)], in0=xcb[:, 0:64],
                                        in1=HTs[:, 64 * h:64 * (h + 1)], op=ALU.mult)
                    c.inc(v.drain(), "eV")
                evacm[iq] = c.mark("eV")

        @blk.gpsimd
        def _(g, q=q):
            c.wait(g, "eV", evacm[7])
            c.inc(g.dma_start(out=o_g1[Q * q:Q * (q + 1), :], in_=G1CH[:]), "qW", 16)

        hqm, hupm = {}, {}
        for iq in range(8):
            @blk.tensor
            def _(t, q=q, iq=iq, foldm=foldm, szm=szm):
                if iq == 0:
                    c.wait(t, "eV", evacm[7])
                    c.wait(t, "qS", szm)
                if iq >= 4:
                    c.wait(t, "eV", hupm[iq - 4])
                c.inc(t.matmul(PS[iq % 4][:], Btok[:], XPP[:, 512 * iq:512 * (iq + 1)],
                               start=True, stop=True), "eT")
            hqm[iq] = c.mark("eT")

            @blk.vector
            def _(v, q=q, iq=iq, eem=eem):
                c.wait(v, "eT", hqm[iq])
                c.wait(v, "eS", eem)
                for l in range(8):
                    h = 8 * iq + l
                    v.scalar_tensor_tensor(out=HTf[:, 64 * h:64 * (h + 1)], in0=HTf[:, 64 * h:64 * (h + 1)],
                                           scalar=dend[:, h:h + 1], in1=PS[iq % 4][:, 64 * l:64 * (l + 1)],
                                           op0=ALU.mult, op1=ALU.add)
                v.drain()
                v.tensor_copy(out=HTb[:, 512 * iq:512 * (iq + 1)], in_=HTf[:, 512 * iq:512 * (iq + 1)])
                c.inc(v.drain(), "eV")
            hupm[iq] = c.mark("eV")
        c.bar()

    @blk.gpsimd
    def _(g):
        c.inc(g.dma_start(out=o_hloc[:], in_=HTf[:]), "qW", 16)
        c.wait(g, "qW")
    c.bar()

    es.close()
    return nc, dbg


def build_l2(debug=0):
    STOP = 9
    nc = bass.Bass("TRN2", target_bir_lowering=False)
    es = ExitStack()

    def din(name, shape, dt=F32):
        return nc.dram_tensor(name, shape, dt, kind="ExternalInput").ap()

    def dout(name, shape, dt=F32):
        return nc.dram_tensor(name, shape, dt, kind="ExternalOutput").ap()

    def sb(name, p, f, dt):
        return es.enter_context(nc.sbuf_tensor(name, [p, f], dt))

    def ps(name, p, f, dt=F32):
        return es.enter_context(nc.psum_tensor(name, [p, f], dt))

    x1 = din("x1", [T, H])
    g1 = din("g1", [T, H], BF16)
    sz = din("sz", [T, H], BF16)
    Cbf = din("Cbf", [NS, T], BF16)
    cum = din("cum", [T, NH])
    ctcore = din("ctcore", [8, NH])
    hloc = din("hloc", [8 * NS, H])
    prefw = din("prefw", [NH, 8])
    norm_w = din("norm_w", [1, H])
    opw = din("opw", [H, H])
    ffn_wT = din("ffn_wT", [H, H])
    ffn_b = din("ffn_b", [1, H])
    lora_A = din("lora_A", [R48, H])
    lora_B = din("lora_B", [3 * H, 16])
    lsi = din("lsi", [1, 4], I32)

    o_out = dout("o_out", [T, H])
    x2_dram = nc.dram_tensor("x2_dram", [T, H], F32, kind="Internal").ap()

    dbg = {}
    if debug:
        dbg["gy"] = dout("d_gy", [Q, H])
        dbg["x2"] = dout("d_x2", [T, H])
        dbg["hin"] = dout("d_hin", [NS, H], BF16)
        dbg["ew"] = dout("d_ew", [NH, 8])
        dbg["wff"] = dout("d_wff", [128, 32768], BF16)
        dbg["lnf"] = dout("d_lnf", [128, 1024], BF16)
        dbg["rmk"] = dout("d_rmk", [64, 1024], BF16)
        dbg["btfb"] = dout("d_btfb", [128, 4096], BF16)

    LNF = sb("LNF", 128, 32768, BF16)
    WB = sb("WB", 128, 32768, BF16)
    AR2 = sb("AR2", 128, 8192, BF16)
    GY = sb("GY", 128, 4096, F32)
    GYN = sb("GYN", 128, 4096, BF16)
    G1B = sb("G1B", 128, 4096, BF16)
    HINb = AR2[:, 0:4096]
    NWREP = AR2[:, 4096:8192]
    Cfm = sb("Cfm", 128, T, BF16)
    cumL2 = sb("cumL2", 128, 512, F32)
    EC2 = sb("EC2", 128, 64, F32)
    ATf = sb("ATf", 128, 32 * R48, BF16)
    BTFB = sb("BTFB", 128, H, BF16)
    biasrow = BTFB[48:49, :]
    rmk2 = sb("rmk2", 64, 1024, BF16)
    xcb2 = sb("xcb2", 128, 1024, F32)
    x2b = sb("x2b", 128, 1024, F32)
    statS3 = sb("statS3", 128, 64, F32)
    statS4 = sb("statS4", 128, 64, F32)
    colM = sb("colM", 128, 1, F32)
    colR = sb("colR", 128, 1, F32)
    colN = sb("colN", 128, 1, F32)
    colTa = sb("colTa", 128, 1, F32)
    colTb = sb("colTb", 128, 1, F32)
    colS = sb("colS", 128, 1, F32)
    colQ = sb("colQ", 128, 1, F32)
    scr = sb("scr", 128, 512, F32)
    EWR = scr
    idb = sb("idb", 128, 128, BF16)
    idf = sb("idf", 128, 128, F32)
    onescf = sb("onescf", 1, 128, F32)
    halfcol = sb("halfcol", 128, 1, F32)
    cts = sb("cts", 8, 64, F32)
    ct64 = sb("ct64", 64, 8, F32)
    sfx = sb("sfx", 64, 8, F32)
    ew = sb("ew", 64, 8, F32)
    pw64 = sb("pw64", 64, 8, F32)
    ewrow = sb("ewrow", 1, 64, F32)
    lsis = sb("lsis", 1, 4, I32)
    lsif = sb("lsif", 1, 4, F32)
    cnt1 = sb("cnt1", 1, 1, F32)
    cst13 = sb("cst13", 1, 3, F32)
    msk13 = sb("msk13", 1, 3, F32)

    HLD = LNF[:, 0:8192].bitcast(F32)
    LAB = WB[:, 0:4096]
    LBB = WB[:, 4096:5632]
    btmp = WB[0:16, 5632:6656]
    OPW = [WB[:, 0:16384], WB[:, 16384:32768]]
    WFF = [WB[:, 0:16384], WB[:, 16384:32768]]
    OTB = [AR2[:, 0:4096], AR2[:, 4096:8192]]
    xst2 = GY
    lnxtok2 = GYN
    SZB = GYN

    p0 = ps("p0", 128, 512)
    p1 = ps("p1", 128, 512)
    p2 = ps("p2", 128, 512)
    p3 = ps("p3", 128, 512)
    p4 = ps("p4", 128, 512)
    p5 = ps("p5", 128, 512)
    p6 = ps("p6", 128, 512)
    pT = ps("pT", 128, 1024, BF16)
    PS = [p0, p1, p2, p3]

    blk = es.enter_context(nc.Block())
    c = Ctx(nc, blk)
    for s in ["qS", "qC", "qG", "qW", "qA", "qB", "qD", "eT", "eV", "eS", "eG"]:
        c.sems[s] = [es.enter_context(nc.semaphore(s)), 0]

    @blk.gpsimd
    def _(g):
        make_ident(g, idb[:])
        make_ident(g, idf[:])
        g.memset(cst13[0:1, 0:1], 1.0)
        g.memset(cst13[0:1, 1:2], 2.0)
        c.inc(g.memset(cst13[0:1, 2:3], 3.0), "eG")
        c.inc(g.dma_start(out=lsis[:], in_=lsi), "qG", 16)
        c.inc(g.dma_start(out=biasrow, in_=ffn_b), "qG", 16)
        c.inc(g.dma_start(out=LAB[0:R48, :], in_=lora_A), "qG", 16)
        c.inc(g.dma_start(out=LBB[:, 0:1536].rearrange("p (a r) -> p a r", r=16),
                          in_=lora_B.rearrange("(a p) r -> p a r", p=128)), "qG", 16)
        c.inc(g.dma_start(out=cts[:], in_=ctcore), "qG", 16)
        c.inc(g.dma_start(out=pw64[:], in_=prefw), "qG", 16)
        c.inc(g.dma_start(out=Cfm[:], in_=Cbf), "qG", 16)
        for qq in range(8):
            c.inc(g.dma_start(out=cumL2[:, 64 * qq:64 * (qq + 1)],
                              in_=cum[Q * qq:Q * (qq + 1), :]), "qG", 16)

    @blk.vector
    def _(v):
        v.memset(rmk2[32:64, :], 1.0)
        v.memset(statS3[:], 0.0)
        v.memset(statS4[:], 0.0)
        v.memset(onescf[:], 1.0)
        v.memset(halfcol[:], 0.5)
        c.wait(v, "eG")
        c.wait(v, "qG")
        v.tensor_copy(out=lsif[:], in_=lsis[:])
        v.drain()
        v.tensor_single_scalar(out=lsif[:], in_=lsif[:], scalar=0.0, op=ALU.is_le)
        v.drain()
        v.reduce_sum(out=cnt1[:], in_=lsif[:], axis=AXX)
        v.drain()
        v.tensor_scalar(out=cnt1[:], in0=cnt1[:], scalar1=-1.0, scalar2=None, op0=ALU.add)
        v.drain()
        c.inc(v.tensor_scalar(out=msk13[:], in0=cst13[:], scalar1=cnt1[:], scalar2=None,
                              op0=ALU.is_equal), "eV")
    c.bar()

    @blk.tensor
    def _(t):
        c.inc(t.transpose(p6[0:3, 102:103], msk13[:], idf[0:1, 0:1]), "eT")
        for i in range(16):
            c.inc(t.transpose(pT[:, 48 * i:48 * (i + 1)], LAB[0:R48, 128 * i:128 * (i + 1)],
                              idb[0:R48, 0:R48]), "eT")
        c.inc(t.transpose(p6[0:64, 110:118], cts[:], idf[0:8, 0:8]), "eT")
    c.bar()

    @blk.vector
    def _(v):
        c.inc(v.tensor_copy(out=ATf[:, 0:768], in_=pT[:, 0:768]), "eV")
        c.inc(v.tensor_copy(out=ct64[:], in_=p6[0:64, 110:118]), "eV")
    c.bar()

    @blk.tensor
    def _(t):
        for i in range(16):
            c.inc(t.transpose(pT[:, 48 * i:48 * (i + 1)], LAB[0:R48, 128 * (16 + i):128 * (17 + i)],
                              idb[0:R48, 0:R48]), "eT")
        c.inc(t.matmul(p6[:, 103:106], onescf[:], msk13[:], start=True, stop=True), "eT")
    c.bar()

    @blk.vector
    def _(v):
        c.inc(v.tensor_copy(out=ATf[:, 768:1536], in_=pT[:, 0:768]), "eV")
        for i in range(32):
            for a in range(3):
                c.inc(v.tensor_scalar(out=ATf[:, 48 * i + 16 * a:48 * i + 16 * (a + 1)],
                                      in0=ATf[:, 48 * i + 16 * a:48 * i + 16 * (a + 1)],
                                      scalar1=p6[:, 103 + a:104 + a], scalar2=None,
                                      op0=ALU.mult), "eV")
        v.tensor_tensor(out=ct64[:], in0=ct64[:], in1=pw64[:], op=ALU.mult)
        v.drain()
        v.memset(sfx[:, 7:8], 0.0)
        v.drain()
        for j in range(6, -1, -1):
            v.tensor_tensor(out=sfx[:, j:j + 1], in0=sfx[:, j + 1:j + 2],
                            in1=ct64[:, j + 1:j + 2], op=ALU.add)
            v.drain()
        c.inc(v.drain(), "eV")
    c.bar()

    @blk.scalar
    def _(s):
        s.activation(out=ew[:], in_=sfx[:], func=ACT.Exp)
        c.inc(s.drain(), "eS")

    @blk.vector
    def _(v):
        c.wait(v, "eS")
        c.inc(v.tensor_tensor(out=ew[:], in0=ew[:], in1=pw64[:], op=ALU.mult), "eV")
    c.bar()

    for a in range(3):
        for w in range(4):
            @blk.tensor
            def _(t, a=a, w=w):
                for j in range(8):
                    i = 8 * w + j
                    c.inc(t.transpose(pT[0:16, 128 * j:128 * (j + 1)],
                                      LBB[:, 16 * (32 * a + i):16 * (32 * a + i + 1)],
                                      idb[:]), "eT")
            c.bar()

            @blk.vector
            def _(v, a=a, w=w):
                c.inc(v.tensor_copy(out=btmp, in_=pT[0:16, 0:1024]), "eV")
            c.bar()

            @blk.gpsimd
            def _(g, a=a, w=w):
                c.inc(g.dma_start(out=BTFB[16 * a:16 * (a + 1), 1024 * w:1024 * (w + 1)],
                                  in_=btmp), "qG", 16)
            c.bar()
    if debug:
        @blk.gpsimd
        def _(g):
            c.inc(g.dma_start(out=dbg["ew"], in_=ew[:]), "qG", 16)

    for j in range(8):
        @blk.tensor
        def _(t, j=j):
            c.inc(t.transpose(p6[0:1, 120:184], ew[:, j:j + 1], idf[0:64, 0:64]), "eT")

        @blk.vector
        def _(v, j=j):
            c.wait(v, "eT")
            c.inc(v.tensor_copy(out=ewrow[:], in_=p6[0:1, 120:184]), "eV")

        @blk.tensor
        def _(t, j=j):
            c.wait(t, "eV")
            c.inc(t.matmul(p6[:, 184:248], onescf[:], ewrow[:], start=True, stop=True), "eT")

        @blk.vector
        def _(v, j=j):
            c.wait(v, "eT")
            c.inc(v.tensor_copy(out=EWR[:, 64 * j:64 * (j + 1)], in_=p6[:, 184:248]), "eV")
        c.bar()

    for jj in range(4):
        @blk.gpsimd
        def _(g, jj=jj):
            c.inc(g.dma_start(out=xcb2[0:1, :], in_=norm_w[0:1, 1024 * jj:1024 * (jj + 1)]), "qG", 16)
        c.bar()

        @blk.tensor
        def _(t, jj=jj):
            c.inc(t.matmul(p4[:], onescf[:], xcb2[0:1, 0:512], start=True, stop=True), "eT")
            c.inc(t.matmul(p5[:], onescf[:], xcb2[0:1, 512:1024], start=True, stop=True), "eT")
        c.bar()

        @blk.vector
        def _(v, jj=jj):
            v.tensor_copy(out=NWREP[:, 1024 * jj:1024 * jj + 512], in_=p4[:])
            c.inc(v.tensor_copy(out=NWREP[:, 1024 * jj + 512:1024 * (jj + 1)], in_=p5[:]), "eV")
        c.bar()

    for j in range(8):
        @blk.gpsimd
        def _(g, j=j):
            c.inc(g.dma_start(out=HLD[:], in_=hloc[NS * j:NS * (j + 1), :]), "qG", 16)
        c.bar()

        @blk.vector
        def _(v, j=j):
            for h in range(NH):
                if j == 0:
                    v.tensor_scalar(out=HINb[:, 64 * h:64 * (h + 1)], in0=HLD[:, 64 * h:64 * (h + 1)],
                                    scalar1=EWR[:, 64 * j + h:64 * j + h + 1], scalar2=None,
                                    op0=ALU.mult)
                else:
                    v.scalar_tensor_tensor(out=HINb[:, 64 * h:64 * (h + 1)],
                                           in0=HLD[:, 64 * h:64 * (h + 1)],
                                           scalar=EWR[:, 64 * j + h:64 * j + h + 1],
                                           in1=HINb[:, 64 * h:64 * (h + 1)],
                                           op0=ALU.mult, op1=ALU.add)
            c.inc(v.drain(), "eV")
        c.bar()
    if debug:
        @blk.gpsimd
        def _(g):
            c.inc(g.dma_start(out=dbg["hin"], in_=HINb[:]), "qG", 16)
        c.bar()

    if STOP < 2:
        es.close()
        return nc, dbg
    NQ2 = 8
    SUB = 9
    gytm = {}
    prev_ev7 = {}
    for q in range(NQ2):
        @blk.gpsimd
        def _(g, q=q):
            if q >= 1:
                c.wait(g, "qS", gytm[q - 1])
                c.wait(g, "eV", prev_ev7[q - 1])
            c.inc(g.dma_start(out=G1B[:], in_=g1[Q * q:Q * (q + 1), :]), "qD", 16)
            c.inc(g.dma_start(out=SZB[:], in_=sz[Q * q:Q * (q + 1), :]), "qD", 16)
        ldm = c.mark("qD")

        @blk.scalar
        def _(s, q=q):
            s.activation(out=EC2[:], in_=cumL2[:, 64 * q:64 * (q + 1)], func=ACT.Exp)
            c.inc(s.drain(), "eS")
        ecm = c.mark("eS")

        mmqm, evqm, corrm = {}, {}, {}
        for iq in range(8):
            @blk.tensor
            def _(t, q=q, iq=iq):
                if iq >= 4:
                    c.wait(t, "eS", corrm[iq - 4])
                t.matmul(PS[iq % 4][:], Cfm[:, Q * q:Q * (q + 1)],
                         HINb[:, 512 * iq:512 * (iq + 1)], start=True, stop=True)
                c.inc(t.matmul(p6[0:1, 448:449], onescf[0:1, 0:1], onescf[0:1, 0:1],
                               start=True, stop=True), "eT")
            mmqm[iq] = c.mark("eT")

            if SUB < 2:
                @blk.vector
                def _(v, q=q, iq=iq):
                    c.wait(v, "eT", mmqm[iq])
                    if SUB == 0:
                        v.tensor_copy(out=GY[:, 512 * iq:512 * (iq + 1)], in_=PS[iq % 4][:])
                    else:
                        v.memset(GY[:, 512 * iq:512 * (iq + 1)], 0.5)
                    c.inc(v.drain(), "eV")
                evqm[iq] = c.mark("eV")
                continue

            cs = scr if iq % 2 == 0 else xcb2[:, 0:512]

            @blk.scalar
            def _(s, q=q, iq=iq, cs=cs, ecm=ecm):
                c.wait(s, "eT", mmqm[iq])
                if iq >= 2:
                    c.wait(s, "eV", evqm[iq - 2])
                for l in range(8):
                    h = 8 * iq + l
                    s.activation(out=cs[:, 64 * l:64 * (l + 1)], in_=PS[iq % 4][:, 64 * l:64 * (l + 1)],
                                 func=ACT.Identity, scale=EC2[:, h:h + 1])
                c.inc(s.drain(), "eS")
            corrm[iq] = c.mark("eS")

            @blk.vector
            def _(v, q=q, iq=iq, cs=cs, ldm=ldm):
                c.wait(v, "eS", corrm[iq])
                c.wait(v, "qD", ldm)
                for l in range(8):
                    h = 8 * iq + l
                    v.tensor_tensor(out=x2b[:, 0:64], in0=cs[:, 64 * l:64 * (l + 1)],
                                    in1=SZB[:, 64 * h:64 * (h + 1)], op=ALU.mult)
                    v.tensor_tensor(out=GY[:, 64 * h:64 * (h + 1)], in0=x2b[:, 0:64],
                                    in1=G1B[:, 64 * h:64 * (h + 1)], op=ALU.add)
                c.inc(v.drain(), "eV")
            evqm[iq] = c.mark("eV")
        prev_ev7[q] = evqm[7]

        @blk.scalar
        def _(s, q=q):
            c.wait(s, "eV", evqm[7])
            for jj in range(8):
                s.activation(out=scr[:], in_=GY[:, 512 * jj:512 * (jj + 1)], func=ACT.Square,
                             accum_out=statS4[:, jj:jj + 1])
            c.inc(s.drain(), "eS")

        @blk.vector
        def _(v, q=q):
            c.wait(v, "eS")
            v.reduce_sum(out=colQ[:], in_=statS4[:, 0:8], axis=AXX)
            v.drain()
            v.tensor_scalar(out=colTa[:], in0=colQ[:], scalar1=1.0 / H, scalar2=EPS,
                            op0=ALU.mult, op1=ALU.add)
            v.drain()
            v.reciprocal(out=colTb[:], in_=colTa[:])
            c.inc(v.drain(), "eV")

        @blk.scalar
        def _(s, q=q):
            c.wait(s, "eV")
            s.activation(out=colR[:], in_=colTb[:], func=ACT.Sqrt)
            c.inc(s.drain(), "eS")

        @blk.vector
        def _(v, q=q):
            c.wait(v, "eS")
            c.wait(v, "qS", c.mark("qS"))
            v.tensor_scalar(out=GYN[:], in0=GY[:], scalar1=colR[:], scalar2=None, op0=ALU.mult)
            v.drain()
            v.tensor_tensor(out=GYN[:], in0=GYN[:], in1=NWREP[:], op=ALU.mult)
            c.inc(v.drain(), "eV")
        gynm = c.mark("eV")

        @blk.sync
        def _(sy, q=q, gynm=gynm):
            c.wait(sy, "eV", gynm)
            for i in range(32):
                c.inc(sy.dma_start(out=LNF[:, 1024 * i + 128 * q:1024 * i + 128 * (q + 1)],
                                   in_=GYN[:, 128 * i:128 * (i + 1)], transpose=True), "qS", 16)
        gytm[q] = c.mark("qS")
    c.bar()
    if debug:
        @blk.gpsimd
        def _(g):
            c.inc(g.dma_start(out=dbg["gy"], in_=GYN[:]), "qG", 16)
        c.bar()

    if STOP < 3:
        es.close()
        return nc, dbg
    opw_mark = {}
    opmm_mark = {}
    opev_mark = {}
    opsq_mark = {}
    opw2_mark = {}
    opcp_mark = {}
    for oc in range(8):
        qsem = "qA" if oc % 2 == 0 else "qB"

        @blk.gpsimd
        def _(g, oc=oc, qsem=qsem):
            if (oc - 2, 7) in opmm_mark:
                c.wait(g, "eT", opmm_mark[(oc - 2, 7)])
            for dt in range(NKT):
                c.inc(g.dma_start(out=OPW[oc % 2][:, 512 * dt:512 * (dt + 1)],
                                  in_=opw[128 * dt:128 * (dt + 1), 512 * oc:512 * (oc + 1)]),
                      qsem, 16)
            opw_mark[oc] = (qsem, c.mark(qsem))

        for q in range(8):
            @blk.scalar
            def _(s, oc=oc, q=q):
                prev = (oc, q - 2) if q >= 2 else (oc - 1, q + 6)
                if prev in opev_mark:
                    c.wait(s, "eV", opev_mark[prev])
                c.inc(s.dma_start(out=xcb2[:, 512 * (q % 2):512 * (q % 2 + 1)],
                                  in_=x1[Q * q:Q * (q + 1), 512 * oc:512 * (oc + 1)]), "qC", 16)
            xldm = c.mark("qC")

            @blk.tensor
            def _(t, oc=oc, q=q):
                if q == 0:
                    c.wait(t, opw_mark[oc][0], opw_mark[oc][1])
                bank = PS[q % 4]
                prevg = (oc, q - 4) if q >= 4 else (oc - 1, q + 4)
                if prevg in opcp_mark:
                    c.wait(t, "eS", opcp_mark[prevg])
                for i in range(NKT):
                    mi = t.matmul(bank[:], LNF[:, 1024 * i + 128 * q:1024 * i + 128 * (q + 1)],
                                  OPW[oc % 2][:, 512 * i:512 * (i + 1)],
                                  start=(i == 0), stop=(i == NKT - 1))
                c.inc(mi, "eT")
                opmm_mark[(oc, q)] = c.mark("eT")

            @blk.scalar
            def _(s, oc=oc, q=q):
                c.wait(s, "eT", opmm_mark[(oc, q)])
                prevslot = (oc, q - 2) if q >= 2 else (oc - 1, q + 6)
                if prevslot in opev_mark:
                    c.wait(s, "eV", opev_mark[prevslot])
                s.copy(out=GY[:, 512 * (q % 2):512 * (q % 2 + 1)], in_=PS[q % 4][:])
                c.inc(s.drain(), "eS")
                opcp_mark[(oc, q)] = c.mark("eS")

            @blk.vector
            def _(v, oc=oc, q=q, xldm=xldm):
                c.wait(v, "eS", opcp_mark[(oc, q)])
                c.wait(v, "qC", xldm)
                prevslot = (oc, q - 2) if q >= 2 else (oc - 1, q + 6)
                if prevslot in opsq_mark:
                    c.wait(v, "eS", opsq_mark[prevslot])
                if prevslot in opw2_mark:
                    c.wait(v, "qW", opw2_mark[prevslot])
                slot = x2b[:, 512 * (q % 2):512 * (q % 2 + 1)]
                v.tensor_tensor(out=slot, in0=GY[:, 512 * (q % 2):512 * (q % 2 + 1)],
                                in1=xcb2[:, 512 * (q % 2):512 * (q % 2 + 1)], op=ALU.add)
                v.drain()
                v.reduce_sum(out=statS3[:, 8 * q + oc:8 * q + oc + 1], in_=slot, axis=AXX)
                c.inc(v.drain(), "eV")
            opev_mark[(oc, q)] = c.mark("eV")

            @blk.scalar
            def _(s, oc=oc, q=q):
                c.wait(s, "eV", opev_mark[(oc, q)])
                s.activation(out=scr[:], in_=x2b[:, 512 * (q % 2):512 * (q % 2 + 1)],
                             func=ACT.Square, accum_out=statS4[:, 8 * q + oc:8 * q + oc + 1])
                c.inc(s.drain(), "eS")
                opsq_mark[(oc, q)] = c.mark("eS")

            @blk.sync
            def _(sy, oc=oc, q=q):
                c.wait(sy, "eV", opev_mark[(oc, q)])
                c.inc(sy.dma_start(out=x2_dram[Q * q:Q * (q + 1), 512 * oc:512 * (oc + 1)],
                                   in_=x2b[:, 512 * (q % 2):512 * (q % 2 + 1)]), "qW", 16)
                opw2_mark[(oc, q)] = c.mark("qW")
    c.bar()
    if debug:
        @blk.gpsimd
        def _(g):
            c.inc(g.dma_start(out=dbg["x2"], in_=x2_dram[:]), "qG", 16)
        c.bar()

    if STOP < 4:
        es.close()
        return nc, dbg

    def emit_colstats2():
        def vsec(vv):
            inv = 1.0 / H
            vv.tensor_scalar(out=colM[:], in0=colS[:], scalar1=inv, scalar2=None, op0=ALU.mult)
            vv.drain()
            vv.tensor_scalar(out=colTa[:], in0=colQ[:], scalar1=inv, scalar2=None, op0=ALU.mult)
            vv.drain()
            vv.tensor_tensor(out=colTb[:], in0=colM[:], in1=colM[:], op=ALU.mult)
            vv.drain()
            vv.tensor_tensor(out=colTa[:], in0=colTa[:], in1=colTb[:], op=ALU.subtract)
            vv.drain()
            vv.tensor_scalar(out=colTb[:], in0=colTa[:], scalar1=EPS, scalar2=None, op0=ALU.add)
            vv.drain()
            vv.reciprocal(out=colTb[:], in_=colTb[:])
            c.inc(vv.drain(), "eV")
        blk.vector(vsec)

        def ssec(ss):
            c.wait(ss, "eV")
            ss.activation(out=colR[:], in_=colTb[:], func=ACT.Sqrt)
            c.inc(ss.drain(), "eS")
        blk.scalar(ssec)

        def vsec3(vv):
            c.wait(vv, "eS")
            vv.tensor_tensor(out=colTa[:], in0=colM[:], in1=colR[:], op=ALU.mult)
            vv.drain()
            vv.tensor_scalar(out=colN[:], in0=colTa[:], scalar1=-1.0, scalar2=None, op0=ALU.mult)
            c.inc(vv.drain(), "eV")
        blk.vector(vsec3)

    for tt in range(8):
        @blk.scalar
        def _(s, tt=tt):
            if tt == 0:
                c.inc(s.dma_start(out=xst2[:], in_=x2_dram[Q * 0:Q * 1, :]), "qC", 16)

        @blk.vector
        def _(v, tt=tt):
            c.wait(v, "qC", c.mark("qC"))
            v.reduce_sum(out=colS[:], in_=statS3[:, 8 * tt:8 * tt + 8], axis=AXX)
            v.drain()
            v.reduce_sum(out=colQ[:], in_=statS4[:, 8 * tt:8 * tt + 8], axis=AXX)
            c.inc(v.drain(), "eV")

        emit_colstats2()

        @blk.vector
        def _(v, tt=tt):
            c.wait(v, "eV")
            c.wait(v, "eS")
            c.wait(v, "qC", c.mark("qC"))
            c.wait(v, "qS", c.mark("qS"))
            v.tensor_scalar(out=lnxtok2[:], in0=xst2[:], scalar1=colR[:], scalar2=colN[:],
                            op0=ALU.mult, op1=ALU.add)
            c.inc(v.drain(), "eV")
        nm = c.mark("eV")

        @blk.scalar
        def _(s, tt=tt, nm=nm):
            c.wait(s, "eV", nm)
            if tt < 7:
                c.inc(s.dma_start(out=xst2[:], in_=x2_dram[Q * (tt + 1):Q * (tt + 2), :]), "qC", 16)

        @blk.sync
        def _(sy, tt=tt, nm=nm):
            c.wait(sy, "eV", nm)
            for i in range(32):
                c.inc(sy.dma_start(out=LNF[:, 1024 * i + 128 * tt:1024 * i + 128 * (tt + 1)],
                                   in_=lnxtok2[:, 128 * i:128 * (i + 1)], transpose=True),
                      "qS", 16)
    c.bar()

    if STOP < 5:
        es.close()
        return nc, dbg

    @blk.tensor
    def _(t):
        for th in range(2):
            pp = p4 if th == 0 else p5
            for k in range(NKT):
                mi = t.matmul(pp[0:R48, :], ATf[:, R48 * k:R48 * (k + 1)],
                              LNF[:, 1024 * k + 512 * th:1024 * k + 512 * (th + 1)],
                              start=(k == 0), stop=(k == NKT - 1))
        c.inc(mi, "eT")

    @blk.scalar
    def _(s):
        c.wait(s, "eT")
        s.copy(out=rmk2[0:R48, 0:512], in_=p4[0:R48, :])
        s.copy(out=rmk2[0:R48, 512:1024], in_=p5[0:R48, :])
        c.inc(s.drain(), "eS")
    c.bar()

    fev_mark = {}
    fmm_mark = {}
    fout_mark = {}
    fcp_mark = {}
    fw_mark = {}
    for oc in range(8):
        qsem = "qA" if oc % 2 == 0 else "qB"

        @blk.gpsimd
        def _(g, oc=oc, qsem=qsem):
            if (oc - 2, 7) in fmm_mark:
                c.wait(g, "eT", fmm_mark[(oc - 2, 7)])
            for k in range(NKT):
                c.inc(g.dma_start(out=WFF[oc % 2][:, 512 * k:512 * (k + 1)],
                                  in_=ffn_wT[128 * k:128 * (k + 1), 512 * oc:512 * (oc + 1)]),
                      qsem, 16)
            fw_mark[oc] = (qsem, c.mark(qsem))

        for tt in range(8):
            @blk.scalar
            def _(s, oc=oc, tt=tt):
                prev = (oc, tt - 2) if tt >= 2 else (oc - 1, tt + 6)
                if prev in fev_mark:
                    c.wait(s, "eV", fev_mark[prev])
                c.inc(s.dma_start(out=xcb2[:, 512 * (tt % 2):512 * (tt % 2 + 1)],
                                  in_=x2_dram[Q * tt:Q * (tt + 1), 512 * oc:512 * (oc + 1)]),
                      "qC", 16)
            xldm = c.mark("qC")

            @blk.tensor
            def _(t, oc=oc, tt=tt):
                if tt == 0:
                    c.wait(t, fw_mark[oc][0], fw_mark[oc][1])
                bank = PS[tt % 4]
                prevg = (oc, tt - 4) if tt >= 4 else (oc - 1, tt + 4)
                if prevg in fcp_mark:
                    c.wait(t, "eS", fcp_mark[prevg])
                for k in range(NKT):
                    t.matmul(bank[:], LNF[:, 1024 * k + 128 * tt:1024 * k + 128 * (tt + 1)],
                             WFF[oc % 2][:, 512 * k:512 * (k + 1)],
                             start=(k == 0), stop=False)
                c.inc(t.matmul(bank[:], rmk2[0:49, 128 * tt:128 * (tt + 1)],
                               BTFB[0:49, 512 * oc:512 * (oc + 1)], start=False, stop=True), "eT")
                fmm_mark[(oc, tt)] = c.mark("eT")

            @blk.scalar
            def _(s, oc=oc, tt=tt):
                c.wait(s, "eT", fmm_mark[(oc, tt)])
                prevslot = (oc, tt - 2) if tt >= 2 else (oc - 1, tt + 6)
                if prevslot in fev_mark:
                    c.wait(s, "eV", fev_mark[prevslot])
                s.activation(out=GY[:, 512 * (tt % 2):512 * (tt % 2 + 1)], in_=PS[tt % 4][:],
                             func=ACT.Identity, scale=halfcol[:])
                c.inc(s.drain(), "eS")
                fcp_mark[(oc, tt)] = c.mark("eS")

            @blk.vector
            def _(v, oc=oc, tt=tt, xldm=xldm):
                c.wait(v, "eS", fcp_mark[(oc, tt)])
                c.wait(v, "qC", xldm)
                prevslot = (oc, tt - 2) if tt >= 2 else (oc - 1, tt + 6)
                if prevslot in fout_mark:
                    c.wait(v, "qW", fout_mark[prevslot])
                slot = x2b[:, 512 * (tt % 2):512 * (tt % 2 + 1)]
                v.tensor_tensor(out=slot, in0=GY[:, 512 * (tt % 2):512 * (tt % 2 + 1)],
                                in1=xcb2[:, 512 * (tt % 2):512 * (tt % 2 + 1)], op=ALU.add)
                c.inc(v.drain(), "eV")
            fev_mark[(oc, tt)] = c.mark("eV")

            @blk.sync
            def _(sy, oc=oc, tt=tt):
                c.wait(sy, "eV", fev_mark[(oc, tt)])
                c.inc(sy.dma_start(out=o_out[Q * tt:Q * (tt + 1), 512 * oc:512 * (oc + 1)],
                                   in_=x2b[:, 512 * (tt % 2):512 * (tt % 2 + 1)]), "qW", 16)
                fout_mark[(oc, tt)] = c.mark("qW")
    c.bar()

    if debug:
        @blk.gpsimd
        def _(g):
            c.inc(g.dma_start(out=dbg["wff"], in_=WB[:]), "qG", 16)
            c.inc(g.dma_start(out=dbg["btfb"], in_=BTFB[:]), "qG", 16)
            c.inc(g.dma_start(out=dbg["lnf"], in_=LNF[:, 0:1024]), "qG", 16)
            c.inc(g.dma_start(out=dbg["rmk"], in_=rmk2[:]), "qG", 16)
        c.bar()

    @blk.gpsimd
    def _(g):
        c.wait(g, "qG")
        c.wait(g, "qW")
    c.bar()

    es.close()
    return nc, dbg


BF = ml_dtypes.bfloat16
H = 4096; T = 1024; PROJ = 8512; CONV = 4352; NH = 64; NS = 128


def l1_inputs(inp, core):
    x = np.asarray(inp["L0_attn_hidden_states"][0])
    xpad = np.concatenate([np.zeros((3, H), np.float32), x], 0)
    lo = T * core
    m = {}
    m["xin"] = np.ascontiguousarray(xpad[lo:lo + T + 3])
    m["halom"] = np.full((128, 4), 0.0 if core == 0 else 1.0, np.float32)
    m["ffn_wT"] = np.ascontiguousarray(np.asarray(inp["L0_ffn_weight"]).T)
    m["ffn_b"] = np.asarray(inp["L0_ffn_bias"]).reshape(1, H)
    m["lora_A"] = np.asarray(inp["L0_ffn_lora_A"]).reshape(48, H)
    m["lora_B"] = np.asarray(inp["L0_ffn_lora_B"]).reshape(3 * H, 16)
    m["lsi"] = np.asarray(inp["L0_ffn_lora_start_indices"]).reshape(1, 4).astype(np.int32)
    m["ipw"] = np.asarray(inp["L0_attn_in_proj_w"])
    m["conv_w"] = np.asarray(inp["L0_attn_conv_w"])
    m["a_log"] = np.asarray(inp["L0_attn_A_log"]).reshape(1, NH)
    m["dt_bias"] = np.asarray(inp["L0_attn_dt_bias"]).reshape(1, NH)
    m["d_in"] = np.asarray(inp["L0_attn_D"]).reshape(1, NH)
    mup = np.zeros((1, PROJ + 64), np.float32)
    mup[0, :PROJ] = np.asarray(inp["L0_attn_mup_vector"])
    m["mup_i"] = mup
    m["ln_w"] = np.asarray(inp["L0_attn_ln_w"]).reshape(32, 128)
    return m


def l2_inputs(inp, core, l1res):
    r = l1res[core]
    m = {}
    m["x1"] = np.asarray(r["o_x1"])
    m["g1"] = np.asarray(r["o_g1"])
    m["sz"] = np.asarray(r["o_sztok"])
    m["Cbf"] = np.asarray(r["o_C"])
    m["cum"] = np.asarray(r["o_cum"])
    m["ctcore"] = np.stack([np.asarray(l1res[j]["o_ctot"])[7] for j in range(8)])
    m["hloc"] = np.concatenate([np.asarray(l1res[j]["o_hloc"]) for j in range(8)], 0)
    m["prefw"] = np.tile((np.arange(8) < core).astype(np.float32), (64, 1))
    m["norm_w"] = np.asarray(inp["L0_attn_norm_w"]).reshape(1, H)
    m["opw"] = np.asarray(inp["L0_attn_out_proj_w"])
    m["ffn_wT"] = np.ascontiguousarray(np.asarray(inp["L0_ffn_weight"]).T)
    m["ffn_b"] = np.asarray(inp["L0_ffn_bias"]).reshape(1, H)
    m["lora_A"] = np.asarray(inp["L0_ffn_lora_A"]).reshape(48, H)
    m["lora_B"] = np.asarray(inp["L0_ffn_lora_B"]).reshape(3 * H, 16)
    m["lsi"] = np.asarray(inp["L0_ffn_lora_start_indices"]).reshape(1, 4).astype(np.int32)
    return m


_BUILT = {}


def kernel(**inputs):
    if "nc1" not in _BUILT:
        _BUILT["nc1"], _ = build_l1(debug=0, upto=9)
        _BUILT["nc2"], _ = build_l2(debug=0)
    nc1 = _BUILT["nc1"]
    nc2 = _BUILT["nc2"]
    in1 = [l1_inputs(inputs, c) for c in range(8)]
    res1 = run_bass_kernel_spmd(nc1, in1, core_ids=list(range(8)))
    l1res = [{k: np.asarray(v) for k, v in res1.results[c].items()} for c in range(8)]
    in2 = [l2_inputs(inputs, c, l1res) for c in range(8)]
    res2 = run_bass_kernel_spmd(nc2, in2, core_ids=list(range(8)))
    out = np.concatenate([np.asarray(res2.results[c]["o_out"], np.float32) for c in range(8)], 0)
    return out.reshape(1, 8192, 4096)


# version 05
# speedup vs baseline: 1.1362x; 1.0047x over previous
import numpy as np
import ml_dtypes
import concourse.bass as bass
import concourse.mybir as mybir
from contextlib import ExitStack
from concourse.bass_utils import run_bass_kernel_spmd

F32 = mybir.dt.float32
BF16 = mybir.dt.bfloat16
I32 = mybir.dt.int32
ALU = mybir.AluOpType
ACT = mybir.ActivationFunctionType
AXX = mybir.AxisListType.X

H = 4096
T = 1024
LW = 1056
PROJ = 8512
CONV = 4352
NS = 128
NH = 64
Q = 128
NCH = 8
NKT = 32
EPS = 1e-5
NEG = -1e30
R48 = 48


class Ctx:
    def __init__(self, nc, blk):
        self.nc = nc
        self.blk = blk
        self.sems = {}

    def inc(self, instr, name, k=1):
        s = self.sems[name]
        instr.then_inc(s[0], k)
        s[1] += k
        return instr

    def mark(self, name):
        return self.sems[name][1]

    def wait(self, eng, name, val=None):
        s = self.sems[name]
        v = s[1] if val is None else val
        if v > 0:
            eng.wait_ge(s[0], v)

    def bar(self):
        names = list(self.sems)
        for sec in (self.blk.tensor, self.blk.vector, self.blk.scalar,
                    self.blk.gpsimd, self.blk.sync):
            def f(eng):
                for n in names:
                    self.wait(eng, n)
            sec(f)


def make_ident(g, ap, n=128):
    g.memset(ap, 0.0)
    g.affine_select(out=ap, in_=ap, compare_op=ALU.not_equal, fill=1.0,
                    base=0, pattern=[[-1, n]], channel_multiplier=1)


def make_maskbias(g, ap, n=128):
    g.memset(ap, 0.0)
    g.affine_select(out=ap, in_=ap, compare_op=ALU.is_ge, fill=NEG,
                    base=0, pattern=[[1, n]], channel_multiplier=-1)


def stats_cols(v, s, mcol, rcol, nmr, sumc, sqc, n_feat, rms_fused, ta, tb):
    inv = 1.0 / n_feat
    v.tensor_scalar(out=mcol, in0=sumc, scalar1=inv, scalar2=None, op0=ALU.mult)
    v.tensor_scalar(out=ta, in0=sqc, scalar1=inv, scalar2=None, op0=ALU.mult)
    v.tensor_tensor(out=tb, in0=mcol, in1=mcol, op=ALU.mult)
    v.tensor_tensor(out=ta, in0=ta, in1=tb, op=ALU.subtract)
    v.tensor_scalar(out=tb, in0=ta, scalar1=EPS, scalar2=None, op0=ALU.add)
    v.reciprocal(out=tb, in_=tb)
    s.activation(out=rcol, in_=tb, func=ACT.Sqrt)
    if rms_fused:
        v.tensor_tensor(out=ta, in0=ta, in1=tb, op=ALU.mult)
        v.tensor_scalar(out=ta, in0=ta, scalar1=EPS, scalar2=None, op0=ALU.add)
        v.reciprocal(out=ta, in_=ta)
        s.activation(out=ta, in_=ta, func=ACT.Sqrt)
        v.tensor_tensor(out=rcol, in0=rcol, in1=ta, op=ALU.mult)
    v.tensor_tensor(out=ta, in0=mcol, in1=rcol, op=ALU.mult)
    v.tensor_scalar(out=nmr, in0=ta, scalar1=-1.0, scalar2=None, op0=ALU.mult)


def build_l1(debug=0, upto=9):
    nc = bass.Bass("TRN2", target_bir_lowering=False)
    es = ExitStack()

    def din(name, shape, dt=F32):
        return nc.dram_tensor(name, shape, dt, kind="ExternalInput").ap()

    def dout(name, shape, dt=F32):
        return nc.dram_tensor(name, shape, dt, kind="ExternalOutput").ap()

    def sb(name, p, f, dt):
        return es.enter_context(nc.sbuf_tensor(name, [p, f], dt))

    def ps(name, p, f, dt=F32):
        return es.enter_context(nc.psum_tensor(name, [p, f], dt))

    xin = din("xin", [T + 3, H])
    halom = din("halom", [128, 4])
    ffn_wT = din("ffn_wT", [H, H])
    ffn_b = din("ffn_b", [1, H])
    lora_A = din("lora_A", [R48, H])
    lora_B = din("lora_B", [3 * H, 16])
    lsi = din("lsi", [1, 4], I32)
    ipw = din("ipw", [H, PROJ])
    conv_w = din("conv_w", [CONV, 4])
    a_log = din("a_log", [1, NH])
    dt_bias = din("dt_bias", [1, NH])
    d_in = din("d_in", [1, NH])
    mup_i = din("mup_i", [1, PROJ + 64])
    ln_w = din("ln_w", [32, 128])

    o_x1 = dout("o_x1", [T, H])
    o_siluz = dout("o_siluz", [H, T], BF16)
    o_g1 = dout("o_g1", [T, H], BF16)
    o_sztok = dout("o_sztok", [T, H], BF16)
    o_C = dout("o_C", [NS, T], BF16)
    o_cum = dout("o_cum", [T, NH])
    o_ctot = dout("o_ctot", [NCH, NH])
    o_hloc = dout("o_hloc", [NS, H])
    xs_dram = nc.dram_tensor("xs_dram", [H, T], BF16, kind="Internal").ap()
    flat_dram = nc.dram_tensor("flat_dram", [NH, 8, Q], F32, kind="Internal").ap()
    x1h_dram = nc.dram_tensor("x1h_dram", [3, H], F32, kind="Internal").ap()
    dbg = {}
    if debug:
        dbg["lnx"] = dout("d_lnx", [32, 128, LW], BF16)
        dbg["xst"] = dout("d_xst", [128, 4096])
        dbg["lnxtok"] = dout("d_lnxtok", [128, 4096], BF16)
        dbg["stats"] = dout("d_stats", [128, 8])
        dbg["s2"] = dout("d_s2", [128, 64])
        dbg["scr4"] = dout("d_scr4", [128, 4])
        dbg["xs"] = dout("d_xs", [H, T], BF16)
        dbg["B"] = dout("d_B", [NS, T], BF16)
        dbg["dt"] = dout("d_dt", [NH, T])
        dbg["x1h"] = dout("d_x1h", [3, H])
        dbg["zz"] = dout("d_zz", [128, T], BF16)
        dbg["wff"] = dout("d_wff", [128, 16384], BF16)
        dbg["rmk"] = dout("d_rmk", [64, LW], BF16)

    LNX = sb("LNX", 128, 32 * LW, BF16)
    A2 = sb("A2", 128, 40960, BF16)
    lnxtok = sb("lnxtok", 128, H, BF16)
    Bfm = sb("Bfm", 128, T, BF16)
    Cfm = sb("Cfm", 128, T, BF16)
    dtf = sb("dtf", 64, T, F32)
    cumb = sb("cumb", 64, Q, F32)
    sch = sb("sch", 64, Q, F32)
    ones64 = sb("ones64", 64, Q, F32)
    cumT = sb("cumT", 128, 64, F32)
    ncumT = sb("ncumT", 128, 64, F32)
    dtT = sb("dtT", 128, 64, F32)
    dend = sb("dend", 128, 64, F32)
    EEs = sb("EEs", 128, 64, F32)
    de64 = sb("de64", 64, Q, F32)
    idb = sb("idb", 128, 128, BF16)
    idf = sb("idf", 128, 128, F32)
    mkb4 = sb("mkb4", 128, 512, BF16)
    onescf = sb("onescf", 1, 128, F32)
    ones128b = sb("ones128b", 128, 1, BF16)
    halfcol = sb("halfcol", 128, 1, F32)
    mupc = sb("mupc", 128, 67, F32)
    lnwc = sb("lnwc", 128, 32, F32)
    dtbias_c = sb("dtbias_c", 64, 1, F32)
    nA_c = sb("nA_c", 64, 1, F32)
    Dcol = sb("Dcol", 128, 32, F32)
    ATf = sb("ATf", 128, 32 * R48, BF16)
    BTFB = sb("BTFB", 128, H, BF16)
    BTf = BTFB[0:R48, :]
    biasrow = BTFB[48:49, :]
    rmk = sb("rmk", 64, LW, BF16)

    statS1 = sb("statS1", 128, 64, F32)
    statS2 = sb("statS2", 128, 64, F32)
    statS3 = sb("statS3", 128, 64, F32)
    statS4 = sb("statS4", 128, 64, F32)
    hs3 = sb("hs3", 3, 8, F32)
    hs4 = sb("hs4", 3, 8, F32)
    hs1 = sb("hs1", 3, 8, F32)
    hs2 = sb("hs2", 3, 8, F32)
    colM = sb("colM", 128, 1, F32)
    colR = sb("colR", 128, 1, F32)
    colN = sb("colN", 128, 1, F32)
    colTa = sb("colTa", 128, 1, F32)
    colTb = sb("colTb", 128, 1, F32)
    scr = sb("scr", 128, 512, F32)
    x1b = sb("x1b", 128, 1024, F32)
    hcolM = sb("hcolM", 3, 1, F32)
    hcolR = sb("hcolR", 3, 1, F32)
    hcolN = sb("hcolN", 3, 1, F32)
    hcolTa = sb("hcolTa", 3, 1, F32)
    hcolTb = sb("hcolTb", 3, 1, F32)
    hscr = sb("hscr", 3, 512, F32)
    halom_s = sb("halom_s", 128, 4, F32)
    xbctmp = sb("xbctmp", 128, 1056, BF16)
    convw = sb("convw", 128, 136, F32)
    xstmp = sb("xstmp", 128, T, BF16)
    lsis = sb("lsis", 1, 4, I32)
    lsif = sb("lsif", 1, 4, F32)
    cnt1 = sb("cnt1", 1, 1, F32)
    cst13 = sb("cst13", 1, 3, F32)
    msk13 = sb("msk13", 1, 3, F32)
    prm = sb("prm", 128, 128, F32)
    prm2 = sb("prm2", 128, 128, F32)
    prm3 = sb("prm3", 1, 192, F32)
    ctotb = sb("ctotb", 64, 1, F32)
    colS = sb("colS", 128, 1, F32)
    colQ = sb("colQ", 128, 1, F32)
    onescb = sb("onescb", 1, 128, BF16)
    xcb = sb("xcb", 128, 1024, F32)
    xbc2tmp = sb("xbc2tmp", 128, 1056, BF16)
    h512 = sb("h512", 3, 512, F32)
    ecrow = sb("ecrow", 1, 64, F32)
    ECtok = sb("ECtok", 128, 64, F32)
    Dtok = sb("Dtok", 128, 64, F32)
    Btok = sb("Btok", 128, 128, BF16)
    ocumb = sb("ocumb", 128, 64, F32)

    WFF = [A2[:, 0:16384], A2[:, 16384:32768]]
    OTB = [A2[:, 32768:36864], A2[:, 36864:40960]]
    WP = [A2[:, 0:8192], A2[:, 8192:16384]]
    DIAG = A2[:, 16384:33792]
    LAB = A2[:, 0:4096]
    LBB = A2[:, 4096:5632]
    XCH = [A2[:, 0:4096], A2[:, 4096:8192]]
    XTOK = A2[:, 8192:12288]
    XPP = A2[:, 12288:16384]
    G1CH = A2[:, 16384:20480]
    HTf = A2[:, 20480:28672].bitcast(F32)
    HTb = A2[:, 28672:32768]
    HTs = A2[:, 32768:36864]
    xst = A2[:, 0:8192].bitcast(F32)
    hlnb1 = A2[:, 8192:12288][0:3, :]
    x1hv = A2[:, 8192:16384].bitcast(F32)[0:3, :]
    hlnb3 = A2[:, 34304:38400][0:3, :]
    hzx = A2[:, 33792:38144][0:3, :]
    xhb1 = A2[:, 12288:20480].bitcast(F32)[0:3, :]
    x1hbuf = A2[:, 8192:16384].bitcast(F32)[0:3, :]
    dtraw = A2[:, 38144:40192].bitcast(F32)[0:64, 0:1024]
    G4 = A2[:, 36864:37376]
    C4 = A2[:, 37376:37888]
    E4 = A2[:, 37888:38400]
    W4 = A2[:, 38400:38912]
    Cw4 = A2[:, 38912:39424]
    btmp = A2[:, 5632:6656][0:16, :]
    dtabs = x1b[0:64, :]
    cumflat = LNX[:, 0:16384].bitcast(F32)[0:1, :]
    ecumflat = LNX[:, 16384:32768].bitcast(F32)[0:1, :]
    cumall = LNX[:, 16384:20480].bitcast(F32)[0:64, 0:1024]

    p0 = ps("p0", 128, 512)
    p1 = ps("p1", 128, 512)
    p2 = ps("p2", 128, 512)
    p3 = ps("p3", 128, 512)
    p4 = ps("p4", 128, 512)
    p5 = ps("p5", 128, 512)
    p6 = ps("p6", 128, 512)
    pT = ps("pT", 128, 1024, BF16)
    PS = [p0, p1, p2, p3]

    blk = es.enter_context(nc.Block())
    c = Ctx(nc, blk)
    for s in ["qS", "qC", "qG", "qW", "qA", "qB", "eT", "eV", "eS", "eG"]:
        c.sems[s] = [es.enter_context(nc.semaphore(s)), 0]

    @blk.gpsimd
    def _(g):
        make_ident(g, idb[:])
        make_ident(g, idf[:])
        make_maskbias(g, mkb4[:, 0:128])
        g.memset(cst13[0:1, 0:1], 1.0)
        g.memset(cst13[0:1, 1:2], 2.0)
        c.inc(g.memset(cst13[0:1, 2:3], 3.0), "eG")
        c.inc(g.memset(ones64[:], 1.0), "eG")
        c.inc(g.dma_start(out=halom_s[:], in_=halom), "qG", 16)
        c.inc(g.dma_start(out=lsis[:], in_=lsi), "qG", 16)
        c.inc(g.dma_start(out=biasrow, in_=ffn_b), "qG", 16)
        c.inc(g.dma_start(out=LAB[0:R48, :], in_=lora_A), "qG", 16)
        c.inc(g.dma_start(out=LBB[:, 0:1536].rearrange("p (a r) -> p a r", r=16),
                          in_=lora_B.rearrange("(a p) r -> p a r", p=128)), "qG", 16)
        c.inc(g.dma_start(out=prm[0:67, :], in_=mup_i.rearrange("o (a b) -> (o a) b", b=128)), "qG", 16)
        c.inc(g.dma_start(out=prm2[0:32, :], in_=ln_w), "qG", 16)
        c.inc(g.dma_start(out=prm3[0:1, 0:64], in_=dt_bias), "qG", 16)
        c.inc(g.dma_start(out=prm3[0:1, 64:128], in_=a_log), "qG", 16)
        c.inc(g.dma_start(out=prm3[0:1, 128:192], in_=d_in), "qG", 16)
        c.inc(g.dma_start(out=convw[:].rearrange("p (a k) -> p a k", k=4),
                          in_=conv_w.rearrange("(a p) k -> p a k", p=128)), "qG", 16)

    @blk.vector
    def _(v):
        v.memset(onescb[:], 1.0)
        v.memset(rmk[32:64, :], 1.0)
        v.memset(statS2[:], 0.0)
        v.memset(statS4[:], 0.0)
        v.memset(hs2[:], 0.0)
        v.memset(hs4[:], 0.0)
        v.memset(onescf[:], 1.0)
        v.memset(ones128b[:], 1.0)
        v.memset(halfcol[:], 0.5)
        c.wait(v, "eG")
        v.tensor_copy(out=mkb4[:, 128:256], in_=mkb4[:, 0:128])
        v.tensor_copy(out=mkb4[:, 256:384], in_=mkb4[:, 0:128])
        v.tensor_copy(out=mkb4[:, 384:512], in_=mkb4[:, 0:128])
        c.wait(v, "qG")
        v.tensor_copy(out=lsif[:], in_=lsis[:])
        v.tensor_single_scalar(out=lsif[:], in_=lsif[:], scalar=0.0, op=ALU.is_le)
        v.reduce_sum(out=cnt1[:], in_=lsif[:], axis=AXX)
        v.tensor_scalar(out=cnt1[:], in0=cnt1[:], scalar1=-1.0, scalar2=None, op0=ALU.add)
        c.inc(v.tensor_scalar(out=msk13[:], in0=cst13[:], scalar1=cnt1[:], scalar2=None,
                              op0=ALU.is_equal), "eV")
    c.bar()

    @blk.tensor
    def _(t):
        c.inc(t.transpose(p6[:, 0:67], prm[0:67, :], idf[0:67, 0:67]), "eT")
        c.inc(t.transpose(p6[:, 67:99], prm2[0:32, :], idf[0:32, 0:32]), "eT")
        c.inc(t.transpose(p6[0:64, 99:100], prm3[0:1, 0:64], idf[0:1, 0:1]), "eT")
        c.inc(t.transpose(p6[0:64, 100:101], prm3[0:1, 64:128], idf[0:1, 0:1]), "eT")
        c.inc(t.transpose(p6[0:3, 102:103], msk13[:], idf[0:1, 0:1]), "eT")
        for i in range(16):
            c.inc(t.transpose(pT[:, 48 * i:48 * (i + 1)], LAB[0:R48, 128 * i:128 * (i + 1)],
                              idb[0:R48, 0:R48]), "eT")
    c.bar()

    @blk.vector
    def _(v):
        v.tensor_copy(out=mupc[:], in_=p6[:, 0:67])
        v.tensor_copy(out=lnwc[:], in_=p6[:, 67:99])
        v.tensor_copy(out=dtbias_c[:], in_=p6[0:64, 99:100])
        c.inc(v.tensor_copy(out=ATf[:, 0:768], in_=pT[:, 0:768]), "eV")

    @blk.scalar
    def _(s):
        s.activation(out=nA_c[:], in_=p6[0:64, 100:101], func=ACT.Exp)
        c.inc(s.activation(out=nA_c[:], in_=nA_c[:], func=ACT.Copy, scale=-1.0), "eS")
    c.bar()

    @blk.tensor
    def _(t):
        for i in range(16):
            c.inc(t.transpose(pT[:, 48 * i:48 * (i + 1)], LAB[0:R48, 128 * (16 + i):128 * (17 + i)],
                              idb[0:R48, 0:R48]), "eT")
        c.inc(t.matmul(p6[:, 103:106], onescf[:], msk13[:], start=True, stop=True), "eT")
        c.inc(t.matmul(p6[:, 106:170], onescf[:], prm3[0:1, 128:192], start=True, stop=True), "eT")
    c.bar()

    @blk.vector
    def _(v):
        c.inc(v.tensor_copy(out=ATf[:, 768:1536], in_=pT[:, 0:768]), "eV")
        for i in range(32):
            for a in range(3):
                c.inc(v.tensor_scalar(out=ATf[:, 48 * i + 16 * a:48 * i + 16 * (a + 1)],
                                      in0=ATf[:, 48 * i + 16 * a:48 * i + 16 * (a + 1)],
                                      scalar1=p6[:, 103 + a:104 + a], scalar2=None,
                                      op0=ALU.mult), "eV")
        dr = p6[:, 106:170]
        c.inc(v.tensor_copy(out=Dcol[0:64, :], in_=dr[0:64, :].rearrange("p (i two) -> p i two", two=2)[:, :, 0:1].squeeze(2)), "eV")
        c.inc(v.tensor_copy(out=Dcol[64:128, :], in_=dr[64:128, :].rearrange("p (i two) -> p i two", two=2)[:, :, 1:2].squeeze(2)), "eV")
        for c34 in range(34):
            c.inc(v.tensor_scalar(out=convw[:, 4 * c34:4 * (c34 + 1)],
                                  in0=convw[:, 4 * c34:4 * (c34 + 1)],
                                  scalar1=mupc[:, 32 + c34:33 + c34], scalar2=None,
                                  op0=ALU.mult), "eV")
    c.bar()

    for a in range(3):
        for w in range(4):
            @blk.tensor
            def _(t, a=a, w=w):
                for j in range(8):
                    i = 8 * w + j
                    c.inc(t.transpose(pT[0:16, 128 * j:128 * (j + 1)],
                                      LBB[:, 16 * (32 * a + i):16 * (32 * a + i + 1)],
                                      idb[:]), "eT")
            c.bar()

            @blk.vector
            def _(v, a=a, w=w):
                c.inc(v.tensor_copy(out=btmp, in_=pT[0:16, 0:1024]), "eV")
            c.bar()

            @blk.gpsimd
            def _(g, a=a, w=w):
                c.inc(g.dma_start(out=BTFB[16 * a:16 * (a + 1), 1024 * w:1024 * (w + 1)],
                                  in_=btmp), "qG", 16)
            c.bar()

    if upto < 1:
        es.close()
        return nc, dbg

    def emit_colstats(sumc, sqc, rms_fused=False):
        def vsec(vv):
            inv = 1.0 / H
            vv.tensor_scalar(out=colM[:], in0=sumc, scalar1=inv, scalar2=None, op0=ALU.mult)
            vv.drain()
            vv.tensor_scalar(out=colTa[:], in0=sqc, scalar1=inv, scalar2=None, op0=ALU.mult)
            vv.drain()
            vv.tensor_tensor(out=colTb[:], in0=colM[:], in1=colM[:], op=ALU.mult)
            vv.drain()
            vv.tensor_tensor(out=colTa[:], in0=colTa[:], in1=colTb[:], op=ALU.subtract)
            vv.drain()
            vv.tensor_scalar(out=colTb[:], in0=colTa[:], scalar1=EPS, scalar2=None, op0=ALU.add)
            vv.drain()
            vv.reciprocal(out=colTb[:], in_=colTb[:])
            c.inc(vv.drain(), "eV")
        blk.vector(vsec)

        def ssec(ss):
            c.wait(ss, "eV")
            ss.activation(out=colR[:], in_=colTb[:], func=ACT.Sqrt)
            c.inc(ss.drain(), "eS")
        blk.scalar(ssec)

        if rms_fused:
            def vsec2(vv):
                c.wait(vv, "eS")
                vv.tensor_tensor(out=colTa[:], in0=colTa[:], in1=colTb[:], op=ALU.mult)
                vv.drain()
                vv.tensor_scalar(out=colTa[:], in0=colTa[:], scalar1=EPS, scalar2=None, op0=ALU.add)
                vv.drain()
                vv.reciprocal(out=colTa[:], in_=colTa[:])
                c.inc(vv.drain(), "eV")
            blk.vector(vsec2)

            def ssec2(ss):
                c.wait(ss, "eV")
                ss.activation(out=colTa[:], in_=colTa[:], func=ACT.Sqrt)
                c.inc(ss.drain(), "eS")
            blk.scalar(ssec2)

            def vsec2b(vv):
                c.wait(vv, "eS")
                vv.tensor_tensor(out=colR[:], in0=colR[:], in1=colTa[:], op=ALU.mult)
                c.inc(vv.drain(), "eV")
            blk.vector(vsec2b)

        def vsec3(vv):
            c.wait(vv, "eS")
            c.wait(vv, "eV")
            vv.tensor_tensor(out=colTa[:], in0=colM[:], in1=colR[:], op=ALU.mult)
            vv.drain()
            vv.tensor_scalar(out=colN[:], in0=colTa[:], scalar1=-1.0, scalar2=None, op0=ALU.mult)
            c.inc(vv.drain(), "eV")
        blk.vector(vsec3)

    xstbufs = [xst, A2[:, 8192:16384].bitcast(F32)]

    def emit_ln_pass(src_dma, rms_fused, use_partials):
        ldm = {}
        nmx = {}
        for tt in range(8):
            xb = xstbufs[tt % 2]

            @blk.scalar
            def _(s, tt=tt, xb=xb):
                if tt == 0:
                    c.inc(s.dma_start(out=xstbufs[0], in_=src_dma(0)), "qC", 16)
                    ldm[0] = c.mark("qC")
                    c.inc(s.dma_start(out=xstbufs[1], in_=src_dma(1)), "qC", 16)
                    ldm[1] = c.mark("qC")
                elif tt < 7:
                    c.wait(s, "eV", nmx[tt - 1])
                    c.inc(s.dma_start(out=xstbufs[(tt + 1) % 2], in_=src_dma(tt + 1)), "qC", 16)
                    ldm[tt + 1] = c.mark("qC")
                if not use_partials:
                    c.wait(s, "qC", ldm[tt])
                    for j in range(8):
                        s.activation(out=scr[:], in_=xb[:, 512 * j:512 * (j + 1)], func=ACT.Square,
                                     accum_out=statS2[:, 8 * tt + j:8 * tt + j + 1])
                    c.inc(s.drain(), "eS")

            @blk.vector
            def _(v, tt=tt, xb=xb):
                if use_partials:
                    v.reduce_sum(out=colS[:], in_=statS3[:, 8 * tt:8 * tt + 8], axis=AXX)
                    v.drain()
                    v.reduce_sum(out=colQ[:], in_=statS4[:, 8 * tt:8 * tt + 8], axis=AXX)
                    c.inc(v.drain(), "eV")
                else:
                    c.wait(v, "qC", ldm[tt])
                    v.reduce_sum(out=colS[:], in_=xb, axis=AXX)
                    v.drain()
                    c.wait(v, "eS")
                    v.reduce_sum(out=colQ[:], in_=statS2[:, 8 * tt:8 * tt + 8], axis=AXX)
                    c.inc(v.drain(), "eV")

            emit_colstats(colS[:], colQ[:], rms_fused)

            @blk.vector
            def _(v, tt=tt, xb=xb):
                c.wait(v, "eV")
                c.wait(v, "eS")
                c.wait(v, "qC", ldm[tt])
                c.wait(v, "qS", c.mark("qS"))
                v.tensor_scalar(out=lnxtok[:], in0=xb, scalar1=colR[:], scalar2=colN[:],
                                op0=ALU.mult, op1=ALU.add)
                c.inc(v.drain(), "eV")
            nmx[tt] = c.mark("eV")

            @blk.sync
            def _(sy, tt=tt):
                c.wait(sy, "eV", nmx[tt])
                for i in range(32):
                    c.inc(sy.dma_start(out=LNX[:, LW * i + 128 * tt:LW * i + 128 * (tt + 1)],
                                       in_=lnxtok[:, 128 * i:128 * (i + 1)], transpose=True),
                          "qS", 16)

    def emit_halo_ln(xsrc_ap, dst_bf, rms_fused, sq_buf):
        @blk.scalar
        def _(s):
            for j in range(8):
                s.activation(out=hscr[:], in_=xsrc_ap[:, 512 * j:512 * (j + 1)], func=ACT.Square,
                             accum_out=sq_buf[:, j:j + 1])
            c.inc(s.drain(), "eS")

        @blk.vector
        def _(v):
            c.wait(v, "eS")
            v.reduce_sum(out=hcolM[:], in_=xsrc_ap, axis=AXX)
            v.drain()
            v.reduce_sum(out=hcolTb[:], in_=sq_buf[:], axis=AXX)
            v.drain()
            inv = 1.0 / H
            v.tensor_scalar(out=hcolM[:], in0=hcolM[:], scalar1=inv, scalar2=None, op0=ALU.mult)
            v.drain()
            v.tensor_scalar(out=hcolTa[:], in0=hcolTb[:], scalar1=inv, scalar2=None, op0=ALU.mult)
            v.drain()
            v.tensor_tensor(out=hcolTb[:], in0=hcolM[:], in1=hcolM[:], op=ALU.mult)
            v.drain()
            v.tensor_tensor(out=hcolTa[:], in0=hcolTa[:], in1=hcolTb[:], op=ALU.subtract)
            v.drain()
            v.tensor_scalar(out=hcolTb[:], in0=hcolTa[:], scalar1=EPS, scalar2=None, op0=ALU.add)
            v.drain()
            v.reciprocal(out=hcolTb[:], in_=hcolTb[:])
            c.inc(v.drain(), "eV")

        @blk.scalar
        def _(s):
            c.wait(s, "eV")
            s.activation(out=hcolR[:], in_=hcolTb[:], func=ACT.Sqrt)
            c.inc(s.drain(), "eS")

        if rms_fused:
            @blk.vector
            def _(v):
                c.wait(v, "eS")
                v.tensor_tensor(out=hcolTa[:], in0=hcolTa[:], in1=hcolTb[:], op=ALU.mult)
                v.drain()
                v.tensor_scalar(out=hcolTa[:], in0=hcolTa[:], scalar1=EPS, scalar2=None, op0=ALU.add)
                v.drain()
                v.reciprocal(out=hcolTa[:], in_=hcolTa[:])
                c.inc(v.drain(), "eV")

            @blk.scalar
            def _(s):
                c.wait(s, "eV")
                s.activation(out=hcolTa[:], in_=hcolTa[:], func=ACT.Sqrt)
                c.inc(s.drain(), "eS")

            @blk.vector
            def _(v):
                c.wait(v, "eS")
                v.tensor_tensor(out=hcolR[:], in0=hcolR[:], in1=hcolTa[:], op=ALU.mult)
                c.inc(v.drain(), "eV")

        @blk.vector
        def _(v):
            c.wait(v, "eS")
            c.wait(v, "eV")
            v.tensor_tensor(out=hcolTa[:], in0=hcolM[:], in1=hcolR[:], op=ALU.mult)
            v.drain()
            v.tensor_scalar(out=hcolN[:], in0=hcolTa[:], scalar1=-1.0, scalar2=None, op0=ALU.mult)
            v.drain()
            v.tensor_scalar(out=dst_bf, in0=xsrc_ap, scalar1=hcolR[:], scalar2=hcolN[:],
                            op0=ALU.mult, op1=ALU.add)
            c.inc(v.drain(), "eV")
        c.bar()

        for w in range(4):
            @blk.tensor
            def _(t, w=w):
                for j in range(8):
                    i = 8 * w + j
                    c.inc(t.transpose(pT[:, 128 * j:128 * j + 3],
                                      dst_bf[:, 128 * i:128 * (i + 1)], idb[0:3, 0:3]), "eT")
            c.bar()

            @blk.vector
            def _(v, w=w):
                for j in range(8):
                    i = 8 * w + j
                    c.inc(v.tensor_copy(out=LNX[:, LW * i + 1024:LW * i + 1027],
                                        in_=pT[:, 128 * j:128 * j + 3]), "eV")
            c.bar()

    emit_ln_pass(lambda tt: xin[3 + 128 * tt:3 + 128 * (tt + 1), :], False, False)
    c.bar()

    @blk.gpsimd
    def _(g):
        c.inc(g.dma_start(out=xhb1, in_=xin[0:3, :]), "qG", 16)
    c.bar()
    emit_halo_ln(xhb1, hlnb1, False, hs2)

    if debug and upto == 1:
        @blk.gpsimd
        def _(g):
            for i in range(32):
                c.inc(g.dma_start(out=dbg["lnx"][i], in_=LNX[:, LW * i:LW * i + LW]), "qG", 16)
            c.wait(g, "qG")
        es.close()
        return nc, dbg

    @blk.tensor
    def _(t):
        for th in range(2):
            pp = p4 if th == 0 else p5
            for k in range(NKT):
                t.matmul(pp[0:R48, :], ATf[:, R48 * k:R48 * (k + 1)],
                         LNX[:, LW * k + 512 * th:LW * k + 512 * (th + 1)],
                         start=(k == 0), stop=(k == NKT - 1))
        for k in range(NKT):
            mi = t.matmul(p6[0:R48, 0:3], ATf[:, R48 * k:R48 * (k + 1)],
                          LNX[:, LW * k + 1024:LW * k + 1027],
                          start=(k == 0), stop=(k == NKT - 1))
        c.inc(mi, "eT")

    @blk.vector
    def _(v):
        c.wait(v, "eT")
        v.tensor_copy(out=rmk[0:R48, 0:512], in_=p4[0:R48, :])
        v.tensor_copy(out=rmk[0:R48, 512:1024], in_=p5[0:R48, :])
        v.tensor_copy(out=rmk[0:R48, 1024:1027], in_=p6[0:R48, 0:3])
        c.inc(v.drain(), "eV")
    c.bar()

    dmat_mark = {}
    cast_mark = {}
    evac_mark = {}
    sq_mark = {}
    x1w_mark = {}
    xld_mark = {}
    mm_mark = {}
    hevac_mark = {}
    hmm_mark = {}
    h512_mark = {}

    for oc in range(8):
        qsem = "qA" if oc % 2 == 0 else "qB"

        @blk.gpsimd
        def _(g, oc=oc, qsem=qsem):
            if (oc - 2) in hmm_mark:
                c.wait(g, "eT", hmm_mark[oc - 2])
            for k in range(NKT):
                c.inc(g.dma_start(out=WFF[oc % 2][:, 512 * k:512 * (k + 1)],
                                  in_=ffn_wT[128 * k:128 * (k + 1), 512 * oc:512 * (oc + 1)]),
                      qsem, 16)
            dmat_mark[oc] = (qsem, c.mark(qsem))

        for tt in range(8):
            @blk.scalar
            def _(s, oc=oc, tt=tt):
                prev = (oc, tt - 2) if tt >= 2 else (oc - 1, tt + 6)
                if prev in evac_mark:
                    c.wait(s, "eV", evac_mark[prev])
                c.inc(s.dma_start(out=xcb[:, 512 * (tt % 2):512 * (tt % 2 + 1)],
                                  in_=xin[3 + 128 * tt:3 + 128 * (tt + 1), 512 * oc:512 * (oc + 1)]),
                      "qC", 16)
                xld_mark[(oc, tt)] = c.mark("qC")

            @blk.tensor
            def _(t, oc=oc, tt=tt):
                if tt == 0:
                    c.wait(t, dmat_mark[oc][0], dmat_mark[oc][1])
                bank = PS[tt % 4]
                prevg = (oc, tt - 4) if tt >= 4 else (oc - 1, tt + 4)
                if prevg in evac_mark:
                    c.wait(t, "eV", evac_mark[prevg])
                for k in range(NKT):
                    t.matmul(bank[:], LNX[:, LW * k + 128 * tt:LW * k + 128 * (tt + 1)],
                             WFF[oc % 2][:, 512 * k:512 * (k + 1)],
                             start=(k == 0), stop=False)
                c.inc(t.matmul(bank[:], rmk[0:49, 128 * tt:128 * (tt + 1)],
                               BTFB[0:49, 512 * oc:512 * (oc + 1)], start=False, stop=True), "eT")
                mm_mark[(oc, tt)] = c.mark("eT")

            @blk.vector
            def _(v, oc=oc, tt=tt):
                c.wait(v, "eT", mm_mark[(oc, tt)])
                c.wait(v, "qC", xld_mark[(oc, tt)])
                prevslot = (oc, tt - 2) if tt >= 2 else (oc - 1, tt + 6)
                if prevslot in x1w_mark:
                    c.wait(v, "qW", x1w_mark[prevslot])
                if prevslot in sq_mark:
                    c.wait(v, "eS", sq_mark[prevslot])
                slot = x1b[:, 512 * (tt % 2):512 * (tt % 2 + 1)]
                v.scalar_tensor_tensor(out=slot, in0=PS[tt % 4][:], scalar=halfcol[:],
                                       in1=xcb[:, 512 * (tt % 2):512 * (tt % 2 + 1)],
                                       op0=ALU.mult, op1=ALU.add)
                v.drain()
                v.reduce_sum(out=statS3[:, 8 * tt + oc:8 * tt + oc + 1], in_=slot, axis=AXX)
                c.inc(v.drain(), "eV")
                evac_mark[(oc, tt)] = c.mark("eV")

            @blk.scalar
            def _(s, oc=oc, tt=tt):
                c.wait(s, "eV", evac_mark[(oc, tt)])
                s.activation(out=scr[:], in_=x1b[:, 512 * (tt % 2):512 * (tt % 2 + 1)],
                             func=ACT.Square, accum_out=statS4[:, 8 * tt + oc:8 * tt + oc + 1])
                c.inc(s.drain(), "eS")
                sq_mark[(oc, tt)] = c.mark("eS")

            @blk.gpsimd
            def _(g, oc=oc, tt=tt):
                c.wait(g, "eV", evac_mark[(oc, tt)])
                c.inc(g.dma_start(out=o_x1[128 * tt:128 * (tt + 1), 512 * oc:512 * (oc + 1)],
                                  in_=x1b[:, 512 * (tt % 2):512 * (tt % 2 + 1)]), "qW", 16)
                x1w_mark[(oc, tt)] = c.mark("qW")

        @blk.scalar
        def _(s, oc=oc):
            if (oc - 1) in hevac_mark:
                c.wait(s, "eV", hevac_mark[oc - 1])
            c.inc(s.dma_start(out=hscr[:], in_=xin[0:3, 512 * oc:512 * (oc + 1)]), "qC", 16)

        @blk.tensor
        def _(t, oc=oc):
            if (oc - 1) in hevac_mark:
                c.wait(t, "eV", hevac_mark[oc - 1])
            for k in range(NKT):
                t.matmul(p6[0:3, 0:512], LNX[:, LW * k + 1024:LW * k + 1027],
                         WFF[oc % 2][:, 512 * k:512 * (k + 1)], start=(k == 0), stop=False)
            c.inc(t.matmul(p6[0:3, 0:512], rmk[0:49, 1024:1027],
                           BTFB[0:49, 512 * oc:512 * (oc + 1)], start=False, stop=True), "eT")
            hmm_mark[oc] = c.mark("eT")

        @blk.vector
        def _(v, oc=oc):
            c.wait(v, "eT", hmm_mark[oc])
            c.wait(v, "qC")
            if oc > 0:
                c.wait(v, "qW", h512_mark[oc - 1])
            v.scalar_tensor_tensor(out=h512[:], in0=p6[0:3, 0:512],
                                   scalar=halfcol[0:3, 0:1], in1=hscr[:], op0=ALU.mult, op1=ALU.add)
            c.inc(v.drain(), "eV")
            hevac_mark[oc] = c.mark("eV")

        @blk.gpsimd
        def _(g, oc=oc):
            c.wait(g, "eV", hevac_mark[oc])
            c.inc(g.dma_start(out=x1h_dram[:, 512 * oc:512 * (oc + 1)], in_=h512[:]), "qW", 16)
            h512_mark[oc] = c.mark("qW")
    c.bar()

    if debug and upto == 2:
        @blk.gpsimd
        def _(g):
            c.inc(g.dma_start(out=dbg["x1h"], in_=x1h_dram[:]), "qG", 16)
            c.inc(g.dma_start(out=dbg["wff"], in_=WFF[1]), "qG", 16)
            c.inc(g.dma_start(out=dbg["rmk"], in_=rmk[0:64, :]), "qG", 16)
            c.wait(g, "qG")
        es.close()
        return nc, dbg

    @blk.vector
    def _(v):
        for c34 in range(34):
            for k4 in range(4):
                d = 4 * c34 + k4
                v.tensor_scalar(out=DIAG[:, 128 * d:128 * (d + 1)], in0=idb[:],
                                scalar1=convw[:, d:d + 1], scalar2=None, op0=ALU.mult)
        c.inc(v.drain(), "eV")

    emit_ln_pass(lambda tt: o_x1[128 * tt:128 * (tt + 1), :], True, True)
    c.bar()

    @blk.gpsimd
    def _(g):
        c.inc(g.dma_start(out=x1hbuf, in_=x1h_dram), "qG", 16)
    c.bar()
    emit_halo_ln(x1hbuf, hlnb3, True, hs4)

    @blk.vector
    def _(v):
        for i in range(32):
            v.tensor_scalar(out=LNX[:, LW * i:LW * i + 1027], in0=LNX[:, LW * i:LW * i + 1027],
                            scalar1=lnwc[:, i:i + 1], scalar2=None, op0=ALU.mult)
        c.inc(v.drain(), "eV")
    c.bar()

    if debug and upto == 3:
        @blk.gpsimd
        def _(g):
            for i in range(32):
                c.inc(g.dma_start(out=dbg["lnx"][i], in_=LNX[:, LW * i:LW * i + LW]), "qG", 16)
            c.wait(g, "qG")
        es.close()
        return nc, dbg

    wp_mark = {}
    ipmm_mark = {}

    MKSTOP = 9
    for p in range(34):
        tiles = [66] if p == 33 else [2 * p, 2 * p + 1]
        width = 64 if p == 33 else 256
        colbase = 8448 if p == 33 else 256 * p
        is_xbc = 32 <= tiles[0] < 66

        @blk.gpsimd
        def _(g, p=p, width=width, colbase=colbase):
            if (p - 2) in ipmm_mark:
                c.wait(g, "eT", ipmm_mark[p - 2])
            for k in range(NKT):
                c.inc(g.dma_start(
                    out=WP[p % 2][:, k * width:(k + 1) * width],
                    in_=ipw[128 * k:128 * (k + 1), colbase:colbase + width]), "qG", 16)
            wp_mark[p] = c.mark("qG")

        if MKSTOP < 2:
            ipmm_mark[p] = c.mark("eT")
            continue

        @blk.tensor
        def _(t, p=p, tiles=tiles, width=width, is_xbc=is_xbc):
            c.wait(t, "qG", wp_mark[p])
            c.wait(t, "eS")
            c.wait(t, "eV")
            for ti, o in enumerate(tiles):
                m = 64 if o == 66 else 128
                for th in range(2):
                    bank = PS[2 * ti + th]
                    for k in range(NKT):
                        mi = t.matmul(bank[0:m, :],
                                      WP[p % 2][:, k * width + 128 * ti:k * width + 128 * ti + m],
                                      LNX[:, LW * k + 512 * th:LW * k + 512 * (th + 1)],
                                      start=(k == 0), stop=(k == NKT - 1))
            if is_xbc:
                for k in range(NKT):
                    mi = t.matmul(p6[0:3, 0:width], LNX[:, LW * k + 1024:LW * k + 1027],
                                  WP[p % 2][:, k * width:(k + 1) * width],
                                  start=(k == 0), stop=(k == NKT - 1))
            c.inc(mi, "eT")
            ipmm_mark[p] = c.mark("eT")

        if MKSTOP < 3:
            continue

        @blk.scalar
        def _(s, p=p, tiles=tiles, width=width, is_xbc=is_xbc):
            c.wait(s, "eT", ipmm_mark[p])
            c.wait(s, "qW")
            for ti, o in enumerate(tiles):
                if o < 32:
                    zb = xstmp if ti == 0 else xbctmp[:, 0:1024]
                    s.activation(out=zb[:, 0:512], in_=PS[2 * ti][:],
                                 func=ACT.Silu, scale=mupc[:, o:o + 1])
                    s.activation(out=zb[:, 512:1024], in_=PS[2 * ti + 1][:],
                                 func=ACT.Silu, scale=mupc[:, o:o + 1])
                elif o < 66:
                    buf = xbctmp if ti == 0 else xbc2tmp
                    s.copy(out=hzx[:, 128 * (o - 32):128 * (o - 31)],
                           in_=p6[0:3, 128 * ti:128 * (ti + 1)])
                    s.copy(out=buf[:, 3:515], in_=PS[2 * ti][:])
                    s.copy(out=buf[:, 515:1027], in_=PS[2 * ti + 1][:])
                else:
                    s.activation(out=dtraw[0:64, 0:512], in_=PS[0][0:64, :], func=ACT.Identity,
                                 bias=dtbias_c[:], scale=mupc[0:64, 66:67])
                    s.activation(out=dtraw[0:64, 512:1024], in_=PS[1][0:64, :], func=ACT.Identity,
                                 bias=dtbias_c[:], scale=mupc[0:64, 66:67])
            c.inc(s.drain(), "eS")
        semark = c.mark("eS")

        if tiles[0] < 32 and MKSTOP >= 4:
            @blk.gpsimd
            def _(g, p=p, tiles=tiles, semark=semark):
                c.wait(g, "eS", semark)
                for ti, o in enumerate(tiles):
                    zb = xstmp if ti == 0 else xbctmp[:, 0:1024]
                    c.inc(g.dma_start(out=o_siluz.rearrange("(i pp) t -> i pp t", pp=128)[o],
                                      in_=zb[:]), "qW", 16)

        if is_xbc:
            @blk.tensor
            def _(t, p=p, tiles=tiles, semark=semark):
                c.wait(t, "eS", semark)
                for ti, o in enumerate(tiles):
                    c.inc(t.transpose(pT[:, 256 * ti:256 * ti + 3],
                                      hzx[:, 128 * (o - 32):128 * (o - 31)],
                                      idb[0:3, 0:3]), "eT")
            tmark = c.mark("eT")

            @blk.vector
            def _(v, p=p, tiles=tiles, tmark=tmark):
                c.wait(v, "eT", tmark)
                for ti, o in enumerate(tiles):
                    buf = xbctmp if ti == 0 else xbc2tmp
                    v.tensor_tensor(out=buf[:, 0:3], in0=pT[:, 256 * ti:256 * ti + 3],
                                    in1=halom_s[:, 0:3], op=ALU.mult)
                c.inc(v.drain(), "eV")
            vmark = c.mark("eV")

            for ti in range(2):
                o = tiles[ti]

                @blk.tensor
                def _(t, p=p, o=o, ti=ti, vmark=vmark):
                    c.wait(t, "eV", vmark)
                    c.wait(t, "eS")
                    buf = xbctmp if ti == 0 else xbc2tmp
                    for th in range(2):
                        bank = p4 if th == 0 else p5
                        for k4 in range(4):
                            d = 4 * (o - 32) + k4
                            mi = t.matmul(bank[:], DIAG[:, 128 * d:128 * (d + 1)],
                                          buf[:, 512 * th + k4:512 * th + k4 + 512],
                                          start=(k4 == 0), stop=(k4 == 3))
                    c.inc(mi, "eT")
                cmark = c.mark("eT")

                @blk.scalar
                def _(s, p=p, o=o, cmark=cmark):
                    c.wait(s, "eT", cmark)
                    c.wait(s, "qW")
                    if o == 64:
                        s.activation(out=Bfm[:, 0:512], in_=p4[:], func=ACT.Silu)
                        s.activation(out=Bfm[:, 512:1024], in_=p5[:], func=ACT.Silu)
                    elif o == 65:
                        s.activation(out=Cfm[:, 0:512], in_=p4[:], func=ACT.Silu)
                        s.activation(out=Cfm[:, 512:1024], in_=p5[:], func=ACT.Silu)
                    else:
                        s.activation(out=xstmp[:, 0:512], in_=p4[:], func=ACT.Silu)
                        s.activation(out=xstmp[:, 512:1024], in_=p5[:], func=ACT.Silu)
                    c.inc(s.drain(), "eS")
                smark = c.mark("eS")

                if o < 64:
                    @blk.gpsimd
                    def _(g, o=o, smark=smark):
                        c.wait(g, "eS", smark)
                        c.inc(g.dma_start(
                            out=xs_dram.rearrange("(i pp) t -> i pp t", pp=128)[o - 32],
                            in_=xstmp[:]), "qW", 16)

        if tiles[0] == 66:
            @blk.vector
            def _(v, p=p):
                c.wait(v, "eS")
                v.tensor_scalar(out=dtabs, in0=dtraw[0:64, :], scalar1=-1.0, scalar2=None,
                                op0=ALU.mult)
                v.drain()
                v.tensor_tensor(out=dtabs, in0=dtabs, in1=dtraw[0:64, :], op=ALU.max)
                c.inc(v.drain(), "eV")

            @blk.scalar
            def _(s, p=p):
                c.wait(s, "eV")
                s.activation(out=dtabs, in_=dtabs, func=ACT.Exp, scale=-1.0)
                s.drain()
                s.activation(out=dtabs, in_=dtabs, func=ACT.Ln, bias=1.0)
                c.inc(s.drain(), "eS")

            @blk.vector
            def _(v, p=p):
                c.wait(v, "eS")
                v.tensor_single_scalar(out=dtf[:], in_=dtraw[0:64, :], scalar=0.0, op=ALU.max)
                v.drain()
                v.tensor_tensor(out=dtf[:], in0=dtf[:], in1=dtabs, op=ALU.add)
                c.inc(v.drain(), "eV")
    c.bar()

    @blk.gpsimd
    def _(g):
        c.inc(g.dma_start(out=o_C, in_=Cfm[:]), "qW", 16)
        c.wait(g, "qW")
    c.bar()

    if debug and upto == 4:

        @blk.gpsimd
        def _(g):
            c.inc(g.dma_start(out=dbg["B"], in_=Bfm[:]), "qG", 16)
            c.inc(g.dma_start(out=dbg["dt"], in_=dtf[:]), "qG", 16)
            c.inc(g.dma_start(out=dbg["zz"], in_=xstmp[:]), "qG", 16)
            for i in range(32):
                c.inc(g.dma_start(out=dbg["xs"].rearrange("(i pp) t -> i pp t", pp=128)[i],
                                  in_=xs_dram.rearrange("(i pp) t -> i pp t", pp=128)[i]), "qG", 16)
            c.wait(g, "qG")
        es.close()
        return nc, dbg

    XC = XCH[0]

    @blk.vector
    def _(v):
        v.memset(A2[:, 20480:32768], 0.0)
        v.memset(prm3[0:1, 0:64], 0.0)
        v.tensor_scalar(out=x1b[0:64, :], in0=dtf[:], scalar1=nA_c[:], scalar2=None,
                        op0=ALU.mult)
        v.drain()
        for qq in range(NCH):
            v.tensor_tensor_scan(out=cumall[:, Q * qq:Q * (qq + 1)], data0=ones64[:],
                                 data1=x1b[0:64, Q * qq:Q * (qq + 1)], initial=0.0,
                                 op0=ALU.mult, op1=ALU.add)
        c.inc(v.drain(), "eV")

    @blk.gpsimd
    def _(g):
        c.wait(g, "eV")
        c.inc(g.dma_start(out=flat_dram.rearrange("h c t -> h (c t)"), in_=cumall[:]), "qW", 16)

    @blk.tensor
    def _(t):
        c.inc(t.matmul(p6[:, 0:64], onescf[:], prm3[0:1, 128:192], start=True, stop=True), "eT")

    @blk.vector
    def _(v):
        c.wait(v, "eT")
        v.tensor_copy(out=Dtok[:], in_=p6[:, 0:64])
        c.inc(v.drain(), "eV")
    c.bar()

    for q in range(NCH):
        @blk.scalar
        def _(s, q=q):
            for i in range(32):
                c.inc(s.dma_start(out=XPP[:, 128 * i:128 * (i + 1)],
                                  in_=xs_dram.rearrange("(i pp) t -> i pp t", pp=128)[i][:, Q * q:Q * (q + 1)]),
                      "qC", 16)
        xstm = c.mark("qC")

        @blk.gpsimd
        def _(g, q=q):
            for i in range(32):
                c.inc(g.dma_start(out=G1CH[:, 128 * i:128 * (i + 1)],
                                  in_=o_siluz.rearrange("(i pp) t -> i pp t", pp=128)[i][:, Q * q:Q * (q + 1)]),
                      "qG", 16)
            c.inc(g.dma_start(out=cumflat[0:1, :], in_=flat_dram[:, q, :]), "qG", 16)
        szlm = c.mark("qG")

        @blk.sync
        def _(sy, q=q, xstm=xstm, szlm=szlm):
            c.wait(sy, "qC", xstm)
            c.wait(sy, "qG", szlm)
            for i in range(32):
                c.inc(sy.dma_start(out=XC[:, 128 * i:128 * (i + 1)],
                                   in_=XPP[:, 128 * i:128 * (i + 1)], transpose=True), "qS", 16)
            for i in range(32):
                c.inc(sy.dma_start(out=HTs[:, 128 * i:128 * (i + 1)],
                                   in_=G1CH[:, 128 * i:128 * (i + 1)], transpose=True), "qS", 16)
            c.inc(sy.dma_start(out=Btok[:], in_=Bfm[:, Q * q:Q * (q + 1)], transpose=True), "qS", 16)
        szm = c.mark("qS")

        @blk.tensor
        def _(t, q=q):
            c.inc(t.transpose(p6[:, 64:128], cumall[:, Q * q:Q * (q + 1)], idf[0:64, 0:64]), "eT")
            c.inc(t.transpose(p6[:, 128:192], dtf[:, Q * q:Q * (q + 1)], idf[0:64, 0:64]), "eT")
            c.inc(t.transpose(p6[0:1, 320:384], cumall[:, Q * q + 127:Q * q + 128],
                              idf[0:64, 0:64]), "eT")
        tam = c.mark("eT")

        @blk.vector
        def _(v, q=q, tam=tam):
            c.wait(v, "eT", tam)
            v.tensor_copy(out=cumT[:], in_=p6[:, 64:128])
            v.tensor_copy(out=dtT[:], in_=p6[:, 128:192])
            v.tensor_copy(out=ecrow[:], in_=p6[0:1, 320:384])
            v.drain()
            v.tensor_scalar(out=ncumT[:], in0=cumT[:], scalar1=-1.0, scalar2=None, op0=ALU.mult)
            c.inc(v.drain(), "eV")
        vam = c.mark("eV")

        @blk.scalar
        def _(s, q=q, vam=vam):
            c.wait(s, "eV", vam)
            s.activation(out=ECtok[:], in_=cumT[:], func=ACT.Exp)
            c.inc(s.drain(), "eS")
        ecm = c.mark("eS")

        @blk.tensor
        def _(t, q=q, vam=vam):
            c.wait(t, "eV", vam)
            c.inc(t.matmul(p6[:, 192:256], onescf[:], ecrow[:], start=True, stop=True), "eT")
            c.inc(t.matmul(p6[:, 384:448], onescf[:], prm3[0:1, 0:64], start=True, stop=True), "eT")
        tbm = c.mark("eT")

        @blk.vector
        def _(v, q=q, tbm=tbm):
            c.wait(v, "eT", tbm)
            v.tensor_tensor(out=xcb[:, 128:192], in0=p6[:, 192:256], in1=cumT[:], op=ALU.subtract)
            v.tensor_tensor(out=ocumb[:], in0=cumT[:], in1=p6[:, 384:448], op=ALU.add)
            v.drain()
            v.tensor_tensor(out=prm3[0:1, 0:64], in0=prm3[0:1, 0:64], in1=ecrow[:], op=ALU.add)
            c.inc(v.drain(), "eV")
        vbm = c.mark("eV")

        @blk.scalar
        def _(s, q=q, vbm=vbm):
            c.wait(s, "eV", vbm)
            s.activation(out=EEs[:, 0:64], in_=xcb[:, 128:192], func=ACT.Exp)
            s.activation(out=dend[:, 0:64], in_=p6[:, 192:256], func=ACT.Exp)
            c.inc(s.drain(), "eS")
        eem = c.mark("eS")

        @blk.gpsimd
        def _(g, q=q, vbm=vbm):
            c.wait(g, "eV", vbm)
            c.inc(g.dma_start(out=o_cum[Q * q:Q * (q + 1), :], in_=ocumb[:]), "qW", 16)
            c.inc(g.dma_start(out=o_ctot[q:q + 1, :], in_=prm3[0:1, 0:64]), "qW", 16)
            c.wait(g, "qS", szm)
            c.inc(g.dma_start(out=o_sztok[Q * q:Q * (q + 1), :], in_=HTs[:]), "qW", 16)

        @blk.vector
        def _(v, q=q, eem=eem, szm=szm):
            c.wait(v, "eS", eem)
            c.wait(v, "qS", szm)
            for h in range(NH):
                v.tensor_scalar(out=XTOK[:, 64 * h:64 * (h + 1)], in0=XC[:, 64 * h:64 * (h + 1)],
                                scalar1=dtT[:, h:h + 1], scalar2=None, op0=ALU.mult)
            for h in range(NH):
                v.tensor_scalar(out=XPP[:, 64 * h:64 * (h + 1)], in0=XTOK[:, 64 * h:64 * (h + 1)],
                                scalar1=EEs[:, h:h + 1], scalar2=None, op0=ALU.mult)
            c.inc(v.drain(), "eV")
        foldm = c.mark("eV")

        @blk.tensor
        def _(t, q=q):
            c.inc(t.matmul(p4[:, 0:128], Bfm[:, Q * q:Q * (q + 1)], Cfm[:, Q * q:Q * (q + 1)],
                           start=True, stop=True), "eT")
        gtm = c.mark("eT")

        @blk.vector
        def _(v, q=q, gtm=gtm):
            c.wait(v, "eT", gtm)
            v.tensor_copy(out=C4[:, 0:128], in_=p4[:, 0:128])
            v.tensor_copy(out=C4[:, 128:256], in_=p4[:, 0:128])
            c.inc(v.drain(), "eV")
        gdm = c.mark("eV")

        repm, maskm, expm, wm, y1m, carrym, evacm = {}, {}, {}, {}, {}, {}, {}
        for ip in range(32):
            iq = ip // 4
            e0 = 256 * (ip % 2)

            if ip % 4 == 0:
                @blk.tensor
                def _(t, q=q, iq=iq):
                    if iq >= 2:
                        c.wait(t, "eV", evacm[iq - 2])
                    c.inc(t.matmul(PS[2 + iq % 2][:], Cfm[:, Q * q:Q * (q + 1)],
                                   HTb[:, 512 * iq:512 * (iq + 1)], start=True, stop=True), "eT")
                carrym[iq] = c.mark("eT")

            @blk.tensor
            def _(t, q=q, ip=ip, e0=e0, szlm=szlm):
                c.wait(t, "qG", szlm)
                if ip >= 2:
                    c.wait(t, "eV", maskm[ip - 2])
                c.inc(t.matmul(p5[:, e0:e0 + 256], onescf[:],
                               cumflat[0:1, 256 * ip:256 * (ip + 1)], start=True, stop=True), "eT")
            repm[ip] = c.mark("eT")

            @blk.vector
            def _(v, q=q, ip=ip, e0=e0):
                c.wait(v, "eT", repm[ip])
                if ip >= 2:
                    c.wait(v, "eS", expm[ip - 2])
                v.tensor_tensor(out=scr[:, e0:e0 + 256], in0=p5[:, e0:e0 + 256],
                                in1=mkb4[:, 0:256], op=ALU.add)
                c.inc(v.drain(), "eV")
            maskm[ip] = c.mark("eV")

            @blk.scalar
            def _(s, q=q, ip=ip, e0=e0):
                c.wait(s, "eV", maskm[ip])
                if ip >= 2:
                    c.wait(s, "eV", wm[ip - 2])
                s.activation(out=E4[:, e0:e0 + 128], in_=scr[:, e0:e0 + 128],
                             func=ACT.Exp, bias=ncumT[:, 2 * ip:2 * ip + 1])
                s.activation(out=E4[:, e0 + 128:e0 + 256], in_=scr[:, e0 + 128:e0 + 256],
                             func=ACT.Exp, bias=ncumT[:, 2 * ip + 1:2 * ip + 2])
                c.inc(s.drain(), "eS")
            expm[ip] = c.mark("eS")

            @blk.vector
            def _(v, q=q, ip=ip, e0=e0):
                c.wait(v, "eS", expm[ip])
                c.wait(v, "eV", gdm)
                if ip >= 2:
                    c.wait(v, "eT", y1m[ip - 2])
                v.tensor_tensor(out=W4[:, e0:e0 + 256], in0=E4[:, e0:e0 + 256],
                                in1=C4[:, 0:256], op=ALU.mult)
                c.inc(v.drain(), "eV")
            wm[ip] = c.mark("eV")

            @blk.tensor
            def _(t, q=q, ip=ip, iq=iq, e0=e0, foldm=foldm):
                c.wait(t, "eV", wm[ip])
                c.wait(t, "eV", foldm)
                if ip % 4 == 0 and iq >= 2:
                    c.wait(t, "eV", evacm[iq - 2])
                l = 2 * ip - 8 * iq
                t.matmul(PS[iq % 2][:, 64 * l:64 * (l + 1)], W4[:, e0:e0 + 128],
                         XTOK[:, 64 * (2 * ip):64 * (2 * ip + 1)], start=True, stop=True)
                c.inc(t.matmul(PS[iq % 2][:, 64 * (l + 1):64 * (l + 2)], W4[:, e0 + 128:e0 + 256],
                               XTOK[:, 64 * (2 * ip + 1):64 * (2 * ip + 2)], start=True, stop=True),
                      "eT")
            y1m[ip] = c.mark("eT")

            if ip % 4 == 3:
                @blk.vector
                def _(v, q=q, ip=ip, iq=iq, ecm=ecm, szm=szm):
                    c.wait(v, "eT", y1m[ip])
                    c.wait(v, "eT", carrym[iq])
                    c.wait(v, "eS", ecm)
                    c.wait(v, "qS", szm)
                    for l in range(8):
                        h = 8 * iq + l
                        v.tensor_scalar(out=xcb[:, 0:64], in0=PS[2 + iq % 2][:, 64 * l:64 * (l + 1)],
                                        scalar1=ECtok[:, h:h + 1], scalar2=None, op0=ALU.mult)
                        v.tensor_tensor(out=xcb[:, 64:128], in0=xcb[:, 0:64],
                                        in1=PS[iq % 2][:, 64 * l:64 * (l + 1)], op=ALU.add)
                        v.scalar_tensor_tensor(out=xcb[:, 0:64], in0=XC[:, 64 * h:64 * (h + 1)],
                                               scalar=Dtok[:, h:h + 1], in1=xcb[:, 64:128],
                                               op0=ALU.mult, op1=ALU.add)
                        v.tensor_tensor(out=G1CH[:, 64 * h:64 * (h + 1)], in0=xcb[:, 0:64],
                                        in1=HTs[:, 64 * h:64 * (h + 1)], op=ALU.mult)
                    c.inc(v.drain(), "eV")
                evacm[iq] = c.mark("eV")

        @blk.gpsimd
        def _(g, q=q):
            c.wait(g, "eV", evacm[7])
            c.inc(g.dma_start(out=o_g1[Q * q:Q * (q + 1), :], in_=G1CH[:]), "qW", 16)

        hqm, hupm = {}, {}
        for iq in range(8):
            @blk.tensor
            def _(t, q=q, iq=iq, foldm=foldm, szm=szm):
                if iq == 0:
                    c.wait(t, "eV", evacm[7])
                    c.wait(t, "qS", szm)
                if iq >= 4:
                    c.wait(t, "eV", hupm[iq - 4])
                c.inc(t.matmul(PS[iq % 4][:], Btok[:], XPP[:, 512 * iq:512 * (iq + 1)],
                               start=True, stop=True), "eT")
            hqm[iq] = c.mark("eT")

            @blk.vector
            def _(v, q=q, iq=iq, eem=eem):
                c.wait(v, "eT", hqm[iq])
                c.wait(v, "eS", eem)
                for l in range(8):
                    h = 8 * iq + l
                    v.scalar_tensor_tensor(out=HTf[:, 64 * h:64 * (h + 1)], in0=HTf[:, 64 * h:64 * (h + 1)],
                                           scalar=dend[:, h:h + 1], in1=PS[iq % 4][:, 64 * l:64 * (l + 1)],
                                           op0=ALU.mult, op1=ALU.add)
                v.drain()
                v.tensor_copy(out=HTb[:, 512 * iq:512 * (iq + 1)], in_=HTf[:, 512 * iq:512 * (iq + 1)])
                c.inc(v.drain(), "eV")
            hupm[iq] = c.mark("eV")
        c.bar()

    @blk.gpsimd
    def _(g):
        c.inc(g.dma_start(out=o_hloc[:], in_=HTf[:]), "qW", 16)
        c.wait(g, "qW")
    c.bar()

    es.close()
    return nc, dbg


def build_l2(debug=0):
    STOP = 9
    nc = bass.Bass("TRN2", target_bir_lowering=False)
    es = ExitStack()

    def din(name, shape, dt=F32):
        return nc.dram_tensor(name, shape, dt, kind="ExternalInput").ap()

    def dout(name, shape, dt=F32):
        return nc.dram_tensor(name, shape, dt, kind="ExternalOutput").ap()

    def sb(name, p, f, dt):
        return es.enter_context(nc.sbuf_tensor(name, [p, f], dt))

    def ps(name, p, f, dt=F32):
        return es.enter_context(nc.psum_tensor(name, [p, f], dt))

    x1 = din("x1", [T, H])
    g1 = din("g1", [T, H], BF16)
    sz = din("sz", [T, H], BF16)
    Cbf = din("Cbf", [NS, T], BF16)
    cum = din("cum", [T, NH])
    ctcore = din("ctcore", [8, NH])
    hloc = din("hloc", [8 * NS, H])
    prefw = din("prefw", [NH, 8])
    norm_w = din("norm_w", [1, H])
    opw = din("opw", [H, H])
    ffn_wT = din("ffn_wT", [H, H])
    ffn_b = din("ffn_b", [1, H])
    lora_A = din("lora_A", [R48, H])
    lora_B = din("lora_B", [3 * H, 16])
    lsi = din("lsi", [1, 4], I32)

    o_out = dout("o_out", [T, H])
    x2_dram = nc.dram_tensor("x2_dram", [T, H], F32, kind="Internal").ap()

    dbg = {}
    if debug:
        dbg["gy"] = dout("d_gy", [Q, H])
        dbg["x2"] = dout("d_x2", [T, H])
        dbg["hin"] = dout("d_hin", [NS, H], BF16)
        dbg["ew"] = dout("d_ew", [NH, 8])
        dbg["wff"] = dout("d_wff", [128, 32768], BF16)
        dbg["lnf"] = dout("d_lnf", [128, 1024], BF16)
        dbg["rmk"] = dout("d_rmk", [64, 1024], BF16)
        dbg["btfb"] = dout("d_btfb", [128, 4096], BF16)

    LNF = sb("LNF", 128, 32768, BF16)
    WB = sb("WB", 128, 32768, BF16)
    AR2 = sb("AR2", 128, 8192, BF16)
    GY = sb("GY", 128, 4096, F32)
    GYN = sb("GYN", 128, 4096, BF16)
    G1B = sb("G1B", 128, 4096, BF16)
    HINb = AR2[:, 0:4096]
    NWREP = AR2[:, 4096:8192]
    Cfm = sb("Cfm", 128, T, BF16)
    cumL2 = sb("cumL2", 128, 512, F32)
    EC2 = sb("EC2", 128, 64, F32)
    ATf = sb("ATf", 128, 32 * R48, BF16)
    BTFB = sb("BTFB", 128, H, BF16)
    biasrow = BTFB[48:49, :]
    rmk2 = sb("rmk2", 64, 1024, BF16)
    xcb2 = sb("xcb2", 128, 1024, F32)
    x2b = sb("x2b", 128, 1024, F32)
    statS3 = sb("statS3", 128, 64, F32)
    statS4 = sb("statS4", 128, 64, F32)
    colM = sb("colM", 128, 1, F32)
    colR = sb("colR", 128, 1, F32)
    colN = sb("colN", 128, 1, F32)
    colTa = sb("colTa", 128, 1, F32)
    colTb = sb("colTb", 128, 1, F32)
    colS = sb("colS", 128, 1, F32)
    colQ = sb("colQ", 128, 1, F32)
    scr = sb("scr", 128, 512, F32)
    EWR = scr
    idb = sb("idb", 128, 128, BF16)
    idf = sb("idf", 128, 128, F32)
    onescf = sb("onescf", 1, 128, F32)
    halfcol = sb("halfcol", 128, 1, F32)
    cts = sb("cts", 8, 64, F32)
    ct64 = sb("ct64", 64, 8, F32)
    sfx = sb("sfx", 64, 8, F32)
    ew = sb("ew", 64, 8, F32)
    pw64 = sb("pw64", 64, 8, F32)
    ewrow = sb("ewrow", 1, 64, F32)
    lsis = sb("lsis", 1, 4, I32)
    lsif = sb("lsif", 1, 4, F32)
    cnt1 = sb("cnt1", 1, 1, F32)
    cst13 = sb("cst13", 1, 3, F32)
    msk13 = sb("msk13", 1, 3, F32)

    HLD = LNF[:, 0:8192].bitcast(F32)
    LAB = WB[:, 0:4096]
    LBB = WB[:, 4096:5632]
    btmp = WB[0:16, 5632:6656]
    OPW = [WB[:, 0:16384], WB[:, 16384:32768]]
    WFF = [WB[:, 0:16384], WB[:, 16384:32768]]
    OTB = [AR2[:, 0:4096], AR2[:, 4096:8192]]
    xst2 = GY
    lnxtok2 = GYN
    SZB = GYN

    p0 = ps("p0", 128, 512)
    p1 = ps("p1", 128, 512)
    p2 = ps("p2", 128, 512)
    p3 = ps("p3", 128, 512)
    p4 = ps("p4", 128, 512)
    p5 = ps("p5", 128, 512)
    p6 = ps("p6", 128, 512)
    pT = ps("pT", 128, 1024, BF16)
    PS = [p0, p1, p2, p3]

    blk = es.enter_context(nc.Block())
    c = Ctx(nc, blk)
    for s in ["qS", "qC", "qG", "qW", "qA", "qB", "qD", "eT", "eV", "eS", "eG"]:
        c.sems[s] = [es.enter_context(nc.semaphore(s)), 0]

    @blk.gpsimd
    def _(g):
        make_ident(g, idb[:])
        make_ident(g, idf[:])
        g.memset(cst13[0:1, 0:1], 1.0)
        g.memset(cst13[0:1, 1:2], 2.0)
        c.inc(g.memset(cst13[0:1, 2:3], 3.0), "eG")
        c.inc(g.dma_start(out=lsis[:], in_=lsi), "qG", 16)
        c.inc(g.dma_start(out=biasrow, in_=ffn_b), "qG", 16)
        c.inc(g.dma_start(out=LAB[0:R48, :], in_=lora_A), "qG", 16)
        c.inc(g.dma_start(out=LBB[:, 0:1536].rearrange("p (a r) -> p a r", r=16),
                          in_=lora_B.rearrange("(a p) r -> p a r", p=128)), "qG", 16)
        c.inc(g.dma_start(out=cts[:], in_=ctcore), "qG", 16)
        c.inc(g.dma_start(out=pw64[:], in_=prefw), "qG", 16)
        c.inc(g.dma_start(out=Cfm[:], in_=Cbf), "qG", 16)
        for qq in range(8):
            c.inc(g.dma_start(out=cumL2[:, 64 * qq:64 * (qq + 1)],
                              in_=cum[Q * qq:Q * (qq + 1), :]), "qG", 16)

    @blk.vector
    def _(v):
        v.memset(rmk2[32:64, :], 1.0)
        v.memset(statS3[:], 0.0)
        v.memset(statS4[:], 0.0)
        v.memset(onescf[:], 1.0)
        v.memset(halfcol[:], 0.5)
        c.wait(v, "eG")
        c.wait(v, "qG")
        v.tensor_copy(out=lsif[:], in_=lsis[:])
        v.drain()
        v.tensor_single_scalar(out=lsif[:], in_=lsif[:], scalar=0.0, op=ALU.is_le)
        v.drain()
        v.reduce_sum(out=cnt1[:], in_=lsif[:], axis=AXX)
        v.drain()
        v.tensor_scalar(out=cnt1[:], in0=cnt1[:], scalar1=-1.0, scalar2=None, op0=ALU.add)
        v.drain()
        c.inc(v.tensor_scalar(out=msk13[:], in0=cst13[:], scalar1=cnt1[:], scalar2=None,
                              op0=ALU.is_equal), "eV")
    c.bar()

    @blk.tensor
    def _(t):
        c.inc(t.transpose(p6[0:3, 102:103], msk13[:], idf[0:1, 0:1]), "eT")
        for i in range(16):
            c.inc(t.transpose(pT[:, 48 * i:48 * (i + 1)], LAB[0:R48, 128 * i:128 * (i + 1)],
                              idb[0:R48, 0:R48]), "eT")
        c.inc(t.transpose(p6[0:64, 110:118], cts[:], idf[0:8, 0:8]), "eT")
    c.bar()

    @blk.vector
    def _(v):
        c.inc(v.tensor_copy(out=ATf[:, 0:768], in_=pT[:, 0:768]), "eV")
        c.inc(v.tensor_copy(out=ct64[:], in_=p6[0:64, 110:118]), "eV")
    c.bar()

    @blk.tensor
    def _(t):
        for i in range(16):
            c.inc(t.transpose(pT[:, 48 * i:48 * (i + 1)], LAB[0:R48, 128 * (16 + i):128 * (17 + i)],
                              idb[0:R48, 0:R48]), "eT")
        c.inc(t.matmul(p6[:, 103:106], onescf[:], msk13[:], start=True, stop=True), "eT")
    c.bar()

    @blk.vector
    def _(v):
        c.inc(v.tensor_copy(out=ATf[:, 768:1536], in_=pT[:, 0:768]), "eV")
        for i in range(32):
            for a in range(3):
                c.inc(v.tensor_scalar(out=ATf[:, 48 * i + 16 * a:48 * i + 16 * (a + 1)],
                                      in0=ATf[:, 48 * i + 16 * a:48 * i + 16 * (a + 1)],
                                      scalar1=p6[:, 103 + a:104 + a], scalar2=None,
                                      op0=ALU.mult), "eV")
        v.tensor_tensor(out=ct64[:], in0=ct64[:], in1=pw64[:], op=ALU.mult)
        v.drain()
        v.memset(sfx[:, 7:8], 0.0)
        v.drain()
        for j in range(6, -1, -1):
            v.tensor_tensor(out=sfx[:, j:j + 1], in0=sfx[:, j + 1:j + 2],
                            in1=ct64[:, j + 1:j + 2], op=ALU.add)
            v.drain()
        c.inc(v.drain(), "eV")
    c.bar()

    @blk.scalar
    def _(s):
        s.activation(out=ew[:], in_=sfx[:], func=ACT.Exp)
        c.inc(s.drain(), "eS")

    @blk.vector
    def _(v):
        c.wait(v, "eS")
        c.inc(v.tensor_tensor(out=ew[:], in0=ew[:], in1=pw64[:], op=ALU.mult), "eV")
    c.bar()

    for a in range(3):
        for w in range(4):
            @blk.tensor
            def _(t, a=a, w=w):
                for j in range(8):
                    i = 8 * w + j
                    c.inc(t.transpose(pT[0:16, 128 * j:128 * (j + 1)],
                                      LBB[:, 16 * (32 * a + i):16 * (32 * a + i + 1)],
                                      idb[:]), "eT")
            c.bar()

            @blk.vector
            def _(v, a=a, w=w):
                c.inc(v.tensor_copy(out=btmp, in_=pT[0:16, 0:1024]), "eV")
            c.bar()

            @blk.gpsimd
            def _(g, a=a, w=w):
                c.inc(g.dma_start(out=BTFB[16 * a:16 * (a + 1), 1024 * w:1024 * (w + 1)],
                                  in_=btmp), "qG", 16)
            c.bar()
    if debug:
        @blk.gpsimd
        def _(g):
            c.inc(g.dma_start(out=dbg["ew"], in_=ew[:]), "qG", 16)

    for j in range(8):
        @blk.tensor
        def _(t, j=j):
            c.inc(t.transpose(p6[0:1, 120:184], ew[:, j:j + 1], idf[0:64, 0:64]), "eT")

        @blk.vector
        def _(v, j=j):
            c.wait(v, "eT")
            c.inc(v.tensor_copy(out=ewrow[:], in_=p6[0:1, 120:184]), "eV")

        @blk.tensor
        def _(t, j=j):
            c.wait(t, "eV")
            c.inc(t.matmul(p6[:, 184:248], onescf[:], ewrow[:], start=True, stop=True), "eT")

        @blk.vector
        def _(v, j=j):
            c.wait(v, "eT")
            c.inc(v.tensor_copy(out=EWR[:, 64 * j:64 * (j + 1)], in_=p6[:, 184:248]), "eV")
        c.bar()

    for jj in range(4):
        @blk.gpsimd
        def _(g, jj=jj):
            c.inc(g.dma_start(out=xcb2[0:1, :], in_=norm_w[0:1, 1024 * jj:1024 * (jj + 1)]), "qG", 16)
        c.bar()

        @blk.tensor
        def _(t, jj=jj):
            c.inc(t.matmul(p4[:], onescf[:], xcb2[0:1, 0:512], start=True, stop=True), "eT")
            c.inc(t.matmul(p5[:], onescf[:], xcb2[0:1, 512:1024], start=True, stop=True), "eT")
        c.bar()

        @blk.vector
        def _(v, jj=jj):
            v.tensor_copy(out=NWREP[:, 1024 * jj:1024 * jj + 512], in_=p4[:])
            c.inc(v.tensor_copy(out=NWREP[:, 1024 * jj + 512:1024 * (jj + 1)], in_=p5[:]), "eV")
        c.bar()

    for j in range(8):
        @blk.gpsimd
        def _(g, j=j):
            c.inc(g.dma_start(out=HLD[:], in_=hloc[NS * j:NS * (j + 1), :]), "qG", 16)
        c.bar()

        @blk.vector
        def _(v, j=j):
            for h in range(NH):
                if j == 0:
                    v.tensor_scalar(out=HINb[:, 64 * h:64 * (h + 1)], in0=HLD[:, 64 * h:64 * (h + 1)],
                                    scalar1=EWR[:, 64 * j + h:64 * j + h + 1], scalar2=None,
                                    op0=ALU.mult)
                else:
                    v.scalar_tensor_tensor(out=HINb[:, 64 * h:64 * (h + 1)],
                                           in0=HLD[:, 64 * h:64 * (h + 1)],
                                           scalar=EWR[:, 64 * j + h:64 * j + h + 1],
                                           in1=HINb[:, 64 * h:64 * (h + 1)],
                                           op0=ALU.mult, op1=ALU.add)
            c.inc(v.drain(), "eV")
        c.bar()
    if debug:
        @blk.gpsimd
        def _(g):
            c.inc(g.dma_start(out=dbg["hin"], in_=HINb[:]), "qG", 16)
        c.bar()

    if STOP < 2:
        es.close()
        return nc, dbg
    NQ2 = 8
    SUB = 9
    gytm = {}
    prev_ev7 = {}
    for q in range(NQ2):
        @blk.gpsimd
        def _(g, q=q):
            if q >= 1:
                c.wait(g, "qS", gytm[q - 1])
                c.wait(g, "eV", prev_ev7[q - 1])
            c.inc(g.dma_start(out=G1B[:], in_=g1[Q * q:Q * (q + 1), :]), "qD", 16)
            c.inc(g.dma_start(out=SZB[:], in_=sz[Q * q:Q * (q + 1), :]), "qD", 16)
        ldm = c.mark("qD")

        @blk.scalar
        def _(s, q=q):
            s.activation(out=EC2[:], in_=cumL2[:, 64 * q:64 * (q + 1)], func=ACT.Exp)
            c.inc(s.drain(), "eS")
        ecm = c.mark("eS")

        mmqm, evqm, corrm = {}, {}, {}
        for iq in range(8):
            @blk.tensor
            def _(t, q=q, iq=iq):
                if iq >= 4:
                    c.wait(t, "eS", corrm[iq - 4])
                t.matmul(PS[iq % 4][:], Cfm[:, Q * q:Q * (q + 1)],
                         HINb[:, 512 * iq:512 * (iq + 1)], start=True, stop=True)
                c.inc(t.matmul(p6[0:1, 448:449], onescf[0:1, 0:1], onescf[0:1, 0:1],
                               start=True, stop=True), "eT")
            mmqm[iq] = c.mark("eT")

            if SUB < 2:
                @blk.vector
                def _(v, q=q, iq=iq):
                    c.wait(v, "eT", mmqm[iq])
                    if SUB == 0:
                        v.tensor_copy(out=GY[:, 512 * iq:512 * (iq + 1)], in_=PS[iq % 4][:])
                    else:
                        v.memset(GY[:, 512 * iq:512 * (iq + 1)], 0.5)
                    c.inc(v.drain(), "eV")
                evqm[iq] = c.mark("eV")
                continue

            cs = scr if iq % 2 == 0 else xcb2[:, 0:512]

            @blk.scalar
            def _(s, q=q, iq=iq, cs=cs, ecm=ecm):
                c.wait(s, "eT", mmqm[iq])
                if iq >= 2:
                    c.wait(s, "eV", evqm[iq - 2])
                for l in range(8):
                    h = 8 * iq + l
                    s.activation(out=cs[:, 64 * l:64 * (l + 1)], in_=PS[iq % 4][:, 64 * l:64 * (l + 1)],
                                 func=ACT.Identity, scale=EC2[:, h:h + 1])
                c.inc(s.drain(), "eS")
            corrm[iq] = c.mark("eS")

            @blk.vector
            def _(v, q=q, iq=iq, cs=cs, ldm=ldm):
                c.wait(v, "eS", corrm[iq])
                c.wait(v, "qD", ldm)
                for l in range(8):
                    h = 8 * iq + l
                    v.tensor_tensor(out=x2b[:, 0:64], in0=cs[:, 64 * l:64 * (l + 1)],
                                    in1=SZB[:, 64 * h:64 * (h + 1)], op=ALU.mult)
                    v.tensor_tensor(out=GY[:, 64 * h:64 * (h + 1)], in0=x2b[:, 0:64],
                                    in1=G1B[:, 64 * h:64 * (h + 1)], op=ALU.add)
                c.inc(v.drain(), "eV")
            evqm[iq] = c.mark("eV")
        prev_ev7[q] = evqm[7]

        @blk.scalar
        def _(s, q=q):
            c.wait(s, "eV", evqm[7])
            for jj in range(8):
                s.activation(out=scr[:], in_=GY[:, 512 * jj:512 * (jj + 1)], func=ACT.Square,
                             accum_out=statS4[:, jj:jj + 1])
            c.inc(s.drain(), "eS")

        @blk.vector
        def _(v, q=q):
            c.wait(v, "eS")
            v.reduce_sum(out=colQ[:], in_=statS4[:, 0:8], axis=AXX)
            v.drain()
            v.tensor_scalar(out=colTa[:], in0=colQ[:], scalar1=1.0 / H, scalar2=EPS,
                            op0=ALU.mult, op1=ALU.add)
            v.drain()
            v.reciprocal(out=colTb[:], in_=colTa[:])
            c.inc(v.drain(), "eV")

        @blk.scalar
        def _(s, q=q):
            c.wait(s, "eV")
            s.activation(out=colR[:], in_=colTb[:], func=ACT.Sqrt)
            c.inc(s.drain(), "eS")

        @blk.vector
        def _(v, q=q):
            c.wait(v, "eS")
            c.wait(v, "qS", c.mark("qS"))
            v.tensor_scalar(out=GYN[:], in0=GY[:], scalar1=colR[:], scalar2=None, op0=ALU.mult)
            v.drain()
            v.tensor_tensor(out=GYN[:], in0=GYN[:], in1=NWREP[:], op=ALU.mult)
            c.inc(v.drain(), "eV")
        gynm = c.mark("eV")

        @blk.sync
        def _(sy, q=q, gynm=gynm):
            c.wait(sy, "eV", gynm)
            for i in range(32):
                c.inc(sy.dma_start(out=LNF[:, 1024 * i + 128 * q:1024 * i + 128 * (q + 1)],
                                   in_=GYN[:, 128 * i:128 * (i + 1)], transpose=True), "qS", 16)
        gytm[q] = c.mark("qS")
    c.bar()
    if debug:
        @blk.gpsimd
        def _(g):
            c.inc(g.dma_start(out=dbg["gy"], in_=GYN[:]), "qG", 16)
        c.bar()

    if STOP < 3:
        es.close()
        return nc, dbg
    opw_mark = {}
    opmm_mark = {}
    opev_mark = {}
    opsq_mark = {}
    opw2_mark = {}
    opcp_mark = {}
    for oc in range(8):
        qsem = "qA" if oc % 2 == 0 else "qB"

        @blk.gpsimd
        def _(g, oc=oc, qsem=qsem):
            if (oc - 2, 7) in opmm_mark:
                c.wait(g, "eT", opmm_mark[(oc - 2, 7)])
            for dt in range(NKT):
                c.inc(g.dma_start(out=OPW[oc % 2][:, 512 * dt:512 * (dt + 1)],
                                  in_=opw[128 * dt:128 * (dt + 1), 512 * oc:512 * (oc + 1)]),
                      qsem, 16)
            opw_mark[oc] = (qsem, c.mark(qsem))

        for q in range(8):
            @blk.scalar
            def _(s, oc=oc, q=q):
                prev = (oc, q - 2) if q >= 2 else (oc - 1, q + 6)
                if prev in opev_mark:
                    c.wait(s, "eV", opev_mark[prev])
                c.inc(s.dma_start(out=xcb2[:, 512 * (q % 2):512 * (q % 2 + 1)],
                                  in_=x1[Q * q:Q * (q + 1), 512 * oc:512 * (oc + 1)]), "qC", 16)
            xldm = c.mark("qC")

            @blk.tensor
            def _(t, oc=oc, q=q):
                if q == 0:
                    c.wait(t, opw_mark[oc][0], opw_mark[oc][1])
                bank = PS[q % 4]
                prevg = (oc, q - 4) if q >= 4 else (oc - 1, q + 4)
                if prevg in opcp_mark:
                    c.wait(t, "eS", opcp_mark[prevg])
                for i in range(NKT):
                    mi = t.matmul(bank[:], LNF[:, 1024 * i + 128 * q:1024 * i + 128 * (q + 1)],
                                  OPW[oc % 2][:, 512 * i:512 * (i + 1)],
                                  start=(i == 0), stop=(i == NKT - 1))
                c.inc(mi, "eT")
                opmm_mark[(oc, q)] = c.mark("eT")

            @blk.scalar
            def _(s, oc=oc, q=q):
                c.wait(s, "eT", opmm_mark[(oc, q)])
                prevslot = (oc, q - 2) if q >= 2 else (oc - 1, q + 6)
                if prevslot in opev_mark:
                    c.wait(s, "eV", opev_mark[prevslot])
                s.copy(out=GY[:, 512 * (q % 2):512 * (q % 2 + 1)], in_=PS[q % 4][:])
                c.inc(s.drain(), "eS")
                opcp_mark[(oc, q)] = c.mark("eS")

            @blk.vector
            def _(v, oc=oc, q=q, xldm=xldm):
                c.wait(v, "eS", opcp_mark[(oc, q)])
                c.wait(v, "qC", xldm)
                prevslot = (oc, q - 2) if q >= 2 else (oc - 1, q + 6)
                if prevslot in opsq_mark:
                    c.wait(v, "eS", opsq_mark[prevslot])
                if prevslot in opw2_mark:
                    c.wait(v, "qW", opw2_mark[prevslot])
                slot = x2b[:, 512 * (q % 2):512 * (q % 2 + 1)]
                v.tensor_tensor(out=slot, in0=GY[:, 512 * (q % 2):512 * (q % 2 + 1)],
                                in1=xcb2[:, 512 * (q % 2):512 * (q % 2 + 1)], op=ALU.add)
                v.drain()
                v.reduce_sum(out=statS3[:, 8 * q + oc:8 * q + oc + 1], in_=slot, axis=AXX)
                c.inc(v.drain(), "eV")
            opev_mark[(oc, q)] = c.mark("eV")

            @blk.scalar
            def _(s, oc=oc, q=q):
                c.wait(s, "eV", opev_mark[(oc, q)])
                s.activation(out=scr[:], in_=x2b[:, 512 * (q % 2):512 * (q % 2 + 1)],
                             func=ACT.Square, accum_out=statS4[:, 8 * q + oc:8 * q + oc + 1])
                c.inc(s.drain(), "eS")
                opsq_mark[(oc, q)] = c.mark("eS")

            @blk.sync
            def _(sy, oc=oc, q=q):
                c.wait(sy, "eV", opev_mark[(oc, q)])
                c.inc(sy.dma_start(out=x2_dram[Q * q:Q * (q + 1), 512 * oc:512 * (oc + 1)],
                                   in_=x2b[:, 512 * (q % 2):512 * (q % 2 + 1)]), "qW", 16)
                opw2_mark[(oc, q)] = c.mark("qW")
    c.bar()
    if debug:
        @blk.gpsimd
        def _(g):
            c.inc(g.dma_start(out=dbg["x2"], in_=x2_dram[:]), "qG", 16)
        c.bar()

    if STOP < 4:
        es.close()
        return nc, dbg

    def emit_colstats2():
        def vsec(vv):
            inv = 1.0 / H
            vv.tensor_scalar(out=colM[:], in0=colS[:], scalar1=inv, scalar2=None, op0=ALU.mult)
            vv.drain()
            vv.tensor_scalar(out=colTa[:], in0=colQ[:], scalar1=inv, scalar2=None, op0=ALU.mult)
            vv.drain()
            vv.tensor_tensor(out=colTb[:], in0=colM[:], in1=colM[:], op=ALU.mult)
            vv.drain()
            vv.tensor_tensor(out=colTa[:], in0=colTa[:], in1=colTb[:], op=ALU.subtract)
            vv.drain()
            vv.tensor_scalar(out=colTb[:], in0=colTa[:], scalar1=EPS, scalar2=None, op0=ALU.add)
            vv.drain()
            vv.reciprocal(out=colTb[:], in_=colTb[:])
            c.inc(vv.drain(), "eV")
        blk.vector(vsec)

        def ssec(ss):
            c.wait(ss, "eV")
            ss.activation(out=colR[:], in_=colTb[:], func=ACT.Sqrt)
            c.inc(ss.drain(), "eS")
        blk.scalar(ssec)

        def vsec3(vv):
            c.wait(vv, "eS")
            vv.tensor_tensor(out=colTa[:], in0=colM[:], in1=colR[:], op=ALU.mult)
            vv.drain()
            vv.tensor_scalar(out=colN[:], in0=colTa[:], scalar1=-1.0, scalar2=None, op0=ALU.mult)
            c.inc(vv.drain(), "eV")
        blk.vector(vsec3)

    for tt in range(8):
        @blk.scalar
        def _(s, tt=tt):
            if tt == 0:
                c.inc(s.dma_start(out=xst2[:], in_=x2_dram[Q * 0:Q * 1, :]), "qC", 16)

        @blk.vector
        def _(v, tt=tt):
            c.wait(v, "qC", c.mark("qC"))
            v.reduce_sum(out=colS[:], in_=statS3[:, 8 * tt:8 * tt + 8], axis=AXX)
            v.drain()
            v.reduce_sum(out=colQ[:], in_=statS4[:, 8 * tt:8 * tt + 8], axis=AXX)
            c.inc(v.drain(), "eV")

        emit_colstats2()

        @blk.vector
        def _(v, tt=tt):
            c.wait(v, "eV")
            c.wait(v, "eS")
            c.wait(v, "qC", c.mark("qC"))
            c.wait(v, "qS", c.mark("qS"))
            v.tensor_scalar(out=lnxtok2[:], in0=xst2[:], scalar1=colR[:], scalar2=colN[:],
                            op0=ALU.mult, op1=ALU.add)
            c.inc(v.drain(), "eV")
        nm = c.mark("eV")

        @blk.scalar
        def _(s, tt=tt, nm=nm):
            c.wait(s, "eV", nm)
            if tt < 7:
                c.inc(s.dma_start(out=xst2[:], in_=x2_dram[Q * (tt + 1):Q * (tt + 2), :]), "qC", 16)

        @blk.sync
        def _(sy, tt=tt, nm=nm):
            c.wait(sy, "eV", nm)
            for i in range(32):
                c.inc(sy.dma_start(out=LNF[:, 1024 * i + 128 * tt:1024 * i + 128 * (tt + 1)],
                                   in_=lnxtok2[:, 128 * i:128 * (i + 1)], transpose=True),
                      "qS", 16)
    c.bar()

    if STOP < 5:
        es.close()
        return nc, dbg

    @blk.tensor
    def _(t):
        for th in range(2):
            pp = p4 if th == 0 else p5
            for k in range(NKT):
                mi = t.matmul(pp[0:R48, :], ATf[:, R48 * k:R48 * (k + 1)],
                              LNF[:, 1024 * k + 512 * th:1024 * k + 512 * (th + 1)],
                              start=(k == 0), stop=(k == NKT - 1))
        c.inc(mi, "eT")

    @blk.scalar
    def _(s):
        c.wait(s, "eT")
        s.copy(out=rmk2[0:R48, 0:512], in_=p4[0:R48, :])
        s.copy(out=rmk2[0:R48, 512:1024], in_=p5[0:R48, :])
        c.inc(s.drain(), "eS")
    c.bar()

    fev_mark = {}
    fmm_mark = {}
    fout_mark = {}
    fcp_mark = {}
    fw_mark = {}
    for oc in range(8):
        qsem = "qA" if oc % 2 == 0 else "qB"

        @blk.gpsimd
        def _(g, oc=oc, qsem=qsem):
            if (oc - 2, 7) in fmm_mark:
                c.wait(g, "eT", fmm_mark[(oc - 2, 7)])
            for k in range(NKT):
                c.inc(g.dma_start(out=WFF[oc % 2][:, 512 * k:512 * (k + 1)],
                                  in_=ffn_wT[128 * k:128 * (k + 1), 512 * oc:512 * (oc + 1)]),
                      qsem, 16)
            fw_mark[oc] = (qsem, c.mark(qsem))

        for tt in range(8):
            @blk.scalar
            def _(s, oc=oc, tt=tt):
                prev = (oc, tt - 2) if tt >= 2 else (oc - 1, tt + 6)
                if prev in fev_mark:
                    c.wait(s, "eV", fev_mark[prev])
                c.inc(s.dma_start(out=xcb2[:, 512 * (tt % 2):512 * (tt % 2 + 1)],
                                  in_=x2_dram[Q * tt:Q * (tt + 1), 512 * oc:512 * (oc + 1)]),
                      "qC", 16)
            xldm = c.mark("qC")

            @blk.tensor
            def _(t, oc=oc, tt=tt):
                if tt == 0:
                    c.wait(t, fw_mark[oc][0], fw_mark[oc][1])
                bank = PS[tt % 4]
                prevg = (oc, tt - 4) if tt >= 4 else (oc - 1, tt + 4)
                if prevg in fcp_mark:
                    c.wait(t, "eS", fcp_mark[prevg])
                for k in range(NKT):
                    t.matmul(bank[:], LNF[:, 1024 * k + 128 * tt:1024 * k + 128 * (tt + 1)],
                             WFF[oc % 2][:, 512 * k:512 * (k + 1)],
                             start=(k == 0), stop=False)
                c.inc(t.matmul(bank[:], rmk2[0:49, 128 * tt:128 * (tt + 1)],
                               BTFB[0:49, 512 * oc:512 * (oc + 1)], start=False, stop=True), "eT")
                fmm_mark[(oc, tt)] = c.mark("eT")

            @blk.scalar
            def _(s, oc=oc, tt=tt):
                c.wait(s, "eT", fmm_mark[(oc, tt)])
                prevslot = (oc, tt - 2) if tt >= 2 else (oc - 1, tt + 6)
                if prevslot in fev_mark:
                    c.wait(s, "eV", fev_mark[prevslot])
                s.activation(out=GY[:, 512 * (tt % 2):512 * (tt % 2 + 1)], in_=PS[tt % 4][:],
                             func=ACT.Identity, scale=halfcol[:])
                c.inc(s.drain(), "eS")
                fcp_mark[(oc, tt)] = c.mark("eS")

            @blk.vector
            def _(v, oc=oc, tt=tt, xldm=xldm):
                c.wait(v, "eS", fcp_mark[(oc, tt)])
                c.wait(v, "qC", xldm)
                prevslot = (oc, tt - 2) if tt >= 2 else (oc - 1, tt + 6)
                if prevslot in fout_mark:
                    c.wait(v, "qW", fout_mark[prevslot])
                slot = x2b[:, 512 * (tt % 2):512 * (tt % 2 + 1)]
                v.tensor_tensor(out=slot, in0=GY[:, 512 * (tt % 2):512 * (tt % 2 + 1)],
                                in1=xcb2[:, 512 * (tt % 2):512 * (tt % 2 + 1)], op=ALU.add)
                c.inc(v.drain(), "eV")
            fev_mark[(oc, tt)] = c.mark("eV")

            @blk.sync
            def _(sy, oc=oc, tt=tt):
                c.wait(sy, "eV", fev_mark[(oc, tt)])
                c.inc(sy.dma_start(out=o_out[Q * tt:Q * (tt + 1), 512 * oc:512 * (oc + 1)],
                                   in_=x2b[:, 512 * (tt % 2):512 * (tt % 2 + 1)]), "qW", 16)
                fout_mark[(oc, tt)] = c.mark("qW")
    c.bar()

    if debug:
        @blk.gpsimd
        def _(g):
            c.inc(g.dma_start(out=dbg["wff"], in_=WB[:]), "qG", 16)
            c.inc(g.dma_start(out=dbg["btfb"], in_=BTFB[:]), "qG", 16)
            c.inc(g.dma_start(out=dbg["lnf"], in_=LNF[:, 0:1024]), "qG", 16)
            c.inc(g.dma_start(out=dbg["rmk"], in_=rmk2[:]), "qG", 16)
        c.bar()

    @blk.gpsimd
    def _(g):
        c.wait(g, "qG")
        c.wait(g, "qW")
    c.bar()

    es.close()
    return nc, dbg


BF = ml_dtypes.bfloat16
H = 4096; T = 1024; PROJ = 8512; CONV = 4352; NH = 64; NS = 128


def l1_inputs(inp, core):
    x = np.asarray(inp["L0_attn_hidden_states"][0])
    xpad = np.concatenate([np.zeros((3, H), np.float32), x], 0)
    lo = T * core
    m = {}
    m["xin"] = np.ascontiguousarray(xpad[lo:lo + T + 3])
    m["halom"] = np.full((128, 4), 0.0 if core == 0 else 1.0, np.float32)
    m["ffn_wT"] = np.ascontiguousarray(np.asarray(inp["L0_ffn_weight"]).T)
    m["ffn_b"] = np.asarray(inp["L0_ffn_bias"]).reshape(1, H)
    m["lora_A"] = np.asarray(inp["L0_ffn_lora_A"]).reshape(48, H)
    m["lora_B"] = np.asarray(inp["L0_ffn_lora_B"]).reshape(3 * H, 16)
    m["lsi"] = np.asarray(inp["L0_ffn_lora_start_indices"]).reshape(1, 4).astype(np.int32)
    m["ipw"] = np.asarray(inp["L0_attn_in_proj_w"])
    m["conv_w"] = np.asarray(inp["L0_attn_conv_w"])
    m["a_log"] = np.asarray(inp["L0_attn_A_log"]).reshape(1, NH)
    m["dt_bias"] = np.asarray(inp["L0_attn_dt_bias"]).reshape(1, NH)
    m["d_in"] = np.asarray(inp["L0_attn_D"]).reshape(1, NH)
    mup = np.zeros((1, PROJ + 64), np.float32)
    mup[0, :PROJ] = np.asarray(inp["L0_attn_mup_vector"])
    m["mup_i"] = mup
    m["ln_w"] = np.asarray(inp["L0_attn_ln_w"]).reshape(32, 128)
    return m


def l2_inputs(inp, core, l1res):
    r = l1res[core]
    m = {}
    m["x1"] = np.asarray(r["o_x1"])
    m["g1"] = np.asarray(r["o_g1"])
    m["sz"] = np.asarray(r["o_sztok"])
    m["Cbf"] = np.asarray(r["o_C"])
    m["cum"] = np.asarray(r["o_cum"])
    m["ctcore"] = np.stack([np.asarray(l1res[j]["o_ctot"])[7] for j in range(8)])
    m["hloc"] = np.concatenate([np.asarray(l1res[j]["o_hloc"]) for j in range(8)], 0)
    m["prefw"] = np.tile((np.arange(8) < core).astype(np.float32), (64, 1))
    m["norm_w"] = np.asarray(inp["L0_attn_norm_w"]).reshape(1, H)
    m["opw"] = np.asarray(inp["L0_attn_out_proj_w"])
    m["ffn_wT"] = np.ascontiguousarray(np.asarray(inp["L0_ffn_weight"]).T)
    m["ffn_b"] = np.asarray(inp["L0_ffn_bias"]).reshape(1, H)
    m["lora_A"] = np.asarray(inp["L0_ffn_lora_A"]).reshape(48, H)
    m["lora_B"] = np.asarray(inp["L0_ffn_lora_B"]).reshape(3 * H, 16)
    m["lsi"] = np.asarray(inp["L0_ffn_lora_start_indices"]).reshape(1, 4).astype(np.int32)
    return m


_BUILT = {}


def kernel(**inputs):
    if "nc1" not in _BUILT:
        _BUILT["nc1"], _ = build_l1(debug=0, upto=9)
        _BUILT["nc2"], _ = build_l2(debug=0)
    nc1 = _BUILT["nc1"]
    nc2 = _BUILT["nc2"]
    in1 = [l1_inputs(inputs, c) for c in range(8)]
    res1 = run_bass_kernel_spmd(nc1, in1, core_ids=list(range(8)))
    l1res = [{k: np.asarray(v) for k, v in res1.results[c].items()} for c in range(8)]
    in2 = [l2_inputs(inputs, c, l1res) for c in range(8)]
    res2 = run_bass_kernel_spmd(nc2, in2, core_ids=list(range(8)))
    out = np.concatenate([np.asarray(res2.results[c]["o_out"], np.float32) for c in range(8)], 0)
    return out.reshape(1, 8192, 4096)
